# version 03
# baseline (speedup 1.0000x reference)
import sys
import contextlib
import numpy as np
import concourse.bass as bass
import concourse.mybir as mybir
from concourse.bass_utils import run_bass_kernel_spmd



F32 = mybir.dt.float32
BF16 = mybir.dt.bfloat16
I32 = mybir.dt.int32
AF = mybir.ActivationFunctionType
ALU = mybir.AluOpType
AX = mybir.AxisListType.X


class Prog:
    def __init__(self, nc, block, stack):
        self.nc, self.block, self.stack = nc, block, stack
        self.pe = stack.enter_context(nc.semaphore("pe"))
        self.el = stack.enter_context(nc.semaphore("el"))
        self.qs = stack.enter_context(nc.semaphore("qs"))
        self.qg = stack.enter_context(nc.semaphore("qg"))
        self.npe = 0
        self.nel = 0
        self.nqs = 0
        self.nqg = 0
        self.ps = []
        self.ps_small = []

    def alloc_psum(self, n_big=4):
        self.ps = [self.stack.enter_context(
            self.nc.psum_tensor(f"ps{i}", [128, 512], F32)) for i in range(n_big)]
        self.pst = [self.stack.enter_context(
            self.nc.psum_tensor(f"pst{i}", [128, 512], F32)) for i in range(2)]
        self.misc1 = self.stack.enter_context(
            self.nc.psum_tensor("psm1", [128, 512], F32))
        self.misc2 = self.stack.enter_context(
            self.nc.psum_tensor("psm2", [128, 512], F32))
        self._pi = 0
        self.bar = self.stack.enter_context(
            self.nc.sbuf_tensor("elbar", [1, 1], F32))

    def psum(self):
        t = self.ps[self._pi % len(self.ps)]
        self._pi += 1
        return t

    def el_barrier(self, waits):
        @self.block.vector
        def _(v):
            for sem, cnt in waits:
                v.wait_ge(sem, cnt)
            r = v.memset(self.bar[:, :], 0.0)
            r._wait_ge(self.el, self.nel)
            r.then_inc(self.el, 1)
        self.nel += 1

    def sb(self, name, shape, dtype=F32):
        return self.stack.enter_context(self.nc.sbuf_tensor(name, shape, dtype))

    class _E:
        def __init__(self, p, eng, pending):
            object.__setattr__(self, "_p", p)
            object.__setattr__(self, "_e", eng)
            object.__setattr__(self, "_pend", pending)

        def __getattr__(self, name):
            f = getattr(self._e, name)
            if not callable(f) or name.startswith("wait"):
                return f
            p = self._p

            def wrapped(*a, **k):
                for sem, cnt in self._pend:
                    self._e.wait_ge(sem, cnt)
                del self._pend[:]
                r = f(*a, **k)
                r._wait_ge(p.el, p.nel)
                r.then_inc(p.el, 1)
                p.nel += 1
                return r
            return wrapped

    def _esec(self, blockdec, fn, wait_pe, wait_dma):
        pending = []
        if wait_pe:
            pending.append((self.pe, self.npe))
        if wait_dma:
            pending.extend(wait_dma)

        @blockdec
        def _(e):
            fn(Prog._E(self, e, pending))

    def vec(self, fn, wait_pe=True, wait_dma=None):
        self._esec(self.block.vector, fn, wait_pe, wait_dma)

    def act(self, fn, wait_pe=True, wait_dma=None):
        self._esec(self.block.scalar, fn, wait_pe, wait_dma)

    def pool(self, fn, wait_pe=True, wait_dma=None):
        self._esec(self.block.gpsimd, fn, wait_pe, wait_dma)

    def tensor(self, fn, wait_el=True, wait_dma=None, extra_el=None):
        prog = self

        class _T:
            def __init__(self):
                self.first = True
                self.front = []

            def _pre(self):
                if self.first:
                    self.first = False
                    if wait_dma:
                        for sem, cnt in wait_dma:
                            self._t.wait_ge(sem, cnt)
                    if wait_el:
                        self.front.append((prog.el, prog.nel))

            def _att(self, r):
                for sem, cnt in self.front[1:]:
                    pass
                if self.front:
                    r._wait_ge(*self.front[-1])
                    del self.front[:]
                return r

            def matmul(self, *a, **k):
                self._pre()
                for sem, cnt in self.front[:-1]:
                    self._t.wait_ge(sem, cnt)
                    self.front.remove((sem, cnt))
                return self._att(self._t.matmul(*a, **k))

            def transpose(self, *a, **k):
                self._pre()
                for sem, cnt in self.front[:-1]:
                    self._t.wait_ge(sem, cnt)
                    self.front.remove((sem, cnt))
                return self._att(self._t.transpose(*a, **k))

            def wait_ge(self, sem, cnt):
                self.front.append((sem, cnt))
                return None

        @self.block.tensor
        def _(t):
            w = _T()
            w._t = t
            last = fn(w)
            last.then_inc(self.pe, 1)
        self.npe += 1

    def dma_s(self, out_ap, in_ap, wait=None):
        @self.block.sync
        def _(s):
            if wait:
                for sem, cnt in wait[:-1]:
                    s.wait_ge(sem, cnt)
            r = s.dma_start(out=out_ap, in_=in_ap)
            if wait:
                r._wait_ge(*wait[-1])
            r.then_inc(self.qs, 16)
        self.nqs += 16

    def dma_g(self, out_ap, in_ap, wait=None):
        @self.block.gpsimd
        def _(g):
            if wait:
                for sem, cnt in wait[:-1]:
                    g.wait_ge(sem, cnt)
            r = g.dma_start(out=out_ap, in_=in_ap)
            if wait:
                r._wait_ge(*wait[-1])
            r.then_inc(self.qg, 16)
        self.nqg += 16


def emit_ln(p: Prog, x_of, out_of, W, sc, eps=1e-5, nch=8, nfeat=None,
            out_chunks=None, extra_mul=None):
    nfeat = nfeat or nch * 128
    out_chunks = out_chunks if out_chunks is not None else range(nch)

    def sq_fn(s):
        for c in range(nch):
            s.activation(sc["sq"][:, c * W:(c + 1) * W], x_of(c), AF.Square)
    p.act(sq_fn)

    x_dt = x_of(0).dtype
    ones_x = sc["ones_col_b"] if x_dt == mybir.dt.bfloat16 else sc["ones_col"]
    sc = dict(sc)
    sc["psA"] = sc["psm1"][0:1, 0:W]
    sc["psB"] = sc["psm2"][0:1, 0:W]

    def mm_fn(t):
        for c in range(nch):
            t.matmul(sc["psA"], sc["ones_col"], sc["sq"][:, c * W:(c + 1) * W],
                     start=(c == 0), stop=(c == nch - 1))
        last = None
        for c in range(nch):
            last = t.matmul(sc["psB"], ones_x, x_of(c),
                            start=(c == 0), stop=(c == nch - 1))
        return last
    p.tensor(mm_fn)

    inv_n = 1.0 / nfeat

    def stats(v):
        v.tensor_scalar_mul(sc["r1"][:, 0:W], sc["psB"], inv_n)
        v.tensor_mul(sc["r2"][:, 0:W], sc["r1"][:, 0:W], sc["r1"][:, 0:W])
        v.tensor_scalar(sc["r3"][:, 0:W], sc["psA"], inv_n, eps,
                        ALU.mult, ALU.add)
        v.tensor_sub(sc["r3"][:, 0:W], sc["r3"][:, 0:W], sc["r2"][:, 0:W])
    p.vec(stats)

    def rstd(s):
        s.activation(sc["r2"][:, 0:W], sc["r3"][:, 0:W], AF.Sqrt)
    p.act(rstd)

    def recip(v):
        v.reciprocal(sc["r3"][:, 0:W], sc["r2"][:, 0:W])
        v.tensor_mul(sc["r1"][:, 0:W], sc["r1"][:, 0:W], sc["r3"][:, 0:W])
    p.vec(recip)

    psC1 = p.psum()
    psC2 = p.psum()

    def bc(t):
        t.matmul(psC1[:, 0:W], sc["ones_row"], sc["r3"][:, 0:W],
                 start=True, stop=True)
        return t.matmul(psC2[:, 0:W], sc["ones_row"], sc["r1"][:, 0:W],
                        start=True, stop=True)
    p.tensor(bc)

    def evacbc(v):
        v.tensor_copy(sc["b1"][:, 0:W], psC1[:, 0:W])
        v.tensor_copy(sc["b2"][:, 0:W], psC2[:, 0:W])
        for c in out_chunks:
            v.tensor_mul(sc["sq"][:, c * W:(c + 1) * W], x_of(c), sc["b1"][:, 0:W])
            v.tensor_sub(out_of(c), sc["sq"][:, c * W:(c + 1) * W], sc["b2"][:, 0:W])
    p.vec(evacbc)


def make_ln_scratch(p: Prog, W, tag=""):
    sc = {}
    sc["sq"] = p.sb("lnsq" + tag, [128, 8 * W], F32)[:, :]
    sc["r1"] = p.sb("lnr1" + tag, [1, W], F32)[:, :]
    sc["r2"] = p.sb("lnr2" + tag, [1, W], F32)[:, :]
    sc["r3"] = p.sb("lnr3" + tag, [1, W], F32)[:, :]
    sc["b1"] = p.sb("lnb1" + tag, [128, W], F32)[:, :]
    sc["b2"] = p.sb("lnb2" + tag, [128, W], F32)[:, :]
    sc["psm1"] = p.misc1
    sc["psm2"] = p.misc2
    return sc


def emit_mm(p: Prog, out_of, lhsT_of, rhs_of, kch, mch, W, evac,
            nsplit=1):
    for m in range(mch):
        ps = p.psum()

        def mm_fn(t, m=m, ps=ps):
            last = None
            for k in range(kch):
                last = t.matmul(ps[:, 0:W], lhsT_of(k, m), rhs_of(k),
                                start=(k == 0), stop=(k == kch - 1))
            return last
        p.tensor(mm_fn)
        evac(m, ps)


F32 = mybir.dt.float32
BF16 = mybir.dt.bfloat16
ALU = mybir.AluOpType


class CM:

    def __init__(self, re, im, nim=None, n=0):
        self.re, self.im, self.nim, self.n = re, im, nim, n


class Inv:
    def __init__(self, nc, block, stack, ident128_f32, psum_from=None):
        self.nc, self.block, self.stack = nc, block, stack
        self.ident = ident128_f32
        self.pe = stack.enter_context(nc.semaphore("ipe"))
        self.dv = stack.enter_context(nc.semaphore("idv"))
        self.npe = 0
        self.ndv = 0
        if psum_from is None:
            self.ps = [stack.enter_context(nc.psum_tensor(f"ips{i}", [128, 512], F32))
                       for i in range(4)]
            self.pst = [stack.enter_context(nc.psum_tensor(f"ipst{i}", [128, 512], F32))
                        for i in range(2)]
        else:
            self.ps = list(psum_from.ps)
            self.pst = list(psum_from.pst)
        self._pi = 0

    def alloc(self, name, n, nim=False):
        nch = max(1, n // 128)
        re = self.stack.enter_context(self.nc.sbuf_tensor(name + "r", [128, nch * n], F32))
        im = self.stack.enter_context(self.nc.sbuf_tensor(name + "i", [128, nch * n], F32))
        ni = self.stack.enter_context(self.nc.sbuf_tensor(name + "n", [128, nch * n], F32)) if nim else None
        return CM(re, im, ni, n)

    def _psum2(self):
        a = self.ps[self._pi % 4]
        b = self.ps[(self._pi + 1) % 4]
        self._pi += 2
        return a, b

    class _SyncV:

        def __init__(self, inv, v, pe_wait):
            object.__setattr__(self, "_inv", inv)
            object.__setattr__(self, "_v", v)
            object.__setattr__(self, "_pw", [pe_wait])

        def __getattr__(self, name):
            f = getattr(self._v, name)
            if not callable(f) or name.startswith("wait"):
                return f
            inv = self._inv

            def wrapped(*a, **k):
                if self._pw[0] is not None:
                    self._v.wait_ge(inv.pe, self._pw[0])
                    self._pw[0] = None
                r = f(*a, **k)
                r._wait_ge(inv.dv, inv.ndv)
                r.then_inc(inv.dv, 1)
                inv.ndv += 1
                return r
            return wrapped

    def dve(self, fn, k=None):
        @self.block.vector
        def _(v):
            fn(Inv._SyncV(self, v, self.npe))

    def make_nim(self, M: CM):
        self.dve(lambda v: v.tensor_scalar_mul(M.nim[:, :], M.im[:, :], -1.0))

    def cmm(self, out: CM, A: CM, B: CM, n, a_r0=0, a_c0=0, b_r0=0, b_c0=0,
            o_r0=0, o_c0=0, post=None, nA=None, nB=None, nO=None):
        nA = nA or A.n
        nB = nB or B.n
        nO = nO or out.n
        nch = n // 128
        for mc in range(nch):
            psr, psi = self._psum2()
            m0 = a_c0 + mc * 128

            @self.block.tensor
            def _(t, psr=psr, psi=psi, m0=m0):
                first = [True]
                _mm = t.matmul

                def matmul_w(*a, **k):
                    r = _mm(*a, **k)
                    if first[0]:
                        first[0] = False
                        r._wait_ge(self.dv, self.ndv)
                    return r
                t = type("TW", (), {"matmul": staticmethod(matmul_w)})()

                def aps(kc):
                    ac = (a_r0 + kc * 128) // 128
                    bc = (b_r0 + kc * 128) // 128
                    return (A.re[:, ac * nA + m0: ac * nA + m0 + 128],
                            A.im[:, ac * nA + m0: ac * nA + m0 + 128],
                            A.nim[:, ac * nA + m0: ac * nA + m0 + 128],
                            B.re[:, bc * nB + b_c0: bc * nB + b_c0 + n],
                            B.im[:, bc * nB + b_c0: bc * nB + b_c0 + n])
                last = None
                for kc in range(nch):
                    a_re, a_im, a_ni, b_re, b_im = aps(kc)
                    t.matmul(psr[:, 0:n], a_re, b_re, start=(kc == 0), stop=False)
                for kc in range(nch):
                    a_re, a_im, a_ni, b_re, b_im = aps(kc)
                    t.matmul(psr[:, 0:n], a_ni, b_im, start=False,
                             stop=(kc == nch - 1))
                for kc in range(nch):
                    a_re, a_im, a_ni, b_re, b_im = aps(kc)
                    t.matmul(psi[:, 0:n], a_re, b_im, start=(kc == 0), stop=False)
                for kc in range(nch):
                    a_re, a_im, a_ni, b_re, b_im = aps(kc)
                    last = t.matmul(psi[:, 0:n], a_im, b_re, start=False,
                                    stop=(kc == nch - 1))
                last.then_inc(self.pe, 1)
            self.npe += 1

            oc = (o_r0 // 128) + mc
            o_re = out.re[:, oc * nO + o_c0: oc * nO + o_c0 + n]
            o_im = out.im[:, oc * nO + o_c0: oc * nO + o_c0 + n]

            if post is None:
                def post_(v, psr, psi, o_re, o_im, mc):
                    v.tensor_copy(o_re, psr)
                    return v.tensor_copy(o_im, psi)
            else:
                post_ = post
            self.dve(lambda v, psr=psr, psi=psi, o_re=o_re, o_im=o_im, mc=mc:
                     post_(v, psr[:, 0:n], psi[:, 0:n], o_re, o_im, mc))

    def ctrans(self, out: CM, src: CM, n, s_r0=0, s_c0=0, o_r0=0, o_c0=0,
               nS=None, nO=None, negate_to_nim=True):
        nS = nS or src.n
        nO = nO or out.n
        nch = n // 128
        for rc in range(nch):
            for cc in range(nch):
                sc = (s_r0 + rc * 128) // 128
                s_re = src.re[:, sc * nS + s_c0 + cc * 128: sc * nS + s_c0 + cc * 128 + 128]
                s_im = src.im[:, sc * nS + s_c0 + cc * 128: sc * nS + s_c0 + cc * 128 + 128]
                pr = self.pst[0]
                pi = self.pst[1]

                @self.block.tensor
                def _(t, s_re=s_re, s_im=s_im, pr=pr, pi=pi):
                    r = t.transpose(pr[:, 0:128], s_re, self.ident)
                    r._wait_ge(self.dv, self.ndv)
                    t.transpose(pi[:, 0:128], s_im, self.ident).then_inc(self.pe, 1)
                self.npe += 1

                oc = (o_r0 + cc * 128) // 128
                o_re = out.re[:, oc * nO + o_c0 + rc * 128: oc * nO + o_c0 + rc * 128 + 128]
                o_im = out.im[:, oc * nO + o_c0 + rc * 128: oc * nO + o_c0 + rc * 128 + 128]
                if negate_to_nim and out.nim is not None:
                    o_ni = out.nim[:, oc * nO + o_c0 + rc * 128: oc * nO + o_c0 + rc * 128 + 128]

                    def post(v, o_re=o_re, o_im=o_im, o_ni=o_ni, pr=pr, pi=pi):
                        v.tensor_copy(o_re, pr[:, 0:128])
                        v.tensor_copy(o_im, pi[:, 0:128])
                        return v.tensor_scalar_mul(o_ni, pi[:, 0:128], -1.0)
                else:
                    def post(v, o_re=o_re, o_im=o_im, pr=pr, pi=pi):
                        v.tensor_copy(o_re, pr[:, 0:128])
                        return v.tensor_copy(o_im, pi[:, 0:128])
                self.dve(post)

    def newton(self, Wt: CM, X: CM, Xs: CM, Xt: CM, M: CM, twoI,
               n, iters, w_r0=0, alpha_bc=None):
        nch = n // 128
        if alpha_bc is not None:
            for c in range(nch):
                wc = (w_r0 + c * 128) // 128
                src_re = Wt.re[:, wc * Wt.n + w_r0: wc * Wt.n + w_r0 + n]
                src_im = Wt.im[:, wc * Wt.n + w_r0: wc * Wt.n + w_r0 + n]
                dst_re = X.re[:, c * n: c * n + n]
                dst_im = X.im[:, c * n: c * n + n]

                def init(v, src_re=src_re, src_im=src_im, dst_re=dst_re, dst_im=dst_im):
                    v.tensor_scalar_mul(dst_re, src_re, alpha_bc)
                    return v.tensor_scalar(dst_im, src_im, alpha_bc, -1.0,
                                           ALU.mult, ALU.mult)
                self.dve(init)

        cur, other = X, Xs
        for it in range(iters):
            def post_m(v, psr, psi, o_re, o_im, mc):
                v.scalar_tensor_tensor(o_re, psr, -1.0,
                                       twoI[:, mc * n:(mc + 1) * n], ALU.mult, ALU.add)
                return v.tensor_scalar_mul(o_im, psi, -1.0)
            self.cmm(M, Wt, cur, n, a_r0=w_r0, a_c0=w_r0, post=post_m,
                     nA=Wt.n, nB=n, nO=n)
            self.ctrans(Xt, cur, n, nS=n, nO=n, negate_to_nim=True)
            self.cmm(other, Xt, M, n, nA=n, nB=n, nO=n)
            cur, other = other, cur
        return cur


def emit_alpha(inv: Inv, M: CM, r0, c0, n, nM, alpha_bc, scratch, ones_col,
               ones_row, tiny11, ps11, ps128):
    nch = n // 128
    acc = scratch

    def emit(v):
        first = True
        last = None
        for plane in (M.re, M.im):
            for c in range(nch):
                mc = (r0 + c * 128) // 128
                blk = plane[:, mc * nM + c0: mc * nM + c0 + n]
                v.tensor_mul(scratch[:, 2:2 + n], blk, blk)
                if first:
                    last = v.reduce_sum(scratch[:, 0:1], scratch[:, 2:2 + n],
                                        axis=mybir.AxisListType.X)
                    first = False
                else:
                    v.reduce_sum(scratch[:, 1:2], scratch[:, 2:2 + n],
                                 axis=mybir.AxisListType.X)
                    last = v.tensor_add(scratch[:, 0:1], scratch[:, 0:1],
                                        scratch[:, 1:2])
        return last
    inv.dve(emit)

    @inv.block.tensor
    def _(t):
        r = t.matmul(ps11[:, :], scratch[:, 0:1], ones_col, start=True, stop=True)
        r._wait_ge(inv.dv, inv.ndv)
        r.then_inc(inv.pe, 1)
    inv.npe += 1

    def emit2(v):
        return v.reciprocal(tiny11[:, :], ps11[:, :])
    inv.dve(emit2)

    @inv.block.tensor
    def _(t):
        r = t.matmul(ps128[:, :], ones_row, tiny11[:, :], start=True, stop=True)
        r._wait_ge(inv.dv, inv.ndv)
        r.then_inc(inv.pe, 1)
    inv.npe += 1

    inv.dve(lambda v: v.tensor_copy(alpha_bc, ps128[:, :]))


def emit_inv256(inv: Inv, N: CM, Nt: CM, r0, out: CM, outT: CM, bufs, consts,
                leaf_iters=30):
    XA, XS = bufs["X128a"], bufs["X128b"]
    Xs, Xt, M128 = bufs["Xs128"], bufs["Xt128"], bufs["M128"]
    P_, Pt, Q, S, St, SiQ = bufs["P128"], bufs["Pt128"], bufs["Q128"], \
        bufs["S128"], bufs["St128"], bufs["SiQ128"]
    a0, d0 = r0, r0 + 128

    emit_alpha(inv, Nt, a0, a0, 128, Nt.n, bufs["alpha"], bufs["scr"],
               consts["ones_col"], consts["ones_row"], bufs["t11"],
               bufs["ps11"], bufs["ps128"])
    Ai = inv.newton(Nt, XA, Xs, Xt, M128, consts["twoI128"], 128, leaf_iters,
                    w_r0=a0, alpha_bc=bufs["alpha"][:, 0:1])
    if Ai is not XA:
        inv.dve(lambda v: (v.tensor_copy(XA.re[:, :], Ai.re[:, :]),
                           v.tensor_copy(XA.im[:, :], Ai.im[:, :]))[-1])
    AiT = bufs["AiT128"]
    inv.ctrans(AiT, XA, 128)
    inv.cmm(P_, AiT, N, 128, b_r0=a0, b_c0=d0, nB=N.n)
    Dre = lambda mc: N.re[:, (d0 // 128) * N.n + d0: (d0 // 128) * N.n + d0 + 128]
    Dim = lambda mc: N.im[:, (d0 // 128) * N.n + d0: (d0 // 128) * N.n + d0 + 128]

    def post_s(v, psr, psi, o_re, o_im, mc):
        v.scalar_tensor_tensor(o_re, psr, -1.0, Dre(mc), ALU.mult, ALU.add)
        return v.scalar_tensor_tensor(o_im, psi, -1.0, Dim(mc), ALU.mult, ALU.add)
    inv.cmm(S, Nt, P_, 128, a_r0=a0, a_c0=d0, post=post_s, nA=Nt.n, nB=128, nO=128)
    inv.ctrans(St, S, 128)
    emit_alpha(inv, St, 0, 0, 128, 128, bufs["alpha"], bufs["scr"],
               consts["ones_col"], consts["ones_row"], bufs["t11"],
               bufs["ps11"], bufs["ps128"])
    Si = inv.newton(St, XS, Xs, Xt, M128, consts["twoI128"], 128, leaf_iters,
                    w_r0=0, alpha_bc=bufs["alpha"][:, 0:1])
    if Si is not XS:
        inv.dve(lambda v: (v.tensor_copy(XS.re[:, :], Si.re[:, :]),
                           v.tensor_copy(XS.im[:, :], Si.im[:, :]))[-1])
    SiT = bufs["SiT128"]
    inv.ctrans(SiT, XS, 128)
    inv.cmm(Q, Nt, XA, 128, a_r0=a0, a_c0=d0, nA=Nt.n, nB=128, nO=128)
    inv.cmm(SiQ, SiT, Q, 128, nA=128, nB=128, nO=128)
    inv.ctrans(Pt, P_, 128)
    def post_tl(v, psr, psi, o_re, o_im, mc):
        v.tensor_add(o_re, psr, XA.re[:, 0:128])
        return v.tensor_add(o_im, psi, XA.im[:, 0:128])
    inv.cmm(out, Pt, SiQ, 128, o_r0=0, o_c0=0, post=post_tl, nA=128, nB=128)
    def post_neg(v, psr, psi, o_re, o_im, mc):
        v.tensor_scalar_mul(o_re, psr, -1.0)
        return v.tensor_scalar_mul(o_im, psi, -1.0)
    inv.cmm(out, Pt, XS, 128, o_r0=0, o_c0=128, post=post_neg, nA=128, nB=128)
    oc = 1
    def emit_bl(v):
        v.tensor_scalar_mul(out.re[:, oc * out.n + 0: oc * out.n + 128], SiQ.re[:, :], -1.0)
        v.tensor_scalar_mul(out.im[:, oc * out.n + 0: oc * out.n + 128], SiQ.im[:, :], -1.0)
        v.tensor_copy(out.re[:, oc * out.n + 128: oc * out.n + 256], XS.re[:, :])
        return v.tensor_copy(out.im[:, oc * out.n + 128: oc * out.n + 256], XS.im[:, :])
    inv.dve(emit_bl)
    inv.ctrans(outT, out, 256)


def emit_invert512(inv: Inv, W: CM, Wt: CM, Wi: CM, bufs, consts,
                   leaf_iters=30, polish=6):
    Ai, AiT = bufs["Ai256"], bufs["AiT256"]
    emit_inv256(inv, W, Wt, 0, Ai, AiT, bufs, consts, leaf_iters)
    P_, Pt, Q, S, St, SiQ = bufs["P256"], bufs["Pt256"], bufs["Q256"], \
        bufs["S256"], bufs["St256"], bufs["SiQ256"]
    inv.cmm(P_, AiT, W, 256, b_r0=0, b_c0=256, nB=W.n)
    def post_s(v, psr, psi, o_re, o_im, mc):
        dc = (256 + mc * 128) // 128
        v.scalar_tensor_tensor(o_re, psr, -1.0,
                               W.re[:, dc * W.n + 256: dc * W.n + 512], ALU.mult, ALU.add)
        return v.scalar_tensor_tensor(o_im, psi, -1.0,
                                      W.im[:, dc * W.n + 256: dc * W.n + 512], ALU.mult, ALU.add)
    inv.cmm(S, Wt, P_, 256, a_r0=0, a_c0=256, post=post_s, nA=Wt.n, nB=256, nO=256)
    inv.ctrans(St, S, 256)
    Si, SiT = bufs["Si256"], bufs["SiT256"]
    emit_inv256(inv, S, St, 0, Si, SiT, bufs, consts, leaf_iters)
    inv.cmm(Q, Wt, Ai, 256, a_r0=0, a_c0=256, nA=Wt.n, nB=256, nO=256)
    inv.cmm(SiQ, SiT, Q, 256, nA=256, nB=256, nO=256)
    inv.ctrans(Pt, P_, 256)
    def post_tl(v, psr, psi, o_re, o_im, mc):
        v.tensor_add(o_re, psr, Ai.re[:, mc * 256: mc * 256 + 256])
        return v.tensor_add(o_im, psi, Ai.im[:, mc * 256: mc * 256 + 256])
    inv.cmm(Wi, Pt, SiQ, 256, o_r0=0, o_c0=0, post=post_tl, nA=256, nB=256)
    def post_neg(v, psr, psi, o_re, o_im, mc):
        v.tensor_scalar_mul(o_re, psr, -1.0)
        return v.tensor_scalar_mul(o_im, psi, -1.0)
    inv.cmm(Wi, Pt, Si, 256, o_r0=0, o_c0=256, post=post_neg, nA=256, nB=256)
    def emit_b(v):
        last = None
        for mc in range(2):
            oc = 2 + mc
            last = v.tensor_scalar_mul(Wi.re[:, oc * Wi.n + 0: oc * Wi.n + 256],
                                       SiQ.re[:, mc * 256: mc * 256 + 256], -1.0)
            last = v.tensor_scalar_mul(Wi.im[:, oc * Wi.n + 0: oc * Wi.n + 256],
                                       SiQ.im[:, mc * 256: mc * 256 + 256], -1.0)
            last = v.tensor_copy(Wi.re[:, oc * Wi.n + 256: oc * Wi.n + 512],
                                 Si.re[:, mc * 256: mc * 256 + 256])
            last = v.tensor_copy(Wi.im[:, oc * Wi.n + 256: oc * Wi.n + 512],
                                 Si.im[:, mc * 256: mc * 256 + 256])
        return last
    inv.dve(emit_b)
    if polish > 0:
        res = inv.newton(Wt, Wi, bufs["Xs512"], bufs["Xt512"], bufs["M512"],
                         consts["twoI512"], 512, polish, w_r0=0, alpha_bc=None)
        if res is not Wi:
            inv.dve(lambda v: (v.tensor_copy(Wi.re[:, :], res.re[:, :]),
                               v.tensor_copy(Wi.im[:, :], res.im[:, :]))[-1])


F32 = np.float32


def chunked(M):
    r, c = M.shape
    nch = r // 128
    out = np.empty((128, nch * c), M.dtype)
    for k in range(nch):
        out[:, k * c:(k + 1) * c] = M[k * 128:(k + 1) * 128, :]
    return np.ascontiguousarray(out)


def unchunk(T, rows, cols):
    nch = rows // 128
    M = np.empty((rows, cols), T.dtype)
    for k in range(nch):
        M[k * 128:(k + 1) * 128, :] = T[:, k * cols:(k + 1) * cols]
    return M


def twoI_tile(n):
    nch = n // 128
    t = np.zeros((128, nch * n), np.float32)
    for c in range(nch):
        for p in range(128):
            t[p, c * n + c * 128 + p] = 2.0
    return t


def vec_chunk(v):
    n = v.shape[0]
    return np.ascontiguousarray(v.reshape(n // 128, 128).T.astype(v.dtype))


PERM_REIM = np.concatenate([np.arange(0, 1024, 2), np.arange(1, 1024, 2)])


def ab_inputs(xb, d, T):
    f = lambda a: np.ascontiguousarray(np.asarray(a, np.float32))
    w_a, w_x = f(d["L0_attn_w_a"]), f(d["L0_attn_w_x"])
    V_r, V_i = f(d["L0_attn_mat_v_real"]), f(d["L0_attn_mat_v_imag"])
    out = dict(
        xT=chunked(f(xb).T.copy()),
        Vt_r=chunked(V_r.T.copy()), Vt_i=chunked(V_i.T.copy()),
        Vn_r=chunked(V_r), Vn_i=chunked(V_i),
        waT=chunked(np.ascontiguousarray(w_a.T[:, PERM_REIM])),
        wxT=chunked(np.ascontiguousarray(w_x.T[:, PERM_REIM])),
        woT=chunked(f(d["L0_attn_w_out"]).T.copy()),
        h0r=vec_chunk(f(d["L0_attn_last_hidden_real"])),
        h0i=vec_chunk(f(d["L0_attn_last_hidden_imag"])),
        W1T=chunked(f(d["L0_ffn_W1"]).T.copy()),
        b1c=vec_chunk(f(d["L0_ffn_b1"])),
        W2T=chunked(f(d["L0_ffn_W2"]).T.copy()),
        b2c=vec_chunk(f(d["L0_ffn_b2"])),
        ident=np.eye(128, dtype=np.float32),
        twoI512=twoI_tile(512), twoI128=twoI_tile(128),
        onesc=np.ones((128, 1), np.float32), onesr=np.ones((1, 128), np.float32),
        onesW=np.ones((128, 512), np.float32),
    )
    return out


def c_inputs(x2b_perm, d, T):
    f = lambda a: np.ascontiguousarray(np.asarray(a, np.float32))
    masks = np.zeros((128, 4 * 512), np.float32)
    ii = np.arange(128)[:, None]
    jj = np.arange(512)[None, :]
    for p in range(4):
        masks[:, p * 512:(p + 1) * 512] = np.where(ii > jj - 128 * p, -1e30, 0.0)
    return dict(
        x2T=chunked(f(x2b_perm).T.copy()),
        dqc=chunked(f(d["L1_attn_W_DQ"])),
        uqt=chunked(f(d["L1_attn_W_UQ"]).T.copy()),
        dkvc=chunked(f(d["L1_attn_W_DKV"])),
        ukvt=chunked(f(d["L1_attn_W_UKV"]).T.copy()),
        qrt=f(d["L1_attn_W_QR"]).T.copy(),
        krt=f(d["L1_attn_W_KR"]).T.copy(),
        wot=f(d["L1_attn_W_O"]).T.copy(),
        qnw=f(d["L1_attn_q_norm_w"]).reshape(64, 1),
        knw=f(d["L1_attn_k_norm_w"]).reshape(64, 1),
        identb=np.eye(64, dtype=np.float32),
        identf=np.eye(128, dtype=np.float32),
        masks=masks,
        onesc=np.ones((128, 1), np.float32), onesr=np.ones((1, 128), np.float32),
    )


def head_perm(j):
    mine = np.arange(256 * j, 256 * j + 256)
    rest = np.concatenate([np.arange(0, 256 * j), np.arange(256 * j + 256, 1024)])
    return np.concatenate([mine, rest])


def f_inputs(xb, d, T, r):
    f = lambda a: np.ascontiguousarray(np.asarray(a, np.float32))
    out = ab_inputs(xb, d, T)
    W1 = f(d["L1_ffn_W1"])
    w1e = np.empty((128, 4 * 32 * 1024), np.float32)
    for e in range(4):
        ck = chunked(W1[e])
        for i in range(32):
            for k in range(8):
                w1e[:, (e * 32 + i) * 1024 + k * 128:(e * 32 + i) * 1024 + (k + 1) * 128] = \
                    ck[:, k * 4096 + i * 128: k * 4096 + (i + 1) * 128]
    masks = np.zeros((128, 16 * 512), np.float32)
    ii = np.arange(128)[:, None]
    jj = np.arange(512)[None, :]
    qpos = 128 * (4 * (jj // 128) + r) + (jj % 128)
    for dd in range(16):
        masks[:, dd * 512:(dd + 1) * 512] = np.where(128 * dd + ii <= qpos, 0.0, -1e30)
    out.update(
        dqc=chunked(f(d["L1_attn_W_DQ"])),
        uqt=chunked(f(d["L1_attn_W_UQ"]).T.copy()),
        dkvc=chunked(f(d["L1_attn_W_DKV"])),
        ukvt=chunked(f(d["L1_attn_W_UKV"]).T.copy()),
        qrt=f(d["L1_attn_W_QR"]).T.copy(),
        krt=f(d["L1_attn_W_KR"]).T.copy(),
        wot=f(d["L1_attn_W_O"]).T.copy(),
        qnw=f(d["L1_attn_q_norm_w"]).reshape(64, 1),
        knw=f(d["L1_attn_k_norm_w"]).reshape(64, 1),
        masks16=masks,
        rmask=np.ascontiguousarray(
            np.broadcast_to((np.arange(4) == r).astype(np.float32), (128, 4))),
        wrc=chunked(f(d["L1_ffn_W_router"])),
        w1e=w1e,
        b1cD=np.ascontiguousarray(f(d["L1_ffn_b1"]).reshape(4, 32, 128)
                                  .transpose(2, 0, 1).reshape(128, 128)),
        woc=chunked(f(d["L1_ffn_Wo"])),
        boc=vec_chunk(f(d["L1_ffn_bo"])),
        gam=vec_chunk(f(d["L1_ffn_ln_gamma"])),
        bet=vec_chunk(f(d["L1_ffn_ln_beta"])),
    )
    return out


def f_reassemble(outs, T):
    TD = T // 4
    res = np.empty((2, T, 1024), np.float32)
    for c in range(8):
        b, r = c // 4, c % 4
        o = unchunk(outs[c], 1024, TD)
        for blk in range(TD // 128):
            g = 4 * blk + r
            res[b, 128 * g:128 * g + 128, :] = o[:, 128 * blk:128 * blk + 128].T
    return res




F32 = mybir.dt.float32
BF16 = mybir.dt.bfloat16
I32 = mybir.dt.int32
AF = mybir.ActivationFunctionType
ALU = mybir.AluOpType
AX = mybir.AxisListType.X
PI = float(np.pi)

W = 512
W2W = 256


AB_INPUTS = [
    ("xT", None), ("Vt_r", [128, 2048]), ("Vt_i", [128, 2048]),
    ("Vn_r", [128, 2048]), ("Vn_i", [128, 2048]),
    ("waT", [128, 8 * 1024]), ("wxT", [128, 8 * 1024]),
    ("woT", [128, 4 * 1024]), ("h0r", [128, 4]), ("h0i", [128, 4]),
    ("W1T", [128, 8 * 4096]), ("b1c", [128, 32]),
    ("W2T", [128, 32 * 1024]), ("b2c", [128, 8]),
    ("ident", [128, 128]), ("twoI512", [128, 2048]), ("twoI128", [128, 128]),
    ("onesc", [128, 1]), ("onesr", [1, 128]), ("onesW", [128, 512]),
]


def build_AB(T=8192, leaf_iters=26, polish=6):
    nc = bass.Bass("TRN2", target_bir_lowering=False)
    din = {}
    for nm, shp in AB_INPUTS:
        if shp is None:
            shp = [128, 8 * T]
        din[nm] = nc.dram_tensor(nm, shp, F32, kind="ExternalInput").ap()
    x2T_out = nc.dram_tensor("x2T", [128, 8 * T], F32, kind="ExternalOutput").ap()
    stack = contextlib.ExitStack()
    block = stack.enter_context(nc.Block())
    p = Prog(nc, block, stack)
    p.alloc_psum(n_big=4)
    emit_L0(nc, stack, block, p, din, x2T_out, T, leaf_iters, polish)

    @block.sync
    def _(s):
        s.wait_ge(p.qs, p.nqs)

    nc._stack = stack
    return nc


def emit_L0(nc, stack, block, p, din, x2T_out, T, leaf_iters=26, polish=6):
    nslab = T // W
    x1_hbm = nc.dram_tensor("x1_scratch", [128, 8 * T], F32)
    outer_stack = p.stack
    l0_stack = contextlib.ExitStack()
    p.stack = l0_stack

    sident = p.sb("sident", [128, 128], F32)
    s2I128 = p.sb("s2I128", [128, 128], F32)
    onesc = p.sb("sonesc", [128, 1], F32)
    onesr = p.sb("sonesr", [1, 128], F32)
    onesW = p.sb("sonesW", [128, 512], F32)
    h0r = p.sb("sh0r", [128, 4], F32)
    h0i = p.sb("sh0i", [128, 4], F32)
    xs = p.sb("t_x", [128, 8 * W], F32)
    xn = p.sb("t_xn", [128, 8 * W], BF16)
    onescb = p.sb("sonescb", [128, 1], BF16)
    lnsc = make_ln_scratch(p, W)
    lnsc["ones_col"] = onesc[:, :]
    lnsc["ones_col_b"] = onescb[:, :]
    lnsc["ones_row"] = onesr[:, :]

    for nm, tile in [("ident", sident), ("twoI128", s2I128), ("onesc", onesc),
                     ("onesr", onesr), ("onesW", onesW), ("h0r", h0r), ("h0i", h0i)]:
        p.dma_s(tile[:, :], din[nm])

    p1_stack = contextlib.ExitStack()

    def p1sb(name, shape, dtype=F32):
        return p1_stack.enter_context(nc.sbuf_tensor(name, shape, dtype))

    Wi_bf_r = p1sb("Wi_bf_r", [128, 2048], BF16)
    Wi_bf_i = p1sb("Wi_bf_i", [128, 2048], BF16)
    Wi_bf_ni = p1sb("Wi_bf_ni", [128, 2048], BF16)
    Vt_bf_r = p1sb("Vt_bf_r", [128, 2048], BF16)
    Vt_bf_ni = p1sb("Vt_bf_ni", [128, 2048], BF16)

    inv_stack = contextlib.ExitStack()
    inv = Inv(nc, block, inv_stack, sident[:, :], psum_from=p)
    s2I512 = inv_stack.enter_context(nc.sbuf_tensor("s2I512", [128, 2048], F32))

    Wm = inv.alloc("Wm", 512)
    Wt = inv.alloc("Wt", 512, nim=True)
    Wi = inv.alloc("Wi", 512)
    consts_inv = {"twoI512": s2I512[:, :], "twoI128": s2I128[:, :],
                  "ones_col": onesc[:, :], "ones_row": onesr[:, :]}
    sscr0 = inv_stack.enter_context(nc.sbuf_tensor("sscr0", [1, 1], F32))
    glue_stack = contextlib.ExitStack()
    inv.stack = glue_stack
    bufs = {}
    for nm, n, nim in [
        ("X128a", 128, False), ("X128b", 128, False), ("Xs128", 128, False),
        ("Xt128", 128, True), ("M128", 128, False), ("P128", 128, False),
        ("Pt128", 128, True), ("Q128", 128, False), ("S128", 128, False),
        ("St128", 128, True), ("SiQ128", 128, False), ("AiT128", 128, True),
        ("SiT128", 128, True),
        ("Ai256", 256, False), ("AiT256", 256, True), ("P256", 256, False),
        ("Pt256", 256, True), ("Q256", 256, False), ("S256", 256, False),
        ("St256", 256, True), ("SiQ256", 256, False), ("Si256", 256, False),
        ("SiT256", 256, True),
    ]:
        bufs[nm] = inv.alloc(nm, n, nim=nim)
    salpha = glue_stack.enter_context(nc.sbuf_tensor("salpha", [128, 1], F32))
    sscr = glue_stack.enter_context(nc.sbuf_tensor("sscr", [128, 516], F32))
    st11 = glue_stack.enter_context(nc.sbuf_tensor("st11", [1, 1], F32))
    bufs["alpha"] = salpha[:, 0:1]
    bufs["scr"] = sscr[:, :]
    bufs["t11"] = st11[:, :]
    bufs["ps11"] = p.misc1[0:1, 0:1]
    bufs["ps128"] = p.misc2[:, 0:1]

    p.dma_s(s2I512[:, :], din["twoI512"])
    p.dma_s(Wm.re[:, :], din["Vt_r"])
    p.dma_s(Wm.im[:, :], din["Vt_i"])
    p.dma_s(Wt.re[:, :], din["Vn_r"])
    p.dma_s(Wt.im[:, :], din["Vn_i"])

    @block.vector
    def _(v):
        v.wait_ge(p.qs, p.nqs)
        v.memset(sscr0[:, 0:1], 0.0).then_inc(inv.dv, 1)
    inv.ndv += 1

    inv.make_nim(Wt)
    emit_invert512(inv, Wm, Wt, Wi, bufs, consts_inv,
                      leaf_iters=leaf_iters, polish=0)
    glue_stack.close()
    pol_stack = contextlib.ExitStack()
    inv.stack = pol_stack
    Xs512 = inv.alloc("Xs512", 512)
    Xt512 = inv.alloc("Xt512", 512, nim=True)
    M512 = inv.alloc("M512", 512)
    res = inv.newton(Wt, Wi, Xs512, Xt512, M512, s2I512[:, :], 512, polish)
    if res is not Wi:
        inv.dve(lambda v: (v.tensor_copy(Wi.re[:, :], res.re[:, :]),
                           v.tensor_copy(Wi.im[:, :], res.im[:, :]))[-1])
    pol_stack.close()
    inv.stack = inv_stack

    def cast(v):
        v.tensor_copy(Wi_bf_r[:, :], Wi.re[:, :])
        v.tensor_copy(Wi_bf_i[:, :], Wi.im[:, :])
        v.tensor_scalar_mul(Wi_bf_ni[:, :], Wi.im[:, :], -1.0)
        v.tensor_copy(Vt_bf_r[:, :], Wm.re[:, :])
        v.tensor_scalar_mul(Vt_bf_ni[:, :], Wm.im[:, :], -1.0)
    inv.dve(cast)

    @block.vector
    def _(v):
        v.wait_ge(inv.dv, inv.ndv)
        v.memset(p.bar[:, :], 0.0).then_inc(p.el, 1)
    p.nel += 1
    inv_stack.close()

    waT = p1sb("s_waT", [128, 8 * 1024], BF16)
    wxT = p1sb("s_wxT", [128, 8 * 1024], BF16)
    woT = p1sb("s_woT", [128, 4 * 1024], BF16)
    p.dma_g(waT[:, :], din["waT"], wait=[(p.el, p.nel)])
    p.dma_g(wxT[:, :], din["wxT"])
    p.dma_g(woT[:, :], din["woT"])
    n_wdma = p.nqg

    ar = p1sb("t_ar", [128, 4 * W], BF16)
    ai = p1sb("t_ai", [128, 4 * W], BF16)
    xr = p1sb("t_xr", [128, 4 * W], BF16)
    xi = p1sb("t_xi", [128, 4 * W], BF16)
    fbig = p1sb("t_fbig", [128, 12 * W], F32)
    f1 = fbig[:, 0 * W:4 * W]
    f2 = fbig[:, 4 * W:8 * W]
    f3 = fbig[:, 8 * W:12 * W]
    x1s = fbig[:, 0:8 * W]
    rho = p1sb("t_rho", [128, 4 * W], F32)
    sinT = p1sb("t_sin", [128, 4 * W], BF16)
    cosT = p1sb("t_cos", [128, 4 * W], BF16)
    gar = p1sb("t_gar", [128, 4 * W], BF16)
    gai = p1sb("t_gai", [128, 4 * W], BF16)
    zr = p1sb("t_zr", [128, 4 * W], F32)
    zi = p1sb("t_zi", [128, 4 * W], F32)
    thh = zr
    Th = zi
    WN = W + 1
    winr = p1sb("t_winr", [128, 4 * WN], BF16)
    wini = p1sb("t_wini", [128, 4 * WN], BF16)
    hrt = p1sb("t_hr", [128, 4 * W], BF16)
    ur, ui = xr, xi
    btr, bti = ar, ai
    mo = xn
    carT = p1sb("c_T", [128, 4], F32)
    carZr = p1sb("c_zr", [128, 4], F32)
    carZi = p1sb("c_zi", [128, 4], F32)

    def ch(tile, c, w=W):
        return tile[:, c * w:(c + 1) * w]

    def init_car(v):
        v.memset(onescb[:, :], 1.0)
        v.memset(carT[:, :], 0.0)
        v.memset(carZr[:, :], 0.0)
        v.memset(carZi[:, :], 0.0)
        for c in range(4):
            v.tensor_copy(winr[:, c * WN: c * WN + 1], h0r[:, c:c + 1])
            v.tensor_copy(wini[:, c * WN: c * WN + 1], h0i[:, c:c + 1])
    p.vec(init_car)

    for s in range(nslab):
        col0 = s * W
        for c in range(8):
            p.dma_s(ch(xs, c), din["xT"][:, c * T + col0: c * T + col0 + W],
                    wait=([(p.el, p.nel)] if c == 0 else None))
        dma_tgt = p.nqs
        p.el_barrier([(p.qs, dma_tgt), (p.qg, n_wdma)])

        emit_ln(p, lambda c: ch(xs, c), lambda c: ch(xn, c), W, lnsc)

        def evac_a(m, ps):
            dst = ar if m < 4 else ai
            p.vec(lambda v, dst=dst, mm=m % 4, ps=ps:
                  v.tensor_copy(ch(dst, mm), ps[:, 0:W]))
        emit_mm(p, None,
                  lambda k, m: waT[:, k * 1024 + m * 128: k * 1024 + m * 128 + 128],
                  lambda k: ch(xn, k), 8, 8, W, evac_a)

        def evac_x(m, ps):
            dst = xr if m < 4 else xi
            p.vec(lambda v, dst=dst, mm=m % 4, ps=ps:
                  v.tensor_copy(ch(dst, mm), ps[:, 0:W]))
        emit_mm(p, None,
                  lambda k, m: wxT[:, k * 1024 + m * 128: k * 1024 + m * 128 + 128],
                  lambda k: ch(xn, k), 8, 8, W, evac_x)

        FW = slice(0, 4 * W)

        def xcf(v):
            v.tensor_mul(f1[:, FW], xr[:, FW], xr[:, FW])
            v.tensor_mul(f2[:, FW], xi[:, FW], xi[:, FW])
            v.tensor_add(f1[:, FW], f1[:, FW], f2[:, FW])
        p.vec(xcf)

        def sqxf(a):
            a.activation(f2[:, FW], f1[:, FW], AF.Sqrt)
        p.act(sqxf)

        def xc2(v):
            for c in range(4):
                v.tensor_mul(winr[:, c * WN + 1: c * WN + 1 + W], ch(xr, c), ch(f2, c))
                v.tensor_mul(wini[:, c * WN + 1: c * WN + 1 + W], ch(xi, c), ch(f2, c))
        p.vec(xc2)

        def gate(v):
            v.tensor_mul(f1[:, FW], ar[:, FW], ar[:, FW])
            v.tensor_mul(f2[:, FW], ai[:, FW], ai[:, FW])
            v.tensor_add(f1[:, FW], f1[:, FW], f2[:, FW])
        p.vec(gate)

        def sqf(a):
            a.activation(f2[:, FW], f1[:, FW], AF.Sqrt)
        p.act(sqf)

        def gate2(v):
            v.tensor_scalar_add(f3[:, FW], f1[:, FW], 1.0)
            v.reciprocal(f3[:, FW], f3[:, FW])
            v.tensor_mul(rho[:, FW], f1[:, FW], f3[:, FW])
            v.tensor_mul(f3[:, FW], f2[:, FW], f3[:, FW])
            v.tensor_mul(gar[:, FW], f3[:, FW], ar[:, FW])
            v.tensor_mul(gai[:, FW], f3[:, FW], ai[:, FW])
            v.tensor_add(f2[:, FW], f2[:, FW], ar[:, FW])
            v.tensor_mul(f3[:, FW], f2[:, FW], f2[:, FW])
            v.tensor_mul(f1[:, FW], ai[:, FW], ai[:, FW])
            v.tensor_add(f3[:, FW], f3[:, FW], f1[:, FW])
        p.vec(gate2)

        def sq2f(a):
            a.activation(f1[:, FW], f3[:, FW], AF.Sqrt)
        p.act(sq2f)

        def gate3(v):
            v.tensor_add(f1[:, FW], f1[:, FW], f2[:, FW])
            v.tensor_scalar_max(f1[:, FW], f1[:, FW], 1e-30)
            v.reciprocal(f1[:, FW], f1[:, FW])
            v.tensor_mul(f2[:, FW], f1[:, FW], ai[:, FW])
        p.vec(gate3)

        def atanf(a):
            a.activation(thh[:, FW], f2[:, FW], AF.Arctan)
        p.act(atanf)

        for mch in range(4):
            ps = p.psum()

            def ufn(t, mch=mch, ps=ps):
                for k in range(4):
                    t.matmul(ps[:, 0:W],
                             Wi_bf_r[:, k * 512 + mch * 128: k * 512 + mch * 128 + 128],
                             winr[:, k * WN: k * WN + W], start=(k == 0), stop=False)
                last = None
                for k in range(4):
                    last = t.matmul(
                        ps[:, 0:W],
                        Wi_bf_ni[:, k * 512 + mch * 128: k * 512 + mch * 128 + 128],
                        wini[:, k * WN: k * WN + W], start=False, stop=(k == 3))
                return last
            p.tensor(ufn)
            p.vec(lambda v, mch=mch, ps=ps: v.tensor_copy(ch(ur, mch), ps[:, 0:W]))
            ps2 = p.psum()

            def ufn2(t, mch=mch, ps=ps2):
                for k in range(4):
                    t.matmul(ps[:, 0:W],
                             Wi_bf_i[:, k * 512 + mch * 128: k * 512 + mch * 128 + 128],
                             winr[:, k * WN: k * WN + W], start=(k == 0), stop=False)
                last = None
                for k in range(4):
                    last = t.matmul(
                        ps[:, 0:W],
                        Wi_bf_r[:, k * 512 + mch * 128: k * 512 + mch * 128 + 128],
                        wini[:, k * WN: k * WN + W], start=False, stop=(k == 3))
                return last
            p.tensor(ufn2)
            p.vec(lambda v, mch=mch, ps=ps2: v.tensor_copy(ch(ui, mch), ps[:, 0:W]))

        def scans1(v):
            for c in range(4):
                v.tensor_tensor_scan(ch(Th, c), onesW[:, 0:W], ch(thh, c),
                                     carT[:, c:c + 1], ALU.mult, ALU.add)
            for c in range(4):
                v.tensor_copy(carT[:, c:c + 1], Th[:, c * W + W - 1: c * W + W])
        p.vec(scans1)

        MAGIC = 12582912.0

        def rr1(v):
            H = Th[:, FW]
            v.tensor_scalar_mul(f1[:, FW], H, 2.0 / PI)
            v.tensor_scalar_add(f1[:, FW], f1[:, FW], MAGIC)
            v.tensor_scalar_add(f1[:, FW], f1[:, FW], -MAGIC)
            v.scalar_tensor_tensor(f1[:, FW], f1[:, FW], -PI / 2, H,
                                   ALU.mult, ALU.add)
            v.tensor_scalar(f2[:, FW], H, 2.0 / PI, 0.25, ALU.mult, ALU.add)
            v.tensor_scalar_add(f2[:, FW], f2[:, FW], MAGIC)
            v.tensor_scalar_add(f2[:, FW], f2[:, FW], -MAGIC)
            v.tensor_scalar_mul(f2[:, FW], f2[:, FW], -PI / 2)
            v.tensor_add(f2[:, FW], f2[:, FW], H)
            v.tensor_scalar_add(f2[:, FW], f2[:, FW], PI / 8)
            for t_ in (f1[:, FW], f2[:, FW]):
                v.tensor_scalar_min(t_, t_, 0.785396)
                v.tensor_scalar_max(t_, t_, -0.785396)
        p.vec(rr1)

        def sincos(a):
            a.activation(sinT[:, FW], f1[:, FW], AF.Sin, scale=4.0)
            a.activation(cosT[:, FW], f2[:, FW], AF.Sin, scale=4.0)
        p.act(sincos)

        def bmix(v):
            v.tensor_mul(f1[:, FW], gar[:, FW], ur[:, FW])
            v.tensor_mul(f2[:, FW], gai[:, FW], ui[:, FW])
            v.tensor_sub(f1[:, FW], f1[:, FW], f2[:, FW])
            v.tensor_mul(f2[:, FW], gar[:, FW], ui[:, FW])
            v.tensor_mul(f3[:, FW], gai[:, FW], ur[:, FW])
            v.tensor_add(f2[:, FW], f2[:, FW], f3[:, FW])
            v.tensor_mul(f3[:, FW], cosT[:, FW], f1[:, FW])
            v.tensor_mul(btr[:, FW], sinT[:, FW], f2[:, FW])
            v.tensor_add(btr[:, FW], btr[:, FW], f3[:, FW])
            v.tensor_mul(f3[:, FW], cosT[:, FW], f2[:, FW])
            v.tensor_mul(f2[:, FW], sinT[:, FW], f1[:, FW])
            v.tensor_sub(bti[:, FW], f3[:, FW], f2[:, FW])
        p.vec(bmix)

        def scans2(v):
            for c in range(4):
                v.tensor_tensor_scan(ch(zr, c), ch(rho, c), ch(btr, c),
                                     carZr[:, c:c + 1], ALU.mult, ALU.add)
                v.tensor_tensor_scan(ch(zi, c), ch(rho, c), ch(bti, c),
                                     carZi[:, c:c + 1], ALU.mult, ALU.add)
            for c in range(4):
                v.tensor_copy(carZr[:, c:c + 1], zr[:, c * W + W - 1: c * W + W])
                v.tensor_copy(carZi[:, c:c + 1], zi[:, c * W + W - 1: c * W + W])
        p.vec(scans2)

        sr, si = gar, gai

        def smix(v):
            v.tensor_mul(f1[:, FW], cosT[:, FW], zr[:, FW])
            v.tensor_mul(f2[:, FW], sinT[:, FW], zi[:, FW])
            v.tensor_sub(sr[:, FW], f1[:, FW], f2[:, FW])
            v.tensor_mul(f1[:, FW], cosT[:, FW], zi[:, FW])
            v.tensor_mul(f2[:, FW], sinT[:, FW], zr[:, FW])
            v.tensor_add(si[:, FW], f1[:, FW], f2[:, FW])
        p.vec(smix)

        for mcH in range(4):
            ps = p.psum()

            def hfn(t, mcH=mcH, ps=ps):
                for k in range(4):
                    t.matmul(ps[:, 0:W],
                             Vt_bf_r[:, k * 512 + mcH * 128: k * 512 + mcH * 128 + 128],
                             ch(sr, k), start=(k == 0), stop=False)
                last = None
                for k in range(4):
                    last = t.matmul(
                        ps[:, 0:W],
                        Vt_bf_ni[:, k * 512 + mcH * 128: k * 512 + mcH * 128 + 128],
                        ch(si, k), start=False, stop=(k == 3))
                return last
            p.tensor(hfn)
            p.vec(lambda v, mcH=mcH, ps=ps:
                  v.tensor_add(ch(hrt, mcH), ps[:, 0:W],
                               winr[:, mcH * WN + 1: mcH * WN + 1 + W]))

        def wincar(v):
            for c in range(4):
                v.tensor_copy(winr[:, c * WN: c * WN + 1],
                              winr[:, c * WN + W: c * WN + W + 1])
                v.tensor_copy(wini[:, c * WN: c * WN + 1],
                              wini[:, c * WN + W: c * WN + W + 1])
        p.vec(wincar)

        def evac_mo(m, ps):
            p.vec(lambda v, m=m, ps=ps: v.tensor_copy(ch(mo, m), ps[:, 0:W]))
        emit_mm(p, None,
                  lambda k, m: woT[:, k * 1024 + m * 128: k * 1024 + m * 128 + 128],
                  lambda k: ch(hrt, k), 4, 8, W, evac_mo)

        emit_ln(p, lambda c: ch(mo, c), lambda c: ch(x1s, c), W, lnsc)

        def resid1(v):
            for c in range(8):
                v.tensor_add(ch(x1s, c), ch(x1s, c), ch(xs, c))
        p.vec(resid1)

        for c in range(8):
            p.dma_s(x1_hbm.ap()[:, c * T + col0: c * T + col0 + W], ch(x1s, c),
                    wait=([(p.el, p.nel)] if c == 0 else None))

    p.el_barrier([(p.qs, p.nqs)])
    p1_stack.close()

    WF = W2W
    nslab2 = T // WF
    W1t = p.sb("s_W1t", [128, 8 * 4096], BF16)
    W2t = p.sb("s_W2t", [128, 32 * 1024], BF16)
    b1c = p.sb("s_b1c", [128, 32], F32)
    b2c = p.sb("s_b2c", [128, 8], F32)
    p.dma_s(b1c[:, :], din["b1c"], wait=[(p.el, p.nel)])
    p.dma_s(b2c[:, :], din["b2c"])
    h1 = p.sb("t_h1", [128, 32 * WF], BF16)
    wld = h1[:, 0:2048].bitcast(F32)
    wabs = h1[:, 2048:4096].bitcast(F32)
    accv = p.sb("t_acc", [128, 2], F32)
    sc11 = p.sb("t_sc11", [1, 1], F32)
    scbc = p.sb("t_scbc", [128, 1], F32)
    x2s = lnsc["sq"]
    fo = xn

    def ternarize(dram_ap, nsub, width, out_tile, n_elem):
        for c in range(nsub):
            p.dma_s(wld[:, 0:width], dram_ap[:, c * width:(c + 1) * width],
                    wait=[(p.el, p.nel)])
            tgt = p.nqs

            def absf(a, c=c):
                a.activation(wabs[:, 0:width], wld[:, 0:width], AF.Abs)
            p.act(absf, wait_dma=[(p.qs, tgt)])

            def red(v, c=c):
                if c == 0:
                    v.reduce_sum(accv[:, 0:1], wabs[:, 0:width], axis=AX)
                else:
                    v.reduce_sum(accv[:, 1:2], wabs[:, 0:width], axis=AX)
                    v.tensor_add(accv[:, 0:1], accv[:, 0:1], accv[:, 1:2])
            p.vec(red)

        def pt(t):
            return t.matmul(p.misc1[0:1, 0:1], accv[:, 0:1], onesc[:, :],
                            start=True, stop=True)
        p.tensor(pt)

        def scale_f(v):
            v.tensor_scalar_mul(sc11[:, :], p.misc1[0:1, 0:1], 1.0 / n_elem)
            v.tensor_scalar_max(sc11[:, :], sc11[:, :], 1e-8)
            v.reciprocal(sc11[:, :], sc11[:, :])
        p.vec(scale_f)

        def bcm(t):
            return t.matmul(p.misc2[:, 0:1], onesr[:, :], sc11[:, :],
                            start=True, stop=True)
        p.tensor(bcm)
        p.vec(lambda v: v.tensor_copy(scbc[:, :], p.misc2[:, 0:1]))

        for c in range(nsub):
            p.dma_s(wld[:, 0:width], dram_ap[:, c * width:(c + 1) * width],
                    wait=[(p.el, p.nel)])
            tgt = p.nqs

            def tern(v, c=c):
                MAGIC = 12582912.0
                v.tensor_scalar_mul(wabs[:, 0:width], wld[:, 0:width], scbc[:, 0:1])
                v.tensor_scalar_min(wabs[:, 0:width], wabs[:, 0:width], 2.5)
                v.tensor_scalar_max(wabs[:, 0:width], wabs[:, 0:width], -2.5)
                v.tensor_scalar_add(wabs[:, 0:width], wabs[:, 0:width], MAGIC)
                v.tensor_scalar_add(wabs[:, 0:width], wabs[:, 0:width], -MAGIC)
                v.tensor_scalar_min(wabs[:, 0:width], wabs[:, 0:width], 1.0)
                v.tensor_scalar_max(wabs[:, 0:width], wabs[:, 0:width], -1.0)
                v.tensor_copy(out_tile[:, c * width:(c + 1) * width],
                              wabs[:, 0:width])
            p.vec(tern, wait_dma=[(p.qs, tgt)])

    ternarize(din["W1T"], 32, 1024, W1t, 4096 * 1024)
    ternarize(din["W2T"], 32, 1024, W2t, 4096 * 1024)

    for s in range(nslab2):
        col0 = s * WF
        for c in range(8):
            p.dma_s(xs[:, c * WF:(c + 1) * WF],
                    x1_hbm.ap()[:, c * T + col0: c * T + col0 + WF],
                    wait=([(p.el, p.nel)] if c == 0 else None))
        tgt = p.nqs
        p.el_barrier([(p.qs, tgt)])

        emit_ln(p, lambda c: xs[:, c * WF:(c + 1) * WF],
                  lambda c: xn[:, c * WF:(c + 1) * WF], WF, lnsc)

        for m in range(32):
            ps = p.psum()

            def m1(t, m=m, ps=ps):
                last = None
                for k in range(8):
                    last = t.matmul(
                        ps[:, 0:WF],
                        W1t[:, k * 4096 + m * 128: k * 4096 + m * 128 + 128],
                        xn[:, k * WF:(k + 1) * WF], start=(k == 0), stop=(k == 7))
                return last
            p.tensor(m1)

            def ge(a, m=m, ps=ps):
                a.activation(h1[:, m * WF:(m + 1) * WF], ps[:, 0:WF], AF.Gelu,
                             bias=b1c[:, m:m + 1])
            p.act(ge)

        for m in range(8):
            ps = p.psum()

            def m2(t, m=m, ps=ps):
                last = None
                for k in range(32):
                    last = t.matmul(
                        ps[:, 0:WF],
                        W2t[:, k * 1024 + m * 128: k * 1024 + m * 128 + 128],
                        h1[:, k * WF:(k + 1) * WF], start=(k == 0), stop=(k == 31))
                return last
            p.tensor(m2)

            def bia(v, m=m, ps=ps):
                v.tensor_scalar_add(fo[:, m * WF:(m + 1) * WF], ps[:, 0:WF],
                                    b2c[:, m:m + 1])
            p.vec(bia)

        emit_ln(p, lambda c: fo[:, c * WF:(c + 1) * WF],
                  lambda c: x2s[:, c * WF:(c + 1) * WF], WF, lnsc)

        def resid2(v):
            for c in range(8):
                v.tensor_add(x2s[:, c * WF:(c + 1) * WF],
                             x2s[:, c * WF:(c + 1) * WF],
                             xs[:, c * WF:(c + 1) * WF])
        p.vec(resid2)

        for c in range(8):
            p.dma_s(x2T_out[:, c * T + col0: c * T + col0 + WF],
                    x2s[:, c * WF:(c + 1) * WF],
                    wait=([(p.el, p.nel)] if c == 0 else None))

    p.el_barrier([(p.qs, p.nqs), (p.qg, p.nqg)])
    p.stack = outer_stack
    l0_stack.close()
    return None


F32 = mybir.dt.float32
BF16 = mybir.dt.bfloat16
I32 = mybir.dt.int32
AF = mybir.ActivationFunctionType
ALU = mybir.AluOpType
AX = mybir.AxisListType.X

WS = 512
RMS_EPS = 1.1920929e-07
SCALE = 64.0 ** -0.5
LN_EPS_IN = 1e-12


def build_F(T=8192, leaf_iters=26, polish=6):
    TD = T // 4
    nslab = T // WS
    nheads = 16
    ntb = T // 128
    nc = bass.Bass("TRN2", target_bir_lowering=False)
    din = {}
    for nm, shp in AB_INPUTS:
        if shp is None:
            shp = [128, 8 * T]
        din[nm] = nc.dram_tensor(nm, shp, F32, kind="ExternalInput").ap()
    for nm, shp, dt in [
        ("dqc", [128, 4 * 64], F32), ("uqt", [128, 4 * 64], F32),
        ("dkvc", [128, 8 * 64], F32), ("ukvt", [128, 8 * 128], F32),
        ("qrt", [64, 64], F32), ("krt", [64, 64], F32), ("wot", [64, 64], F32),
        ("qnw", [64, 1], F32), ("knw", [64, 1], F32),
        ("masks16", [128, 16 * 512], F32), ("rmask", [128, 4], F32),
        ("wrc", [128, 8 * 4], F32), ("w1e", [128, 4 * 32 * 1024], F32),
        ("b1cD", [128, 4 * 32], F32), ("woc", [128, 32 * 1024], F32),
        ("boc", [128, 8], F32), ("gam", [128, 8], F32), ("bet", [128, 8], F32),
    ]:
        din[nm] = nc.dram_tensor(nm, shp, dt, kind="ExternalInput").ap()
    outT = nc.dram_tensor("outT", [128, 8 * TD], F32, kind="ExternalOutput").ap()
    x2_hbm = nc.dram_tensor("x2_scratch", [128, 8 * T], F32)
    x2own_hbm = nc.dram_tensor("x2own_scratch", [128, 8 * TD], F32)
    kv_hbm = nc.dram_tensor("kv_scratch", [128, nheads * T], BF16)
    vt_hbm = nc.dram_tensor("vt_scratch", [128, nheads * ntb * 65], BF16)
    ao_hbm = nc.dram_tensor("ao_scratch", [128, 8 * TD], BF16)

    stack = contextlib.ExitStack()
    block = stack.enter_context(nc.Block())
    p = Prog(nc, block, stack)
    p.alloc_psum(n_big=4)

    emit_L0(nc, stack, block, p, din, x2_hbm.ap(), T, leaf_iters, polish)

    srm = p.sb("srmask", [128, 4], F32)
    p.dma_s(srm[:, :], din["rmask"], wait=[(p.el, p.nel)])

    p3 = contextlib.ExitStack()
    p.stack = p3

    def p3sb(name, shape, dtype=F32):
        return p3.enter_context(nc.sbuf_tensor(name, shape, dtype))

    onescb2 = p3sb("onescb2", [128, 1], BF16)
    qnw = p3sb("sqnw", [64, 1], F32)
    knw = p3sb("sknw", [64, 1], F32)
    identf = p3sb("sidentf", [128, 128], F32)
    masks = p3sb("smasks16", [128, 16 * 512], BF16)
    wsm = {}
    for nm, shp in [("dqc", [128, 4 * 64]), ("uqt", [128, 4 * 64]),
                    ("dkvc", [128, 8 * 64]), ("ukvt", [128, 8 * 128]),
                    ("qrt", [64, 64]), ("krt", [64, 64]), ("wot", [64, 64])]:
        wsm[nm] = p3sb("s_" + nm, shp, BF16)
    p.dma_s(qnw[:, :], din["qnw"], wait=[(p.el, p.nel)])
    p.dma_s(knw[:, :], din["knw"])
    p.dma_s(identf[:, :], din["ident"])
    p.dma_g(masks[:, :], din["masks16"], wait=[(p.el, p.nel)])
    for nm in ["dqc", "uqt", "dkvc", "ukvt", "qrt", "krt", "wot"]:
        p.dma_g(wsm[nm][:, :], din[nm])
    nq3s, nq3g = p.nqs, p.nqg

    xs3 = p3sb("t_xs3", [128, 8 * WS], F32)
    xns = p3sb("t_xns", [128, 8 * WS], BF16)
    xh0 = p3sb("t_xh0", [64, WS], BF16)
    kvs = p3sb("t_kvs", [128, WS], F32)
    kpiece = p3sb("t_kp", [128, WS], BF16)
    vtp = p3sb("t_vtp", [128, 4 * 65], BF16)
    rbc = p3sb("t_rbc", [128, WS], F32)
    att = p3sb("t_att", [128, WS], F32)
    attb = p3sb("t_attb", [128, WS], BF16)
    qct = p3sb("t_qct", [64, 64], BF16)
    kvct = p3sb("t_kvct", [64, 128], BF16)
    sonesc3 = p3sb("sonesc3", [128, 1], F32)
    sonesr3 = p3sb("sonesr3", [1, 128], F32)
    lnsc3 = make_ln_scratch(p, WS, tag="3")
    lnsc3["ones_col"] = sonesc3[:, :]
    lnsc3["ones_col_b"] = onescb2[:, :]
    lnsc3["ones_row"] = sonesr3[:, :]
    p.dma_s(lnsc3["ones_col"], din["onesc"])
    p.dma_s(lnsc3["ones_row"], din["onesr"])
    nq3s = p.nqs

    p.el_barrier([(p.qs, nq3s), (p.qg, nq3g)])
    p.vec(lambda v: v.memset(onescb2[:, :], 1.0))

    def qc_fn(t):
        last = None
        for k in range(4):
            last = t.matmul(p.misc1[0:64, 0:64], wsm["dqc"][:, k * 64:(k + 1) * 64],
                            wsm["uqt"][:, k * 64:(k + 1) * 64],
                            start=(k == 0), stop=(k == 3))
        return last
    p.tensor(qc_fn)
    p.vec(lambda v: v.tensor_copy(qct[:, :], p.misc1[0:64, 0:64]))

    def kvc_fn(t):
        last = None
        for k in range(8):
            last = t.matmul(p.misc2[0:64, 0:128], wsm["dkvc"][:, k * 64:(k + 1) * 64],
                            wsm["ukvt"][:, k * 128:(k + 1) * 128],
                            start=(k == 0), stop=(k == 7))
        return last
    p.tensor(kvc_fn)
    p.vec(lambda v: v.tensor_copy(kvct[:, :], p.misc2[0:64, 0:128]))

    for s in range(nslab):
        col0 = s * WS
        for c in range(8):
            p.dma_s(xs3[:, c * WS:(c + 1) * WS],
                    x2_hbm.ap()[:, c * T + col0: c * T + col0 + WS],
                    wait=([(p.el, p.nel)] if c == 0 else None))
        tgt = p.nqs
        p.el_barrier([(p.qs, tgt)])
        emit_ln(p, lambda c: xs3[:, c * WS:(c + 1) * WS],
                  lambda c: xns[:, c * WS:(c + 1) * WS], WS, lnsc3)

        for h in range(nheads):
            hc, hr = h // 2, (h % 2) * 64

            def cph(v, hc=hc, hr=hr):
                v.tensor_copy(xh0[:, :], xns[hr:hr + 64, hc * WS:(hc + 1) * WS])
            p.vec(cph, wait_dma=[(p.qg, p.nqg)])

            ps = p.psum()

            def km(t, ps=ps):
                return t.matmul(ps[0:64, 0:WS], kvct[:, 0:64], xh0[:, :],
                                start=True, stop=True)
            p.tensor(km)
            p.vec(lambda v, ps=ps: v.tensor_copy(kvs[0:64, :], ps[0:64, 0:WS]))

            def sqk(a):
                a.activation(att[0:64, :], kvs[0:64, :], AF.Square)
            p.act(sqk)

            def msk(t):
                return t.matmul(p.misc1[0:1, 0:WS], lnsc3["ones_col"][0:64, 0:1],
                                att[0:64, :], start=True, stop=True)
            p.tensor(msk)

            def kr1(v):
                v.tensor_scalar(rbc[0:1, :], p.misc1[0:1, 0:WS], 1.0 / 64,
                                RMS_EPS, ALU.mult, ALU.add)
            p.vec(kr1)

            def kr2(a):
                a.activation(rbc[0:1, :], rbc[0:1, :], AF.Sqrt)
            p.act(kr2)
            p.vec(lambda v: v.reciprocal(rbc[0:1, :], rbc[0:1, :]))

            def kr3(t):
                return t.matmul(p.misc1[:, 0:WS], lnsc3["ones_row"], rbc[0:1, :],
                                start=True, stop=True)
            p.tensor(kr3)

            def kn(v):
                v.tensor_copy(rbc[:, :], p.misc1[:, 0:WS])
                v.tensor_mul(att[0:64, :], kvs[0:64, :], rbc[0:64, :])
                v.tensor_scalar(kpiece[0:64, :], att[0:64, :], knw[:, 0:1], 1.0,
                                ALU.mult, ALU.mult)
            p.vec(kn)

            ps2 = p.psum()

            def krm(t, ps=ps2):
                return t.matmul(ps[0:64, 0:WS], wsm["krt"][:, :], xh0[:, :],
                                start=True, stop=True)
            p.tensor(krm)
            p.vec(lambda v, ps=ps2: v.tensor_copy(kpiece[64:128, :], ps[0:64, 0:WS]))
            p.dma_g(kv_hbm.ap()[:, h * T + col0: h * T + col0 + WS], kpiece[:, :],
                    wait=[(p.el, p.nel)])

            def vo(v):
                v.memset(vtp[:, :].rearrange("p (b o) -> p b o", o=65)[:, :, 64:65], 1.0)
            p.vec(vo)
            for i in range(4):
                ps3 = p.psum()

                def vm(t, i=i, ps=ps3):
                    return t.matmul(ps[0:128, 0:64], xh0[:, i * 128:(i + 1) * 128],
                                    kvct[:, 64:128], start=True, stop=True)
                p.tensor(vm)
                p.vec(lambda v, i=i, ps=ps3:
                      v.tensor_copy(vtp[:, i * 65: i * 65 + 64], ps[0:128, 0:64]))
            p.dma_g(vt_hbm.ap()[:, h * ntb * 65 + 4 * s * 65: h * ntb * 65 + 4 * s * 65 + 260],
                    vtp[:, :], wait=[(p.el, p.nel)])

    oxn = p3sb("t_oxn", [128, 8 * TD], BF16)
    Kh = p3sb("t_Kh", [128, T], BF16)
    vTh = p3sb("t_vTh", [128, ntb * 65], BF16)
    Qown = p3sb("t_Qo", [128, TD], BF16)
    qcs = p3sb("t_qcs", [64, WS], F32)
    rm = p3sb("t_rm", [128, TD // 128 + 1], F32)
    rmrow = p3sb("t_rmrow", [1, 128], F32)
    nmbc = p3sb("t_nmbc", [128, WS], F32)
    ctxb = p3sb("t_ctxb", [64, WS], BF16)
    ao = p3sb("t_ao", [128, 8 * TD], BF16)
    n_own_slab = TD // WS
    n_qt = TD // 128
    n_g = TD // 512

    xg = p3sb("t_xg", [128, 2 * 2048], F32)
    sel_done = {}
    nqs_writes = 0
    for s4 in range(n_own_slab):
        for c in range(8):
            k = s4 * 8 + c
            buf = xg[:, (k % 2) * 2048:(k % 2) * 2048 + 2048]
            src = x2_hbm.ap()[:, c * T + 2048 * s4: c * T + 2048 * s4 + 2048] \
                .rearrange("p (g r i) -> p r g i", g=4, r=4)
            for j in range(4):
                wl0 = None
                if j == 0:
                    wl0 = [(p.qg, p.nqg)]
                    wl0.append((p.el, sel_done[k - 2]) if k >= 2
                               else (p.el, p.nel))
                p.dma_g(buf[:, j * WS:(j + 1) * WS], src[:, j], wait=wl0)
            wt = p.nqg
            wl = [(p.qg, wt)]
            if c == 0 and s4 > 0:
                wl.append((p.qs, nqs_writes))

            def sel(v, c=c, buf=buf):
                d = xs3[:, c * WS:(c + 1) * WS]
                v.tensor_scalar(d, buf[:, 0:WS], srm[:, 0:1], None, ALU.mult)
                for j in range(1, 4):
                    v.scalar_tensor_tensor(d, buf[:, j * WS:(j + 1) * WS],
                                           srm[:, j:j + 1], d, ALU.mult, ALU.add)
            p.vec(sel, wait_dma=wl)
            sel_done[k] = p.nel
        for c in range(8):
            p.dma_s(x2own_hbm.ap()[:, c * TD + s4 * WS: c * TD + s4 * WS + WS],
                    xs3[:, c * WS:(c + 1) * WS],
                    wait=([(p.el, p.nel)] if c == 0 else None))
        nqs_writes = p.nqs
        emit_ln(p, lambda c: xs3[:, c * WS:(c + 1) * WS],
                  lambda c: oxn[:, c * TD + s4 * WS: c * TD + s4 * WS + WS],
                  WS, lnsc3)

    for h in range(nheads):
        hc, hr = h // 2, (h % 2) * 64
        p.dma_s(Kh[:, :], kv_hbm.ap()[:, h * T:(h + 1) * T],
                wait=[(p.el, p.nel)])
        p.dma_s(vTh[:, :], vt_hbm.ap()[:, h * ntb * 65:(h + 1) * ntb * 65])
        nkv = p.nqs
        p.el_barrier([(p.qs, nkv)])

        for nt in range(n_own_slab):
            sl = slice(nt * WS, (nt + 1) * WS)

            def cpo(v, sl=sl, hc=hc, hr=hr):
                v.tensor_copy(xh0[:, :], oxn[hr:hr + 64, hc * TD + nt * WS: hc * TD + nt * WS + WS])
            p.vec(cpo)
            ps = p.psum()

            def qm(t, ps=ps):
                return t.matmul(ps[0:64, 0:WS], qct[:, :], xh0[:, :],
                                start=True, stop=True)
            p.tensor(qm)
            p.vec(lambda v, ps=ps: v.tensor_copy(qcs[:, :], ps[0:64, 0:WS]))

            def sqq(a):
                a.activation(att[0:64, :], qcs[:, :], AF.Square)
            p.act(sqq)

            def msq(t):
                return t.matmul(p.misc1[0:1, 0:WS], lnsc3["ones_col"][0:64, 0:1],
                                att[0:64, :], start=True, stop=True)
            p.tensor(msq)

            def qr1(v):
                v.tensor_scalar(rbc[0:1, :], p.misc1[0:1, 0:WS], 1.0 / 64,
                                RMS_EPS, ALU.mult, ALU.add)
            p.vec(qr1)

            def qr2(a):
                a.activation(rbc[0:1, :], rbc[0:1, :], AF.Sqrt)
            p.act(qr2)
            p.vec(lambda v: v.reciprocal(rbc[0:1, :], rbc[0:1, :]))

            def qr3(t):
                return t.matmul(p.misc1[:, 0:WS], lnsc3["ones_row"], rbc[0:1, :],
                                start=True, stop=True)
            p.tensor(qr3)

            def qn(v, sl=sl):
                v.tensor_copy(rbc[:, :], p.misc1[:, 0:WS])
                v.tensor_mul(qcs[:, :], qcs[:, :], rbc[0:64, :])
                v.tensor_scalar(Qown[0:64, sl], qcs[:, :], qnw[:, 0:1], SCALE,
                                ALU.mult, ALU.mult)
            p.vec(qn)
            ps2 = p.psum()

            def qrm(t, ps=ps2, sl=sl):
                return t.matmul(ps[0:64, 0:WS], wsm["qrt"][:, :], Qown[0:64, sl],
                                start=True, stop=True)
            p.tensor(qrm)
            p.vec(lambda v, ps=ps2, sl=sl:
                  v.tensor_copy(Qown[64:128, sl], ps[0:64, 0:WS]))

        for i in range(n_qt):
            for kb in range(i + 1):
                ps = p.psum()

                def sA(t, ps=ps, i=i, kb=kb):
                    return t.matmul(ps[:, 0:WS], Qown[:, i * 128:(i + 1) * 128],
                                    Kh[:, kb * 512:(kb + 1) * 512],
                                    start=True, stop=True)
                p.tensor(sA)

                def mx(v, ps=ps, i=i, kb=kb):
                    if kb == 0:
                        v.reduce_max(rm[:, i:i + 1], ps[:, 0:WS], axis=AX,
                                     negate=True)
                    else:
                        v.reduce_max(rm[:, n_qt:n_qt + 1], ps[:, 0:WS], axis=AX,
                                     negate=True)
                        v.tensor_tensor(rm[:, i:i + 1], rm[:, i:i + 1],
                                        rm[:, n_qt:n_qt + 1], ALU.min)
                p.vec(mx)

        for g in range(n_g):
            for j in range(4):
                def trj(t, g=g, j=j):
                    return t.transpose(p.pst[1][0:1, 0:128],
                                       rm[:, 4 * g + j:4 * g + j + 1], identf[:, :])
                p.tensor(trj)
                p.vec(lambda v: v.tensor_copy(rmrow[:, :], p.pst[1][0:1, 0:128]))

                def bcj(t, j=j):
                    return t.matmul(p.misc1[:, j * 128:(j + 1) * 128],
                                    lnsc3["ones_row"], rmrow[:, :],
                                    start=True, stop=True)
                p.tensor(bcj)
            p.vec(lambda v: v.tensor_copy(nmbc[:, :], p.misc1[:, 0:512]))

            for ck in range(16 * g + 16):
                d = ck - 16 * g
                ps = p.psum()

                def sB(t, ps=ps, g=g, ck=ck):
                    return t.matmul(ps[:, 0:WS], Kh[:, ck * 128:(ck + 1) * 128],
                                    Qown[:, g * 512:(g + 1) * 512],
                                    start=True, stop=True)
                p.tensor(sB)

                def sub(v, ps=ps, d=d):
                    v.tensor_add(att[:, :], ps[:, 0:WS], nmbc[:, :])
                    if d >= 0:
                        v.tensor_add(att[:, :], att[:, :],
                                     masks[:, d * 512:(d + 1) * 512])
                p.vec(sub)

                def ex(a):
                    a.activation(attb[:, :], att[:, :], AF.Exp)
                p.act(ex)

                def av(t, g=g, ck=ck):
                    return t.matmul(p.pst[0][0:65, 0:WS],
                                    vTh[:, ck * 65:(ck + 1) * 65], attb[:, :],
                                    start=(ck == 0), stop=(ck == 16 * g + 15))
                p.tensor(av)

            def nrm(v):
                v.reciprocal(rbc[0:1, :], p.pst[0][64:65, 0:WS])
            p.vec(nrm)

            def nbc(t):
                return t.matmul(p.misc2[0:64, 0:WS], lnsc3["ones_row"][:, 0:64],
                                rbc[0:1, :], start=True, stop=True)
            p.tensor(nbc)

            def cmul(v):
                v.tensor_copy(rbc[0:64, :], p.misc2[0:64, 0:WS])
                v.tensor_mul(att[0:64, :], p.pst[0][0:64, 0:WS], rbc[0:64, :])
                v.tensor_copy(ctxb[:, :], att[0:64, :])
            p.vec(cmul)
            ps5 = p.psum()

            def wof(t, ps=ps5):
                return t.matmul(ps[0:64, 0:WS], wsm["wot"][:, :], ctxb[:, :],
                                start=True, stop=True)
            p.tensor(wof)
            p.vec(lambda v, ps=ps5, g=g, hc=hc, hr=hr:
                  v.tensor_copy(ao[hr:hr + 64, hc * TD + g * 512: hc * TD + g * 512 + WS],
                                ps[0:64, 0:WS]))

    p.el_barrier([])
    for c in range(8):
        p.dma_s(ao_hbm.ap()[:, c * TD:(c + 1) * TD], ao[:, c * TD:(c + 1) * TD],
                wait=[(p.el, p.nel)])
    p.el_barrier([(p.qs, p.nqs)])
    p.stack = stack
    p3.close()

    emit_moe(nc, stack, block, p, din, outT, x2own_hbm, ao_hbm, TD, T)

    @block.sync
    def _(s):
        s.wait_ge(p.qs, p.nqs)

    @block.gpsimd
    def _(g):
        g.wait_ge(p.qg, p.nqg)

    nc._stack = stack
    return nc


def emit_moe(nc, stack, block, p, din, outT, x2own_hbm, ao_hbm, TD, T):
    W = 512
    nslab = TD // W
    p4 = contextlib.ExitStack()
    outer = p.stack
    p.stack = p4

    def sb4(name, shape, dtype=F32):
        return p4.enter_context(nc.sbuf_tensor(name, shape, dtype))

    onesc = sb4("onesc4", [128, 1], F32)
    onesr = sb4("onesr4", [1, 128], F32)
    onescb = sb4("onescb4", [128, 1], BF16)
    wrc = sb4("s_wrc", [128, 8 * 4], BF16)
    b1c = sb4("s_b1cD", [128, 4 * 32], F32)
    woc = sb4("s_woc", [128, 32 * 1024], BF16)
    boc = sb4("s_boc", [128, 8], F32)
    gam = sb4("s_gam", [128, 8], F32)
    bet = sb4("s_bet", [128, 8], F32)
    p.dma_s(onesc[:, :], din["onesc"], wait=[(p.el, p.nel)])
    p.dma_s(onesr[:, :], din["onesr"])
    p.dma_s(b1c[:, :], din["b1cD"])
    p.dma_s(boc[:, :], din["boc"])
    p.dma_s(gam[:, :], din["gam"])
    p.dma_s(bet[:, :], din["bet"])
    p.dma_g(wrc[:, :], din["wrc"], wait=[(p.el, p.nel)])
    p.dma_g(woc[:, :], din["woc"])
    nq_s, nq_g = p.nqs, p.nqg

    x2s = sb4("t_x2_4", [128, 8 * W], F32)
    ats = sb4("t_at_4", [128, 8 * W], BF16)
    x3s = sb4("t_x3_4", [128, 8 * W], F32)
    xn3 = sb4("t_xn3_4", [128, 8 * W], BF16)
    inter = sb4("t_inter4", [128, 32 * W], BF16)
    hs = sb4("t_hs4", [128, W], BF16)
    ys = x2s
    lgr = sb4("t_lgr4", [1, 4 * W], F32)
    mxr = sb4("t_mxr4", [1, W], F32)
    mrow = sb4("t_mrow4", [1, 4 * W], F32)
    accm = sb4("t_accm4", [1, W], F32)
    mbc = sb4("t_mbc4", [128, W], BF16)
    w1b = [sb4(f"t_w1b4{i}", [128, 1024], BF16) for i in range(2)]
    lnsc = make_ln_scratch(p, W, tag="4")
    lnsc["ones_col"] = onesc[:, :]
    lnsc["ones_col_b"] = onescb[:, :]
    lnsc["ones_row"] = onesr[:, :]

    def ch(tile, c, w=W):
        return tile[:, c * w:(c + 1) * w]

    p.el_barrier([(p.qs, nq_s), (p.qg, nq_g)])
    p.vec(lambda v: v.memset(onescb[:, :], 1.0))

    for s in range(nslab):
        col0 = s * W
        for c in range(8):
            p.dma_s(ch(x2s, c),
                    x2own_hbm.ap()[:, c * TD + col0: c * TD + col0 + W],
                    wait=([(p.el, p.nel)] if c == 0 else None))
        for c in range(8):
            p.dma_s(ch(ats, c), ao_hbm.ap()[:, c * TD + col0: c * TD + col0 + W],
                    wait=([(p.el, p.nel)] if c == 0 else None))
        p.el_barrier([(p.qs, p.nqs), (p.qg, p.nqg)])

        emit_ln(p, lambda c: ch(ats, c), lambda c: ch(x3s, c), W, lnsc)

        def r1(v):
            for c in range(8):
                v.tensor_add(ch(x3s, c), ch(x3s, c), ch(x2s, c))
        p.vec(r1)

        emit_ln(p, lambda c: ch(x3s, c), lambda c: ch(xn3, c), W, lnsc)

        for e in range(4):
            def lg(t, e=e):
                last = None
                for k in range(8):
                    last = t.matmul(p.misc1[0:1, 0:W],
                                    wrc[:, k * 4 + e: k * 4 + e + 1],
                                    ch(xn3, k), start=(k == 0), stop=(k == 7))
                return last
            p.tensor(lg)
            p.vec(lambda v, e=e: v.tensor_copy(lgr[:, e * W:(e + 1) * W],
                                               p.misc1[0:1, 0:W]))

        def am(v):
            v.tensor_tensor(mxr[:, :], lgr[:, 0:W], lgr[:, W:2 * W], ALU.max)
            v.tensor_tensor(mxr[:, :], mxr[:, :], lgr[:, 2 * W:3 * W], ALU.max)
            v.tensor_tensor(mxr[:, :], mxr[:, :], lgr[:, 3 * W:4 * W], ALU.max)
            v.memset(accm[:, :], 0.0)
            for e in range(4):
                sl = slice(e * W, (e + 1) * W)
                v.tensor_tensor(mrow[:, sl], lgr[:, sl], mxr[:, :], ALU.is_ge)
                v.scalar_tensor_tensor(mrow[:, sl], accm[:, :], -1.0,
                                       mrow[:, sl], ALU.mult, ALU.add)
                v.tensor_scalar_max(mrow[:, sl], mrow[:, sl], 0.0)
                v.tensor_add(accm[:, :], accm[:, :], mrow[:, sl])
        p.vec(am)

        for e in range(4):
            def mb(t, e=e):
                return t.matmul(p.misc2[:, 0:W], onesr[:, :],
                                mrow[:, e * W:(e + 1) * W], start=True, stop=True)
            p.tensor(mb)
            p.vec(lambda v: v.tensor_copy(mbc[:, :], p.misc2[:, 0:W]))

            for i in range(32):
                buf = w1b[i % 2]
                p.dma_g(buf[:, :],
                        din["w1e"][:, (e * 32 + i) * 1024:(e * 32 + i + 1) * 1024],
                        wait=[(p.el, p.nel)])
                wt = p.nqg
                ps = p.psum()

                def mm1(t, buf=buf, ps=ps, wt=wt):
                    t.wait_ge(p.qg, wt)
                    last = None
                    for k in range(8):
                        last = t.matmul(ps[:, 0:W], buf[:, k * 128:(k + 1) * 128],
                                        ch(xn3, k), start=(k == 0), stop=(k == 7))
                    return last
                p.tensor(mm1)

                def ge(a, e=e, i=i, ps=ps):
                    a.activation(hs[:, :], ps[:, 0:W], AF.Gelu,
                                 bias=b1c[:, e * 32 + i: e * 32 + i + 1])
                p.act(ge)

                def acc(v, e=e, i=i):
                    if e == 0:
                        v.tensor_mul(ch(inter, i), hs[:, :], mbc[:, :])
                    else:
                        v.tensor_mul(hs[:, :], hs[:, :], mbc[:, :])
                        v.tensor_add(ch(inter, i), ch(inter, i), hs[:, :])
                p.vec(acc)

        for m in range(8):
            ps = p.psum()

            def mo(t, m=m, ps=ps):
                last = None
                for k in range(32):
                    last = t.matmul(ps[:, 0:W],
                                    woc[:, k * 1024 + m * 128: k * 1024 + m * 128 + 128],
                                    ch(inter, k), start=(k == 0), stop=(k == 31))
                return last
            p.tensor(mo)

            def ye(v, m=m, ps=ps):
                v.tensor_scalar_add(ch(ys, m), ps[:, 0:W], boc[:, m:m + 1])
                v.tensor_add(ch(ys, m), ch(ys, m), ch(xn3, m))
            p.vec(ye)

        y2 = inter[:, 0:16 * W].bitcast(F32)
        emit_ln(p, lambda c: ch(ys, c), lambda c: ch(y2, c), W, lnsc,
                  eps=LN_EPS_IN)

        def gb(v):
            for c in range(8):
                v.tensor_scalar(ch(y2, c), ch(y2, c), gam[:, c:c + 1],
                                bet[:, c:c + 1], ALU.mult, ALU.add)
        p.vec(gb)

        emit_ln(p, lambda c: ch(y2, c), lambda c: ch(y2, c), W, lnsc)

        def r2(v):
            for c in range(8):
                v.tensor_add(ch(y2, c), ch(y2, c), ch(x3s, c))
        p.vec(r2)

        for c in range(8):
            p.dma_s(outT[:, c * TD + col0: c * TD + col0 + W], ch(y2, c),
                    wait=([(p.el, p.nel)] if c == 0 else None))
    p.stack = outer
    p4.close()



_BUILT = {}


def _get(name, fn):
    if name not in _BUILT:
        _BUILT[name] = fn()
    return _BUILT[name]


def kernel(**inputs):
    d = {k: np.asarray(v) for k, v in inputs.items()}
    x = np.asarray(d["L0_attn_x"], np.float32)
    B, T, DM = x.shape
    ncF = _get("F", lambda: build_F(T=T))
    in_maps = [f_inputs(x[c // 4], d, T, c % 4) for c in range(8)]
    res = run_bass_kernel_spmd(ncF, in_maps, core_ids=list(range(8)))
    outs = [res.results[c]["outT"] for c in range(8)]
    return f_reassemble(outs, T)


# version 04
# speedup vs baseline: 1.2001x; 1.2001x over previous
import sys
import contextlib
import numpy as np
import concourse.bass as bass
import concourse.mybir as mybir
from concourse.bass_utils import run_bass_kernel_spmd



F32 = mybir.dt.float32
BF16 = mybir.dt.bfloat16
I32 = mybir.dt.int32
AF = mybir.ActivationFunctionType
ALU = mybir.AluOpType
AX = mybir.AxisListType.X


class Prog:
    def __init__(self, nc, block, stack):
        self.nc, self.block, self.stack = nc, block, stack
        self.pe = stack.enter_context(nc.semaphore("pe"))
        self.el = stack.enter_context(nc.semaphore("el"))
        self.qs = stack.enter_context(nc.semaphore("qs"))
        self.qg = stack.enter_context(nc.semaphore("qg"))
        self.npe = 0
        self.nel = 0
        self.nqs = 0
        self.nqg = 0
        self.ps = []
        self.ps_small = []

    def alloc_psum(self, n_big=4):
        self.ps = [self.stack.enter_context(
            self.nc.psum_tensor(f"ps{i}", [128, 512], F32)) for i in range(n_big)]
        self.pst = [self.stack.enter_context(
            self.nc.psum_tensor(f"pst{i}", [128, 512], F32)) for i in range(2)]
        self.misc1 = self.stack.enter_context(
            self.nc.psum_tensor("psm1", [128, 512], F32))
        self.misc2 = self.stack.enter_context(
            self.nc.psum_tensor("psm2", [128, 512], F32))
        self._pi = 0
        self.bar = self.stack.enter_context(
            self.nc.sbuf_tensor("elbar", [1, 1], F32))

    def psum(self):
        t = self.ps[self._pi % len(self.ps)]
        self._pi += 1
        return t

    def el_barrier(self, waits):
        @self.block.vector
        def _(v):
            for sem, cnt in waits:
                v.wait_ge(sem, cnt)
            r = v.memset(self.bar[:, :], 0.0)
            r._wait_ge(self.el, self.nel)
            r.then_inc(self.el, 1)
        self.nel += 1

    def sb(self, name, shape, dtype=F32):
        return self.stack.enter_context(self.nc.sbuf_tensor(name, shape, dtype))

    class _E:
        def __init__(self, p, eng, pending):
            object.__setattr__(self, "_p", p)
            object.__setattr__(self, "_e", eng)
            object.__setattr__(self, "_pend", pending)

        def __getattr__(self, name):
            f = getattr(self._e, name)
            if not callable(f) or name.startswith("wait"):
                return f
            p = self._p

            def wrapped(*a, **k):
                for sem, cnt in self._pend:
                    self._e.wait_ge(sem, cnt)
                del self._pend[:]
                r = f(*a, **k)
                r._wait_ge(p.el, p.nel)
                r.then_inc(p.el, 1)
                p.nel += 1
                return r
            return wrapped

    def _esec(self, blockdec, fn, wait_pe, wait_dma):
        pending = []
        if wait_pe:
            pending.append((self.pe, self.npe))
        if wait_dma:
            pending.extend(wait_dma)

        @blockdec
        def _(e):
            fn(Prog._E(self, e, pending))

    def vec(self, fn, wait_pe=True, wait_dma=None):
        self._esec(self.block.vector, fn, wait_pe, wait_dma)

    def act(self, fn, wait_pe=True, wait_dma=None):
        self._esec(self.block.scalar, fn, wait_pe, wait_dma)

    def pool(self, fn, wait_pe=True, wait_dma=None):
        self._esec(self.block.gpsimd, fn, wait_pe, wait_dma)

    def tensor(self, fn, wait_el=True, wait_dma=None, extra_el=None):
        prog = self

        class _T:
            def __init__(self):
                self.first = True
                self.front = []

            def _pre(self):
                if self.first:
                    self.first = False
                    if wait_dma:
                        for sem, cnt in wait_dma:
                            self._t.wait_ge(sem, cnt)
                    if wait_el:
                        self.front.append((prog.el, prog.nel))

            def _att(self, r):
                for sem, cnt in self.front[1:]:
                    pass
                if self.front:
                    r._wait_ge(*self.front[-1])
                    del self.front[:]
                return r

            def matmul(self, *a, **k):
                self._pre()
                for sem, cnt in self.front[:-1]:
                    self._t.wait_ge(sem, cnt)
                    self.front.remove((sem, cnt))
                return self._att(self._t.matmul(*a, **k))

            def transpose(self, *a, **k):
                self._pre()
                for sem, cnt in self.front[:-1]:
                    self._t.wait_ge(sem, cnt)
                    self.front.remove((sem, cnt))
                return self._att(self._t.transpose(*a, **k))

            def wait_ge(self, sem, cnt):
                self.front.append((sem, cnt))
                return None

        @self.block.tensor
        def _(t):
            w = _T()
            w._t = t
            last = fn(w)
            last.then_inc(self.pe, 1)
        self.npe += 1

    def dma_s(self, out_ap, in_ap, wait=None):
        @self.block.sync
        def _(s):
            if wait:
                for sem, cnt in wait[:-1]:
                    s.wait_ge(sem, cnt)
            r = s.dma_start(out=out_ap, in_=in_ap)
            if wait:
                r._wait_ge(*wait[-1])
            r.then_inc(self.qs, 16)
        self.nqs += 16

    def dma_g(self, out_ap, in_ap, wait=None):
        @self.block.gpsimd
        def _(g):
            if wait:
                for sem, cnt in wait[:-1]:
                    g.wait_ge(sem, cnt)
            r = g.dma_start(out=out_ap, in_=in_ap)
            if wait:
                r._wait_ge(*wait[-1])
            r.then_inc(self.qg, 16)
        self.nqg += 16


def emit_ln(p: Prog, x_of, out_of, W, sc, eps=1e-5, nch=8, nfeat=None,
            out_chunks=None, extra_mul=None, x_full=None, out_full=None):
    nfeat = nfeat or nch * 128
    out_chunks = out_chunks if out_chunks is not None else range(nch)

    x_dt = x_of(0).dtype
    sc = dict(sc)
    sc["psA"] = sc["psm1"][0:1, 0:W]
    sc["psB"] = sc["psm2"][0:1, 0:W]
    sq = sc["sq"]
    assert sq.shape[1] == 8 * W, (sq.shape, W)
    sqb = sq[:, 0:4 * W].bitcast(mybir.dt.bfloat16)
    xbb = sq[:, 4 * W:8 * W].bitcast(mybir.dt.bfloat16)

    def sq_fn(a):
        if x_full is not None:
            a.activation(sqb, x_full, AF.Square)
        else:
            for c in range(nch):
                a.activation(sqb[:, c * W:(c + 1) * W], x_of(c), AF.Square)
    p.act(sq_fn)

    if x_dt == mybir.dt.bfloat16:
        xb_of = x_of
    else:
        def cast_fn(v):
            if x_full is not None:
                v.tensor_copy(xbb, x_full)
            else:
                for c in range(nch):
                    v.tensor_copy(xbb[:, c * W:(c + 1) * W], x_of(c))
        p.vec(cast_fn)
        xb_of = lambda c: xbb[:, c * W:(c + 1) * W]

    onesb = sc["ones_col_b"]

    def mm_fn(t):
        for c in range(nch):
            t.matmul(sc["psA"], onesb, sqb[:, c * W:(c + 1) * W],
                     start=(c == 0), stop=(c == nch - 1))
        last = None
        for c in range(nch):
            last = t.matmul(sc["psB"], onesb, xb_of(c),
                            start=(c == 0), stop=(c == nch - 1))
        return last
    p.tensor(mm_fn)

    inv_n = 1.0 / nfeat

    def stats(v):
        v.tensor_scalar_mul(sc["r1"][:, 0:W], sc["psB"], inv_n)
        v.tensor_mul(sc["r2"][:, 0:W], sc["r1"][:, 0:W], sc["r1"][:, 0:W])
        v.tensor_scalar(sc["r3"][:, 0:W], sc["psA"], inv_n, eps,
                        ALU.mult, ALU.add)
        v.tensor_sub(sc["r3"][:, 0:W], sc["r3"][:, 0:W], sc["r2"][:, 0:W])
    p.vec(stats)

    def rstd(a):
        a.activation(sc["r2"][:, 0:W], sc["r3"][:, 0:W], AF.Ln)
        a.activation(sc["r3"][:, 0:W], sc["r2"][:, 0:W], AF.Exp, scale=-0.5)
    p.act(rstd)

    p.vec(lambda v: v.tensor_mul(sc["r1"][:, 0:W], sc["r1"][:, 0:W],
                                 sc["r3"][:, 0:W]))

    psC1 = p.psum()
    psC2 = p.psum()

    def bc(t):
        t.matmul(psC1[:, 0:W], sc["ones_row"], sc["r3"][:, 0:W],
                 start=True, stop=True)
        return t.matmul(psC2[:, 0:W], sc["ones_row"], sc["r1"][:, 0:W],
                        start=True, stop=True)
    p.tensor(bc)

    def evacbc(v):
        v.tensor_copy(sc["b1"][:, 0:W], psC1[:, 0:W])
        v.tensor_copy(sc["b2"][:, 0:W], psC2[:, 0:W])
        if x_full is not None and out_full is not None:
            x3 = x_full.rearrange("p (c w) -> p c w", c=nch)
            o3 = out_full.rearrange("p (c w) -> p c w", c=nch)
            t3 = sq[:, :].rearrange("p (c w) -> p c w", c=nch)
            b1b = sc["b1"][:, 0:W].unsqueeze(1).broadcast_to([128, nch, W])
            b2b = sc["b2"][:, 0:W].unsqueeze(1).broadcast_to([128, nch, W])
            v.tensor_tensor(t3, x3, b1b, ALU.mult)
            v.tensor_tensor(o3, t3, b2b, ALU.subtract)
        else:
            for c in out_chunks:
                v.tensor_mul(sq[:, c * W:(c + 1) * W], x_of(c), sc["b1"][:, 0:W])
                v.tensor_sub(out_of(c), sq[:, c * W:(c + 1) * W], sc["b2"][:, 0:W])
    p.vec(evacbc)


def make_ln_scratch(p: Prog, W, tag=""):
    sc = {}
    sc["sq"] = p.sb("lnsq" + tag, [128, 8 * W], F32)[:, :]
    sc["r1"] = p.sb("lnr1" + tag, [1, W], F32)[:, :]
    sc["r2"] = p.sb("lnr2" + tag, [1, W], F32)[:, :]
    sc["r3"] = p.sb("lnr3" + tag, [1, W], F32)[:, :]
    sc["b1"] = p.sb("lnb1" + tag, [128, W], F32)[:, :]
    sc["b2"] = p.sb("lnb2" + tag, [128, W], F32)[:, :]
    sc["psm1"] = p.misc1
    sc["psm2"] = p.misc2
    return sc


def emit_mm(p: Prog, out_of, lhsT_of, rhs_of, kch, mch, W, evac,
            nsplit=1):
    for m in range(mch):
        ps = p.psum()

        def mm_fn(t, m=m, ps=ps):
            last = None
            for k in range(kch):
                last = t.matmul(ps[:, 0:W], lhsT_of(k, m), rhs_of(k),
                                start=(k == 0), stop=(k == kch - 1))
            return last
        p.tensor(mm_fn)
        evac(m, ps)


F32 = mybir.dt.float32
BF16 = mybir.dt.bfloat16
ALU = mybir.AluOpType


class CM:

    def __init__(self, re, im, nim=None, n=0):
        self.re, self.im, self.nim, self.n = re, im, nim, n


class Inv:
    def __init__(self, nc, block, stack, ident128_f32, psum_from=None):
        self.nc, self.block, self.stack = nc, block, stack
        self.ident = ident128_f32
        self.pe = stack.enter_context(nc.semaphore("ipe"))
        self.dv = stack.enter_context(nc.semaphore("idv"))
        self.npe = 0
        self.ndv = 0
        if psum_from is None:
            self.ps = [stack.enter_context(nc.psum_tensor(f"ips{i}", [128, 512], F32))
                       for i in range(4)]
            self.pst = [stack.enter_context(nc.psum_tensor(f"ipst{i}", [128, 512], F32))
                        for i in range(2)]
        else:
            self.ps = list(psum_from.ps)
            self.pst = list(psum_from.pst)
        self._pi = 0

    def alloc(self, name, n, nim=False):
        nch = max(1, n // 128)
        re = self.stack.enter_context(self.nc.sbuf_tensor(name + "r", [128, nch * n], F32))
        im = self.stack.enter_context(self.nc.sbuf_tensor(name + "i", [128, nch * n], F32))
        ni = self.stack.enter_context(self.nc.sbuf_tensor(name + "n", [128, nch * n], F32)) if nim else None
        return CM(re, im, ni, n)

    def _psum2(self):
        a = self.ps[self._pi % 4]
        b = self.ps[(self._pi + 1) % 4]
        self._pi += 2
        return a, b

    class _SyncV:

        def __init__(self, inv, v, pe_wait):
            object.__setattr__(self, "_inv", inv)
            object.__setattr__(self, "_v", v)
            object.__setattr__(self, "_pw", [pe_wait])

        def __getattr__(self, name):
            f = getattr(self._v, name)
            if not callable(f) or name.startswith("wait"):
                return f
            inv = self._inv

            def wrapped(*a, **k):
                if self._pw[0] is not None:
                    self._v.wait_ge(inv.pe, self._pw[0])
                    self._pw[0] = None
                r = f(*a, **k)
                r._wait_ge(inv.dv, inv.ndv)
                r.then_inc(inv.dv, 1)
                inv.ndv += 1
                return r
            return wrapped

    def dve(self, fn, k=None):
        @self.block.vector
        def _(v):
            fn(Inv._SyncV(self, v, self.npe))

    def make_nim(self, M: CM):
        self.dve(lambda v: v.tensor_scalar_mul(M.nim[:, :], M.im[:, :], -1.0))

    def cmm(self, out: CM, A: CM, B: CM, n, a_r0=0, a_c0=0, b_r0=0, b_c0=0,
            o_r0=0, o_c0=0, post=None, nA=None, nB=None, nO=None):
        nA = nA or A.n
        nB = nB or B.n
        nO = nO or out.n
        nch = n // 128
        for mc in range(nch):
            psr, psi = self._psum2()
            m0 = a_c0 + mc * 128

            @self.block.tensor
            def _(t, psr=psr, psi=psi, m0=m0):
                first = [True]
                _mm = t.matmul

                def matmul_w(*a, **k):
                    r = _mm(*a, **k)
                    if first[0]:
                        first[0] = False
                        r._wait_ge(self.dv, self.ndv)
                    return r
                t = type("TW", (), {"matmul": staticmethod(matmul_w)})()

                def aps(kc):
                    ac = (a_r0 + kc * 128) // 128
                    bc = (b_r0 + kc * 128) // 128
                    return (A.re[:, ac * nA + m0: ac * nA + m0 + 128],
                            A.im[:, ac * nA + m0: ac * nA + m0 + 128],
                            A.nim[:, ac * nA + m0: ac * nA + m0 + 128],
                            B.re[:, bc * nB + b_c0: bc * nB + b_c0 + n],
                            B.im[:, bc * nB + b_c0: bc * nB + b_c0 + n])
                last = None
                for kc in range(nch):
                    a_re, a_im, a_ni, b_re, b_im = aps(kc)
                    t.matmul(psr[:, 0:n], a_re, b_re, start=(kc == 0), stop=False)
                for kc in range(nch):
                    a_re, a_im, a_ni, b_re, b_im = aps(kc)
                    t.matmul(psr[:, 0:n], a_ni, b_im, start=False,
                             stop=(kc == nch - 1))
                for kc in range(nch):
                    a_re, a_im, a_ni, b_re, b_im = aps(kc)
                    t.matmul(psi[:, 0:n], a_re, b_im, start=(kc == 0), stop=False)
                for kc in range(nch):
                    a_re, a_im, a_ni, b_re, b_im = aps(kc)
                    last = t.matmul(psi[:, 0:n], a_im, b_re, start=False,
                                    stop=(kc == nch - 1))
                last.then_inc(self.pe, 1)
            self.npe += 1

            oc = (o_r0 // 128) + mc
            o_re = out.re[:, oc * nO + o_c0: oc * nO + o_c0 + n]
            o_im = out.im[:, oc * nO + o_c0: oc * nO + o_c0 + n]

            if post is None:
                def post_(v, psr, psi, o_re, o_im, mc):
                    v.tensor_copy(o_re, psr)
                    return v.tensor_copy(o_im, psi)
            else:
                post_ = post
            self.dve(lambda v, psr=psr, psi=psi, o_re=o_re, o_im=o_im, mc=mc:
                     post_(v, psr[:, 0:n], psi[:, 0:n], o_re, o_im, mc))

    def ctrans(self, out: CM, src: CM, n, s_r0=0, s_c0=0, o_r0=0, o_c0=0,
               nS=None, nO=None, negate_to_nim=True):
        nS = nS or src.n
        nO = nO or out.n
        nch = n // 128
        for rc in range(nch):
            for cc in range(nch):
                sc = (s_r0 + rc * 128) // 128
                s_re = src.re[:, sc * nS + s_c0 + cc * 128: sc * nS + s_c0 + cc * 128 + 128]
                s_im = src.im[:, sc * nS + s_c0 + cc * 128: sc * nS + s_c0 + cc * 128 + 128]
                pr = self.pst[0]
                pi = self.pst[1]

                @self.block.tensor
                def _(t, s_re=s_re, s_im=s_im, pr=pr, pi=pi):
                    r = t.transpose(pr[:, 0:128], s_re, self.ident)
                    r._wait_ge(self.dv, self.ndv)
                    t.transpose(pi[:, 0:128], s_im, self.ident).then_inc(self.pe, 1)
                self.npe += 1

                oc = (o_r0 + cc * 128) // 128
                o_re = out.re[:, oc * nO + o_c0 + rc * 128: oc * nO + o_c0 + rc * 128 + 128]
                o_im = out.im[:, oc * nO + o_c0 + rc * 128: oc * nO + o_c0 + rc * 128 + 128]
                if negate_to_nim and out.nim is not None:
                    o_ni = out.nim[:, oc * nO + o_c0 + rc * 128: oc * nO + o_c0 + rc * 128 + 128]

                    def post(v, o_re=o_re, o_im=o_im, o_ni=o_ni, pr=pr, pi=pi):
                        v.tensor_copy(o_re, pr[:, 0:128])
                        v.tensor_copy(o_im, pi[:, 0:128])
                        return v.tensor_scalar_mul(o_ni, pi[:, 0:128], -1.0)
                else:
                    def post(v, o_re=o_re, o_im=o_im, pr=pr, pi=pi):
                        v.tensor_copy(o_re, pr[:, 0:128])
                        return v.tensor_copy(o_im, pi[:, 0:128])
                self.dve(post)

    def newton(self, Wt: CM, X: CM, Xs: CM, Xt: CM, M: CM, twoI,
               n, iters, w_r0=0, alpha_bc=None):
        nch = n // 128
        if alpha_bc is not None:
            for c in range(nch):
                wc = (w_r0 + c * 128) // 128
                src_re = Wt.re[:, wc * Wt.n + w_r0: wc * Wt.n + w_r0 + n]
                src_im = Wt.im[:, wc * Wt.n + w_r0: wc * Wt.n + w_r0 + n]
                dst_re = X.re[:, c * n: c * n + n]
                dst_im = X.im[:, c * n: c * n + n]

                def init(v, src_re=src_re, src_im=src_im, dst_re=dst_re, dst_im=dst_im):
                    v.tensor_scalar_mul(dst_re, src_re, alpha_bc)
                    return v.tensor_scalar(dst_im, src_im, alpha_bc, -1.0,
                                           ALU.mult, ALU.mult)
                self.dve(init)

        cur, other = X, Xs
        for it in range(iters):
            def post_m(v, psr, psi, o_re, o_im, mc):
                v.scalar_tensor_tensor(o_re, psr, -1.0,
                                       twoI[:, mc * n:(mc + 1) * n], ALU.mult, ALU.add)
                return v.tensor_scalar_mul(o_im, psi, -1.0)
            self.cmm(M, Wt, cur, n, a_r0=w_r0, a_c0=w_r0, post=post_m,
                     nA=Wt.n, nB=n, nO=n)
            self.ctrans(Xt, cur, n, nS=n, nO=n, negate_to_nim=True)
            self.cmm(other, Xt, M, n, nA=n, nB=n, nO=n)
            cur, other = other, cur
        return cur


def emit_alpha(inv: Inv, M: CM, r0, c0, n, nM, alpha_bc, scratch, ones_col,
               ones_row, tiny11, ps11, ps128):
    nch = n // 128
    acc = scratch

    def emit(v):
        first = True
        last = None
        for plane in (M.re, M.im):
            for c in range(nch):
                mc = (r0 + c * 128) // 128
                blk = plane[:, mc * nM + c0: mc * nM + c0 + n]
                v.tensor_mul(scratch[:, 2:2 + n], blk, blk)
                if first:
                    last = v.reduce_sum(scratch[:, 0:1], scratch[:, 2:2 + n],
                                        axis=mybir.AxisListType.X)
                    first = False
                else:
                    v.reduce_sum(scratch[:, 1:2], scratch[:, 2:2 + n],
                                 axis=mybir.AxisListType.X)
                    last = v.tensor_add(scratch[:, 0:1], scratch[:, 0:1],
                                        scratch[:, 1:2])
        return last
    inv.dve(emit)

    @inv.block.tensor
    def _(t):
        r = t.matmul(ps11[:, :], scratch[:, 0:1], ones_col, start=True, stop=True)
        r._wait_ge(inv.dv, inv.ndv)
        r.then_inc(inv.pe, 1)
    inv.npe += 1

    def emit2(v):
        return v.reciprocal(tiny11[:, :], ps11[:, :])
    inv.dve(emit2)

    @inv.block.tensor
    def _(t):
        r = t.matmul(ps128[:, :], ones_row, tiny11[:, :], start=True, stop=True)
        r._wait_ge(inv.dv, inv.ndv)
        r.then_inc(inv.pe, 1)
    inv.npe += 1

    inv.dve(lambda v: v.tensor_copy(alpha_bc, ps128[:, :]))


def emit_inv256(inv: Inv, N: CM, Nt: CM, r0, out: CM, outT: CM, bufs, consts,
                leaf_iters=30):
    XA, XS = bufs["X128a"], bufs["X128b"]
    Xs, Xt, M128 = bufs["Xs128"], bufs["Xt128"], bufs["M128"]
    P_, Pt, Q, S, St, SiQ = bufs["P128"], bufs["Pt128"], bufs["Q128"], \
        bufs["S128"], bufs["St128"], bufs["SiQ128"]
    a0, d0 = r0, r0 + 128

    emit_alpha(inv, Nt, a0, a0, 128, Nt.n, bufs["alpha"], bufs["scr"],
               consts["ones_col"], consts["ones_row"], bufs["t11"],
               bufs["ps11"], bufs["ps128"])
    Ai = inv.newton(Nt, XA, Xs, Xt, M128, consts["twoI128"], 128, leaf_iters,
                    w_r0=a0, alpha_bc=bufs["alpha"][:, 0:1])
    if Ai is not XA:
        inv.dve(lambda v: (v.tensor_copy(XA.re[:, :], Ai.re[:, :]),
                           v.tensor_copy(XA.im[:, :], Ai.im[:, :]))[-1])
    AiT = bufs["AiT128"]
    inv.ctrans(AiT, XA, 128)
    inv.cmm(P_, AiT, N, 128, b_r0=a0, b_c0=d0, nB=N.n)
    Dre = lambda mc: N.re[:, (d0 // 128) * N.n + d0: (d0 // 128) * N.n + d0 + 128]
    Dim = lambda mc: N.im[:, (d0 // 128) * N.n + d0: (d0 // 128) * N.n + d0 + 128]

    def post_s(v, psr, psi, o_re, o_im, mc):
        v.scalar_tensor_tensor(o_re, psr, -1.0, Dre(mc), ALU.mult, ALU.add)
        return v.scalar_tensor_tensor(o_im, psi, -1.0, Dim(mc), ALU.mult, ALU.add)
    inv.cmm(S, Nt, P_, 128, a_r0=a0, a_c0=d0, post=post_s, nA=Nt.n, nB=128, nO=128)
    inv.ctrans(St, S, 128)
    emit_alpha(inv, St, 0, 0, 128, 128, bufs["alpha"], bufs["scr"],
               consts["ones_col"], consts["ones_row"], bufs["t11"],
               bufs["ps11"], bufs["ps128"])
    Si = inv.newton(St, XS, Xs, Xt, M128, consts["twoI128"], 128, leaf_iters,
                    w_r0=0, alpha_bc=bufs["alpha"][:, 0:1])
    if Si is not XS:
        inv.dve(lambda v: (v.tensor_copy(XS.re[:, :], Si.re[:, :]),
                           v.tensor_copy(XS.im[:, :], Si.im[:, :]))[-1])
    SiT = bufs["SiT128"]
    inv.ctrans(SiT, XS, 128)
    inv.cmm(Q, Nt, XA, 128, a_r0=a0, a_c0=d0, nA=Nt.n, nB=128, nO=128)
    inv.cmm(SiQ, SiT, Q, 128, nA=128, nB=128, nO=128)
    inv.ctrans(Pt, P_, 128)
    def post_tl(v, psr, psi, o_re, o_im, mc):
        v.tensor_add(o_re, psr, XA.re[:, 0:128])
        return v.tensor_add(o_im, psi, XA.im[:, 0:128])
    inv.cmm(out, Pt, SiQ, 128, o_r0=0, o_c0=0, post=post_tl, nA=128, nB=128)
    def post_neg(v, psr, psi, o_re, o_im, mc):
        v.tensor_scalar_mul(o_re, psr, -1.0)
        return v.tensor_scalar_mul(o_im, psi, -1.0)
    inv.cmm(out, Pt, XS, 128, o_r0=0, o_c0=128, post=post_neg, nA=128, nB=128)
    oc = 1
    def emit_bl(v):
        v.tensor_scalar_mul(out.re[:, oc * out.n + 0: oc * out.n + 128], SiQ.re[:, :], -1.0)
        v.tensor_scalar_mul(out.im[:, oc * out.n + 0: oc * out.n + 128], SiQ.im[:, :], -1.0)
        v.tensor_copy(out.re[:, oc * out.n + 128: oc * out.n + 256], XS.re[:, :])
        return v.tensor_copy(out.im[:, oc * out.n + 128: oc * out.n + 256], XS.im[:, :])
    inv.dve(emit_bl)
    inv.ctrans(outT, out, 256)


def emit_invert512(inv: Inv, W: CM, Wt: CM, Wi: CM, bufs, consts,
                   leaf_iters=30, polish=6):
    Ai, AiT = bufs["Ai256"], bufs["AiT256"]
    emit_inv256(inv, W, Wt, 0, Ai, AiT, bufs, consts, leaf_iters)
    P_, Pt, Q, S, St, SiQ = bufs["P256"], bufs["Pt256"], bufs["Q256"], \
        bufs["S256"], bufs["St256"], bufs["SiQ256"]
    inv.cmm(P_, AiT, W, 256, b_r0=0, b_c0=256, nB=W.n)
    def post_s(v, psr, psi, o_re, o_im, mc):
        dc = (256 + mc * 128) // 128
        v.scalar_tensor_tensor(o_re, psr, -1.0,
                               W.re[:, dc * W.n + 256: dc * W.n + 512], ALU.mult, ALU.add)
        return v.scalar_tensor_tensor(o_im, psi, -1.0,
                                      W.im[:, dc * W.n + 256: dc * W.n + 512], ALU.mult, ALU.add)
    inv.cmm(S, Wt, P_, 256, a_r0=0, a_c0=256, post=post_s, nA=Wt.n, nB=256, nO=256)
    inv.ctrans(St, S, 256)
    Si, SiT = bufs["Si256"], bufs["SiT256"]
    emit_inv256(inv, S, St, 0, Si, SiT, bufs, consts, leaf_iters)
    inv.cmm(Q, Wt, Ai, 256, a_r0=0, a_c0=256, nA=Wt.n, nB=256, nO=256)
    inv.cmm(SiQ, SiT, Q, 256, nA=256, nB=256, nO=256)
    inv.ctrans(Pt, P_, 256)
    def post_tl(v, psr, psi, o_re, o_im, mc):
        v.tensor_add(o_re, psr, Ai.re[:, mc * 256: mc * 256 + 256])
        return v.tensor_add(o_im, psi, Ai.im[:, mc * 256: mc * 256 + 256])
    inv.cmm(Wi, Pt, SiQ, 256, o_r0=0, o_c0=0, post=post_tl, nA=256, nB=256)
    def post_neg(v, psr, psi, o_re, o_im, mc):
        v.tensor_scalar_mul(o_re, psr, -1.0)
        return v.tensor_scalar_mul(o_im, psi, -1.0)
    inv.cmm(Wi, Pt, Si, 256, o_r0=0, o_c0=256, post=post_neg, nA=256, nB=256)
    def emit_b(v):
        last = None
        for mc in range(2):
            oc = 2 + mc
            last = v.tensor_scalar_mul(Wi.re[:, oc * Wi.n + 0: oc * Wi.n + 256],
                                       SiQ.re[:, mc * 256: mc * 256 + 256], -1.0)
            last = v.tensor_scalar_mul(Wi.im[:, oc * Wi.n + 0: oc * Wi.n + 256],
                                       SiQ.im[:, mc * 256: mc * 256 + 256], -1.0)
            last = v.tensor_copy(Wi.re[:, oc * Wi.n + 256: oc * Wi.n + 512],
                                 Si.re[:, mc * 256: mc * 256 + 256])
            last = v.tensor_copy(Wi.im[:, oc * Wi.n + 256: oc * Wi.n + 512],
                                 Si.im[:, mc * 256: mc * 256 + 256])
        return last
    inv.dve(emit_b)
    if polish > 0:
        res = inv.newton(Wt, Wi, bufs["Xs512"], bufs["Xt512"], bufs["M512"],
                         consts["twoI512"], 512, polish, w_r0=0, alpha_bc=None)
        if res is not Wi:
            inv.dve(lambda v: (v.tensor_copy(Wi.re[:, :], res.re[:, :]),
                               v.tensor_copy(Wi.im[:, :], res.im[:, :]))[-1])


F32 = np.float32


def chunked(M):
    r, c = M.shape
    nch = r // 128
    out = np.empty((128, nch * c), M.dtype)
    for k in range(nch):
        out[:, k * c:(k + 1) * c] = M[k * 128:(k + 1) * 128, :]
    return np.ascontiguousarray(out)


def unchunk(T, rows, cols):
    nch = rows // 128
    M = np.empty((rows, cols), T.dtype)
    for k in range(nch):
        M[k * 128:(k + 1) * 128, :] = T[:, k * cols:(k + 1) * cols]
    return M


def twoI_tile(n):
    nch = n // 128
    t = np.zeros((128, nch * n), np.float32)
    for c in range(nch):
        for p in range(128):
            t[p, c * n + c * 128 + p] = 2.0
    return t


def vec_chunk(v):
    n = v.shape[0]
    return np.ascontiguousarray(v.reshape(n // 128, 128).T.astype(v.dtype))


PERM_REIM = np.concatenate([np.arange(0, 1024, 2), np.arange(1, 1024, 2)])


def ab_inputs(xb, d, T):
    f = lambda a: np.ascontiguousarray(np.asarray(a, np.float32))
    w_a, w_x = f(d["L0_attn_w_a"]), f(d["L0_attn_w_x"])
    V_r, V_i = f(d["L0_attn_mat_v_real"]), f(d["L0_attn_mat_v_imag"])
    out = dict(
        xT=chunked(f(xb).T.copy()),
        Vt_r=chunked(V_r.T.copy()), Vt_i=chunked(V_i.T.copy()),
        Vn_r=chunked(V_r), Vn_i=chunked(V_i),
        waT=chunked(np.ascontiguousarray(w_a.T[:, PERM_REIM])),
        wxT=chunked(np.ascontiguousarray(w_x.T[:, PERM_REIM])),
        woT=chunked(f(d["L0_attn_w_out"]).T.copy()),
        h0r=vec_chunk(f(d["L0_attn_last_hidden_real"])),
        h0i=vec_chunk(f(d["L0_attn_last_hidden_imag"])),
        W1T=chunked(f(d["L0_ffn_W1"]).T.copy()),
        b1c=vec_chunk(f(d["L0_ffn_b1"])),
        W2T=chunked(f(d["L0_ffn_W2"]).T.copy()),
        b2c=vec_chunk(f(d["L0_ffn_b2"])),
        ident=np.eye(128, dtype=np.float32),
        twoI512=twoI_tile(512), twoI128=twoI_tile(128),
        onesc=np.ones((128, 1), np.float32), onesr=np.ones((1, 128), np.float32),
        onesW=np.ones((128, 512), np.float32),
    )
    return out


def c_inputs(x2b_perm, d, T):
    f = lambda a: np.ascontiguousarray(np.asarray(a, np.float32))
    masks = np.zeros((128, 4 * 512), np.float32)
    ii = np.arange(128)[:, None]
    jj = np.arange(512)[None, :]
    for p in range(4):
        masks[:, p * 512:(p + 1) * 512] = np.where(ii > jj - 128 * p, -1e30, 0.0)
    return dict(
        x2T=chunked(f(x2b_perm).T.copy()),
        dqc=chunked(f(d["L1_attn_W_DQ"])),
        uqt=chunked(f(d["L1_attn_W_UQ"]).T.copy()),
        dkvc=chunked(f(d["L1_attn_W_DKV"])),
        ukvt=chunked(f(d["L1_attn_W_UKV"]).T.copy()),
        qrt=f(d["L1_attn_W_QR"]).T.copy(),
        krt=f(d["L1_attn_W_KR"]).T.copy(),
        wot=f(d["L1_attn_W_O"]).T.copy(),
        qnw=f(d["L1_attn_q_norm_w"]).reshape(64, 1),
        knw=f(d["L1_attn_k_norm_w"]).reshape(64, 1),
        identb=np.eye(64, dtype=np.float32),
        identf=np.eye(128, dtype=np.float32),
        masks=masks,
        onesc=np.ones((128, 1), np.float32), onesr=np.ones((1, 128), np.float32),
    )


def head_perm(j):
    mine = np.arange(256 * j, 256 * j + 256)
    rest = np.concatenate([np.arange(0, 256 * j), np.arange(256 * j + 256, 1024)])
    return np.concatenate([mine, rest])


def f_inputs(xb, d, T, r):
    f = lambda a: np.ascontiguousarray(np.asarray(a, np.float32))
    out = ab_inputs(xb, d, T)
    W1 = f(d["L1_ffn_W1"])
    w1e = np.empty((128, 4 * 32 * 1024), np.float32)
    for e in range(4):
        ck = chunked(W1[e])
        for i in range(32):
            for k in range(8):
                w1e[:, (e * 32 + i) * 1024 + k * 128:(e * 32 + i) * 1024 + (k + 1) * 128] = \
                    ck[:, k * 4096 + i * 128: k * 4096 + (i + 1) * 128]
    masks = np.zeros((128, 16 * 512), np.float32)
    ii = np.arange(128)[:, None]
    jj = np.arange(512)[None, :]
    qpos = 128 * (4 * (jj // 128) + r) + (jj % 128)
    for dd in range(16):
        masks[:, dd * 512:(dd + 1) * 512] = np.where(128 * dd + ii <= qpos, 0.0, -1e30)
    out.update(
        dqc=chunked(f(d["L1_attn_W_DQ"])),
        uqt=chunked(f(d["L1_attn_W_UQ"]).T.copy()),
        dkvc=chunked(f(d["L1_attn_W_DKV"])),
        ukvt=chunked(f(d["L1_attn_W_UKV"]).T.copy()),
        qrt=f(d["L1_attn_W_QR"]).T.copy(),
        krt=f(d["L1_attn_W_KR"]).T.copy(),
        wot=f(d["L1_attn_W_O"]).T.copy(),
        qnw=f(d["L1_attn_q_norm_w"]).reshape(64, 1),
        knw=f(d["L1_attn_k_norm_w"]).reshape(64, 1),
        masks16=masks,
        rmask=np.ascontiguousarray(
            np.broadcast_to((np.arange(4) == r).astype(np.float32), (128, 4))),
        wrc=chunked(f(d["L1_ffn_W_router"])),
        w1e=w1e,
        b1cD=np.ascontiguousarray(f(d["L1_ffn_b1"]).reshape(4, 32, 128)
                                  .transpose(2, 0, 1).reshape(128, 128)),
        woc=chunked(f(d["L1_ffn_Wo"])),
        boc=vec_chunk(f(d["L1_ffn_bo"])),
        gam=vec_chunk(f(d["L1_ffn_ln_gamma"])),
        bet=vec_chunk(f(d["L1_ffn_ln_beta"])),
    )
    return out


def f_reassemble(outs, T):
    TD = T // 4
    res = np.empty((2, T, 1024), np.float32)
    for c in range(8):
        b, r = c // 4, c % 4
        o = unchunk(outs[c], 1024, TD)
        for blk in range(TD // 128):
            g = 4 * blk + r
            res[b, 128 * g:128 * g + 128, :] = o[:, 128 * blk:128 * blk + 128].T
    return res




F32 = mybir.dt.float32
BF16 = mybir.dt.bfloat16
I32 = mybir.dt.int32
AF = mybir.ActivationFunctionType
ALU = mybir.AluOpType
AX = mybir.AxisListType.X
PI = float(np.pi)

W = 512
W2W = 512


AB_INPUTS = [
    ("xT", None), ("Vt_r", [128, 2048]), ("Vt_i", [128, 2048]),
    ("Vn_r", [128, 2048]), ("Vn_i", [128, 2048]),
    ("waT", [128, 8 * 1024]), ("wxT", [128, 8 * 1024]),
    ("woT", [128, 4 * 1024]), ("h0r", [128, 4]), ("h0i", [128, 4]),
    ("W1T", [128, 8 * 4096]), ("b1c", [128, 32]),
    ("W2T", [128, 32 * 1024]), ("b2c", [128, 8]),
    ("ident", [128, 128]), ("twoI512", [128, 2048]), ("twoI128", [128, 128]),
    ("onesc", [128, 1]), ("onesr", [1, 128]), ("onesW", [128, 512]),
]


def build_AB(T=8192, leaf_iters=26, polish=6):
    nc = bass.Bass("TRN2", target_bir_lowering=False)
    din = {}
    for nm, shp in AB_INPUTS:
        if shp is None:
            shp = [128, 8 * T]
        din[nm] = nc.dram_tensor(nm, shp, F32, kind="ExternalInput").ap()
    x2T_out = nc.dram_tensor("x2T", [128, 8 * T], F32, kind="ExternalOutput").ap()
    stack = contextlib.ExitStack()
    block = stack.enter_context(nc.Block())
    p = Prog(nc, block, stack)
    p.alloc_psum(n_big=4)
    emit_L0(nc, stack, block, p, din, x2T_out, T, leaf_iters, polish)

    @block.sync
    def _(s):
        s.wait_ge(p.qs, p.nqs)

    nc._stack = stack
    return nc


def emit_L0(nc, stack, block, p, din, x2T_out, T, leaf_iters=26, polish=6):
    nslab = T // W
    x1_hbm = nc.dram_tensor("x1_scratch", [128, 8 * T], F32)
    w2t_hbm = nc.dram_tensor("w2t_scratch", [128, 32 * 1024], BF16)
    outer_stack = p.stack
    l0_stack = contextlib.ExitStack()
    p.stack = l0_stack

    sident = p.sb("sident", [128, 128], F32)
    s2I128 = p.sb("s2I128", [128, 128], F32)
    onesc = p.sb("sonesc", [128, 1], F32)
    onesr = p.sb("sonesr", [1, 128], F32)
    onesW = p.sb("sonesW", [128, 512], F32)
    h0r = p.sb("sh0r", [128, 4], F32)
    h0i = p.sb("sh0i", [128, 4], F32)
    xs = p.sb("t_x", [128, 8 * W], F32)
    xn = p.sb("t_xn", [128, 8 * W], BF16)
    onescb = p.sb("sonescb", [128, 1], BF16)
    lnsc = make_ln_scratch(p, W)
    lnsc["ones_col"] = onesc[:, :]
    lnsc["ones_col_b"] = onescb[:, :]
    lnsc["ones_row"] = onesr[:, :]

    for nm, tile in [("ident", sident), ("twoI128", s2I128), ("onesc", onesc),
                     ("onesr", onesr), ("onesW", onesW), ("h0r", h0r), ("h0i", h0i)]:
        p.dma_s(tile[:, :], din[nm])

    _sc0 = nc.enter_named_scope("p0_inv", False)[0]
    p1_stack = contextlib.ExitStack()

    def p1sb(name, shape, dtype=F32):
        return p1_stack.enter_context(nc.sbuf_tensor(name, shape, dtype))

    Wi_bf_r = p1sb("Wi_bf_r", [128, 2048], BF16)
    Wi_bf_i = p1sb("Wi_bf_i", [128, 2048], BF16)
    Wi_bf_ni = p1sb("Wi_bf_ni", [128, 2048], BF16)
    Vt_bf_r = p1sb("Vt_bf_r", [128, 2048], BF16)
    Vt_bf_ni = p1sb("Vt_bf_ni", [128, 2048], BF16)

    inv_stack = contextlib.ExitStack()
    inv = Inv(nc, block, inv_stack, sident[:, :], psum_from=p)
    s2I512 = inv_stack.enter_context(nc.sbuf_tensor("s2I512", [128, 2048], F32))

    Wm = inv.alloc("Wm", 512)
    Wt = inv.alloc("Wt", 512, nim=True)
    Wi = inv.alloc("Wi", 512)
    consts_inv = {"twoI512": s2I512[:, :], "twoI128": s2I128[:, :],
                  "ones_col": onesc[:, :], "ones_row": onesr[:, :]}
    sscr0 = inv_stack.enter_context(nc.sbuf_tensor("sscr0", [1, 1], F32))
    glue_stack = contextlib.ExitStack()
    inv.stack = glue_stack
    bufs = {}
    for nm, n, nim in [
        ("X128a", 128, False), ("X128b", 128, False), ("Xs128", 128, False),
        ("Xt128", 128, True), ("M128", 128, False), ("P128", 128, False),
        ("Pt128", 128, True), ("Q128", 128, False), ("S128", 128, False),
        ("St128", 128, True), ("SiQ128", 128, False), ("AiT128", 128, True),
        ("SiT128", 128, True),
        ("Ai256", 256, False), ("AiT256", 256, True), ("P256", 256, False),
        ("Pt256", 256, True), ("Q256", 256, False), ("S256", 256, False),
        ("St256", 256, True), ("SiQ256", 256, False), ("Si256", 256, False),
        ("SiT256", 256, True),
    ]:
        bufs[nm] = inv.alloc(nm, n, nim=nim)
    salpha = glue_stack.enter_context(nc.sbuf_tensor("salpha", [128, 1], F32))
    sscr = glue_stack.enter_context(nc.sbuf_tensor("sscr", [128, 516], F32))
    st11 = glue_stack.enter_context(nc.sbuf_tensor("st11", [1, 1], F32))
    bufs["alpha"] = salpha[:, 0:1]
    bufs["scr"] = sscr[:, :]
    bufs["t11"] = st11[:, :]
    bufs["ps11"] = p.misc1[0:1, 0:1]
    bufs["ps128"] = p.misc2[:, 0:1]

    p.dma_s(s2I512[:, :], din["twoI512"])
    p.dma_s(Wm.re[:, :], din["Vt_r"])
    p.dma_s(Wm.im[:, :], din["Vt_i"])
    p.dma_s(Wt.re[:, :], din["Vn_r"])
    p.dma_s(Wt.im[:, :], din["Vn_i"])

    @block.vector
    def _(v):
        v.wait_ge(p.qs, p.nqs)
        v.memset(sscr0[:, 0:1], 0.0).then_inc(inv.dv, 1)
    inv.ndv += 1

    inv.make_nim(Wt)
    emit_invert512(inv, Wm, Wt, Wi, bufs, consts_inv,
                      leaf_iters=leaf_iters, polish=0)
    glue_stack.close()
    pol_stack = contextlib.ExitStack()
    inv.stack = pol_stack
    Xs512 = inv.alloc("Xs512", 512)
    Xt512 = inv.alloc("Xt512", 512, nim=True)
    M512 = inv.alloc("M512", 512)
    res = inv.newton(Wt, Wi, Xs512, Xt512, M512, s2I512[:, :], 512, polish)
    if res is not Wi:
        inv.dve(lambda v: (v.tensor_copy(Wi.re[:, :], res.re[:, :]),
                           v.tensor_copy(Wi.im[:, :], res.im[:, :]))[-1])
    pol_stack.close()
    inv.stack = inv_stack

    def cast(v):
        v.tensor_copy(Wi_bf_r[:, :], Wi.re[:, :])
        v.tensor_copy(Wi_bf_i[:, :], Wi.im[:, :])
        v.tensor_scalar_mul(Wi_bf_ni[:, :], Wi.im[:, :], -1.0)
        v.tensor_copy(Vt_bf_r[:, :], Wm.re[:, :])
        v.tensor_scalar_mul(Vt_bf_ni[:, :], Wm.im[:, :], -1.0)
    inv.dve(cast)

    @block.vector
    def _(v):
        v.wait_ge(inv.dv, inv.ndv)
        v.memset(p.bar[:, :], 0.0).then_inc(p.el, 1)
    p.nel += 1
    inv_stack.close()

    nc.leave_named_scope("p0_inv", _sc0, False)
    _sc1 = nc.enter_named_scope("p1_mixer", False)[0]
    waT = p1sb("s_waT", [128, 8 * 1024], BF16)
    wxT = p1sb("s_wxT", [128, 8 * 1024], BF16)
    woT = p1sb("s_woT", [128, 4 * 1024], BF16)
    p.dma_g(waT[:, :], din["waT"], wait=[(p.el, p.nel)])
    p.dma_g(wxT[:, :], din["wxT"])
    p.dma_g(woT[:, :], din["woT"])
    n_wdma = p.nqg

    ar = p1sb("t_ar", [128, 4 * W], BF16)
    ai = p1sb("t_ai", [128, 4 * W], BF16)
    xr = p1sb("t_xr", [128, 4 * W], BF16)
    xi = p1sb("t_xi", [128, 4 * W], BF16)
    fbig = p1sb("t_fbig", [128, 12 * W], F32)
    f1 = fbig[:, 0 * W:4 * W]
    f2 = fbig[:, 4 * W:8 * W]
    f3 = fbig[:, 8 * W:12 * W]
    x1s = fbig[:, 0:8 * W]
    rho = p1sb("t_rho", [128, 4 * W], F32)
    sinT = p1sb("t_sin", [128, 4 * W], BF16)
    cosT = p1sb("t_cos", [128, 4 * W], BF16)
    gar = p1sb("t_gar", [128, 4 * W], BF16)
    gai = p1sb("t_gai", [128, 4 * W], BF16)
    zr = p1sb("t_zr", [128, 4 * W], F32)
    zi = p1sb("t_zi", [128, 4 * W], F32)
    thh = zr
    Th = zi
    WN = W + 1
    winr = p1sb("t_winr", [128, 4 * WN], BF16)
    wini = p1sb("t_wini", [128, 4 * WN], BF16)
    hrt = p1sb("t_hr", [128, 4 * W], BF16)
    ur, ui = xr, xi
    btr, bti = ar, ai
    mo = xn
    carT = p1sb("c_T", [128, 4], F32)
    carZr = p1sb("c_zr", [128, 4], F32)
    carZi = p1sb("c_zi", [128, 4], F32)

    def ch(tile, c, w=W):
        return tile[:, c * w:(c + 1) * w]

    def init_car(v):
        v.memset(onescb[:, :], 1.0)
        v.memset(carT[:, :], 0.0)
        v.memset(carZr[:, :], 0.0)
        v.memset(carZi[:, :], 0.0)
        for c in range(4):
            v.tensor_copy(winr[:, c * WN: c * WN + 1], h0r[:, c:c + 1])
            v.tensor_copy(wini[:, c * WN: c * WN + 1], h0i[:, c:c + 1])
    p.vec(init_car)

    for s in range(nslab):
        col0 = s * W
        for c in range(8):
            p.dma_s(ch(xs, c), din["xT"][:, c * T + col0: c * T + col0 + W],
                    wait=([(p.el, p.nel)] if c == 0 else None))
        dma_tgt = p.nqs
        p.el_barrier([(p.qs, dma_tgt), (p.qg, n_wdma)])

        emit_ln(p, lambda c: ch(xs, c), lambda c: ch(xn, c), W, lnsc,
                  x_full=xs[:, 0:8 * W], out_full=xn[:, 0:8 * W])

        def evac_a(m, ps):
            dst = ar if m < 4 else ai
            p.vec(lambda v, dst=dst, mm=m % 4, ps=ps:
                  v.tensor_copy(ch(dst, mm), ps[:, 0:W]))
        emit_mm(p, None,
                  lambda k, m: waT[:, k * 1024 + m * 128: k * 1024 + m * 128 + 128],
                  lambda k: ch(xn, k), 8, 8, W, evac_a)

        def evac_x(m, ps):
            dst = xr if m < 4 else xi
            p.vec(lambda v, dst=dst, mm=m % 4, ps=ps:
                  v.tensor_copy(ch(dst, mm), ps[:, 0:W]))
        emit_mm(p, None,
                  lambda k, m: wxT[:, k * 1024 + m * 128: k * 1024 + m * 128 + 128],
                  lambda k: ch(xn, k), 8, 8, W, evac_x)

        FW = slice(0, 4 * W)

        def xcf(v):
            v.tensor_mul(f1[:, FW], xr[:, FW], xr[:, FW])
            v.tensor_mul(f2[:, FW], xi[:, FW], xi[:, FW])
            v.tensor_add(f1[:, FW], f1[:, FW], f2[:, FW])
        p.vec(xcf)

        def sqxf(a):
            a.activation(f2[:, FW], f1[:, FW], AF.Sqrt)
        p.act(sqxf)

        def xc2(v):
            for c in range(4):
                v.tensor_mul(winr[:, c * WN + 1: c * WN + 1 + W], ch(xr, c), ch(f2, c))
                v.tensor_mul(wini[:, c * WN + 1: c * WN + 1 + W], ch(xi, c), ch(f2, c))
        p.vec(xc2)

        def gate(v):
            v.tensor_mul(f1[:, FW], ar[:, FW], ar[:, FW])
            v.tensor_mul(f2[:, FW], ai[:, FW], ai[:, FW])
            v.tensor_add(f1[:, FW], f1[:, FW], f2[:, FW])
        p.vec(gate)

        def sqf(a):
            a.activation(f2[:, FW], f1[:, FW], AF.Sqrt)
        p.act(sqf)

        def gate2(v):
            v.tensor_scalar_add(f3[:, FW], f1[:, FW], 1.0)
            v.reciprocal(f3[:, FW], f3[:, FW])
            v.tensor_mul(rho[:, FW], f1[:, FW], f3[:, FW])
            v.tensor_mul(f3[:, FW], f2[:, FW], f3[:, FW])
            v.tensor_mul(gar[:, FW], f3[:, FW], ar[:, FW])
            v.tensor_mul(gai[:, FW], f3[:, FW], ai[:, FW])
            v.tensor_add(f2[:, FW], f2[:, FW], ar[:, FW])
            v.tensor_mul(f3[:, FW], f2[:, FW], f2[:, FW])
            v.tensor_mul(f1[:, FW], ai[:, FW], ai[:, FW])
            v.tensor_add(f3[:, FW], f3[:, FW], f1[:, FW])
        p.vec(gate2)

        def sq2f(a):
            a.activation(f1[:, FW], f3[:, FW], AF.Sqrt)
        p.act(sq2f)

        def gate3(v):
            v.tensor_add(f1[:, FW], f1[:, FW], f2[:, FW])
            v.tensor_scalar_max(f1[:, FW], f1[:, FW], 1e-30)
            v.reciprocal(f1[:, FW], f1[:, FW])
            v.tensor_mul(f2[:, FW], f1[:, FW], ai[:, FW])
        p.vec(gate3)

        def atanf(a):
            a.activation(thh[:, FW], f2[:, FW], AF.Arctan)
        p.act(atanf)

        for mch in range(4):
            ps = p.psum()

            def ufn(t, mch=mch, ps=ps):
                for k in range(4):
                    t.matmul(ps[:, 0:W],
                             Wi_bf_r[:, k * 512 + mch * 128: k * 512 + mch * 128 + 128],
                             winr[:, k * WN: k * WN + W], start=(k == 0), stop=False)
                last = None
                for k in range(4):
                    last = t.matmul(
                        ps[:, 0:W],
                        Wi_bf_ni[:, k * 512 + mch * 128: k * 512 + mch * 128 + 128],
                        wini[:, k * WN: k * WN + W], start=False, stop=(k == 3))
                return last
            p.tensor(ufn)
            p.vec(lambda v, mch=mch, ps=ps: v.tensor_copy(ch(ur, mch), ps[:, 0:W]))
            ps2 = p.psum()

            def ufn2(t, mch=mch, ps=ps2):
                for k in range(4):
                    t.matmul(ps[:, 0:W],
                             Wi_bf_i[:, k * 512 + mch * 128: k * 512 + mch * 128 + 128],
                             winr[:, k * WN: k * WN + W], start=(k == 0), stop=False)
                last = None
                for k in range(4):
                    last = t.matmul(
                        ps[:, 0:W],
                        Wi_bf_r[:, k * 512 + mch * 128: k * 512 + mch * 128 + 128],
                        wini[:, k * WN: k * WN + W], start=False, stop=(k == 3))
                return last
            p.tensor(ufn2)
            p.vec(lambda v, mch=mch, ps=ps2: v.tensor_copy(ch(ui, mch), ps[:, 0:W]))

        def scans1(v):
            for c in range(4):
                v.tensor_tensor_scan(ch(Th, c), onesW[:, 0:W], ch(thh, c),
                                     carT[:, c:c + 1], ALU.mult, ALU.add)
            for c in range(4):
                v.tensor_copy(carT[:, c:c + 1], Th[:, c * W + W - 1: c * W + W])
        p.vec(scans1)

        MAGIC = 12582912.0

        def rr1(v):
            H = Th[:, FW]
            v.tensor_scalar_mul(f1[:, FW], H, 2.0 / PI)
            v.tensor_scalar_add(f1[:, FW], f1[:, FW], MAGIC)
            v.tensor_scalar_add(f1[:, FW], f1[:, FW], -MAGIC)
            v.scalar_tensor_tensor(f1[:, FW], f1[:, FW], -PI / 2, H,
                                   ALU.mult, ALU.add)
            v.tensor_scalar(f2[:, FW], H, 2.0 / PI, 0.25, ALU.mult, ALU.add)
            v.tensor_scalar_add(f2[:, FW], f2[:, FW], MAGIC)
            v.tensor_scalar_add(f2[:, FW], f2[:, FW], -MAGIC)
            v.tensor_scalar_mul(f2[:, FW], f2[:, FW], -PI / 2)
            v.tensor_add(f2[:, FW], f2[:, FW], H)
            v.tensor_scalar_add(f2[:, FW], f2[:, FW], PI / 8)
            for t_ in (f1[:, FW], f2[:, FW]):
                v.tensor_scalar_min(t_, t_, 0.785396)
                v.tensor_scalar_max(t_, t_, -0.785396)
        p.vec(rr1)

        def sincos(a):
            a.activation(sinT[:, FW], f1[:, FW], AF.Sin, scale=4.0)
            a.activation(cosT[:, FW], f2[:, FW], AF.Sin, scale=4.0)
        p.act(sincos)

        def bmix(v):
            v.tensor_mul(f1[:, FW], gar[:, FW], ur[:, FW])
            v.tensor_mul(f2[:, FW], gai[:, FW], ui[:, FW])
            v.tensor_sub(f1[:, FW], f1[:, FW], f2[:, FW])
            v.tensor_mul(f2[:, FW], gar[:, FW], ui[:, FW])
            v.tensor_mul(f3[:, FW], gai[:, FW], ur[:, FW])
            v.tensor_add(f2[:, FW], f2[:, FW], f3[:, FW])
            v.tensor_mul(f3[:, FW], cosT[:, FW], f1[:, FW])
            v.tensor_mul(btr[:, FW], sinT[:, FW], f2[:, FW])
            v.tensor_add(btr[:, FW], btr[:, FW], f3[:, FW])
            v.tensor_mul(f3[:, FW], cosT[:, FW], f2[:, FW])
            v.tensor_mul(f2[:, FW], sinT[:, FW], f1[:, FW])
            v.tensor_sub(bti[:, FW], f3[:, FW], f2[:, FW])
        p.vec(bmix)

        def scans2(v):
            for c in range(4):
                v.tensor_tensor_scan(ch(zr, c), ch(rho, c), ch(btr, c),
                                     carZr[:, c:c + 1], ALU.mult, ALU.add)
                v.tensor_tensor_scan(ch(zi, c), ch(rho, c), ch(bti, c),
                                     carZi[:, c:c + 1], ALU.mult, ALU.add)
            for c in range(4):
                v.tensor_copy(carZr[:, c:c + 1], zr[:, c * W + W - 1: c * W + W])
                v.tensor_copy(carZi[:, c:c + 1], zi[:, c * W + W - 1: c * W + W])
        p.vec(scans2)

        sr, si = gar, gai

        def smix(v):
            v.tensor_mul(f1[:, FW], cosT[:, FW], zr[:, FW])
            v.tensor_mul(f2[:, FW], sinT[:, FW], zi[:, FW])
            v.tensor_sub(sr[:, FW], f1[:, FW], f2[:, FW])
            v.tensor_mul(f1[:, FW], cosT[:, FW], zi[:, FW])
            v.tensor_mul(f2[:, FW], sinT[:, FW], zr[:, FW])
            v.tensor_add(si[:, FW], f1[:, FW], f2[:, FW])
        p.vec(smix)

        for mcH in range(4):
            ps = p.psum()

            def hfn(t, mcH=mcH, ps=ps):
                for k in range(4):
                    t.matmul(ps[:, 0:W],
                             Vt_bf_r[:, k * 512 + mcH * 128: k * 512 + mcH * 128 + 128],
                             ch(sr, k), start=(k == 0), stop=False)
                last = None
                for k in range(4):
                    last = t.matmul(
                        ps[:, 0:W],
                        Vt_bf_ni[:, k * 512 + mcH * 128: k * 512 + mcH * 128 + 128],
                        ch(si, k), start=False, stop=(k == 3))
                return last
            p.tensor(hfn)
            p.vec(lambda v, mcH=mcH, ps=ps:
                  v.tensor_add(ch(hrt, mcH), ps[:, 0:W],
                               winr[:, mcH * WN + 1: mcH * WN + 1 + W]))

        def wincar(v):
            for c in range(4):
                v.tensor_copy(winr[:, c * WN: c * WN + 1],
                              winr[:, c * WN + W: c * WN + W + 1])
                v.tensor_copy(wini[:, c * WN: c * WN + 1],
                              wini[:, c * WN + W: c * WN + W + 1])
        p.vec(wincar)

        def evac_mo(m, ps):
            p.vec(lambda v, m=m, ps=ps: v.tensor_copy(ch(mo, m), ps[:, 0:W]))
        emit_mm(p, None,
                  lambda k, m: woT[:, k * 1024 + m * 128: k * 1024 + m * 128 + 128],
                  lambda k: ch(hrt, k), 4, 8, W, evac_mo)

        emit_ln(p, lambda c: ch(mo, c), lambda c: ch(x1s, c), W, lnsc,
                  x_full=mo[:, 0:8 * W], out_full=x1s[:, 0:8 * W])

        def resid1(v):
            for c in range(8):
                v.tensor_add(ch(x1s, c), ch(x1s, c), ch(xs, c))
        p.vec(resid1)

        for c in range(8):
            p.dma_s(x1_hbm.ap()[:, c * T + col0: c * T + col0 + W], ch(x1s, c),
                    wait=([(p.el, p.nel)] if c == 0 else None))

    p.el_barrier([(p.qs, p.nqs)])
    p1_stack.close()

    nc.leave_named_scope("p1_mixer", _sc1, False)
    _sc2 = nc.enter_named_scope("p2_ffn", False)[0]
    WF = W2W
    nslab2 = T // WF
    W1t = p.sb("s_W1t", [128, 8 * 4096], BF16)
    w2b = p.sb("t_w2b", [128, 2 * 1024], BF16)
    w2stage = p.sb("t_w2stage", [128, 2 * 1024], BF16)
    b1c = p.sb("s_b1c", [128, 32], F32)
    b2c = p.sb("s_b2c", [128, 8], F32)
    p.dma_s(b1c[:, :], din["b1c"], wait=[(p.el, p.nel)])
    p.dma_s(b2c[:, :], din["b2c"])
    h1 = p.sb("t_h1", [128, 32 * WF], BF16)
    wld = h1[:, 0:2048].bitcast(F32)
    wabs = h1[:, 2048:4096].bitcast(F32)
    accv = p.sb("t_acc", [128, 2], F32)
    sc11 = p.sb("t_sc11", [1, 1], F32)
    scbc = p.sb("t_scbc", [128, 1], F32)
    x2s = lnsc["sq"]
    fo = xn

    def ternarize(dram_ap, nsub, width, out_tile, n_elem, out_hbm=None,
                  stage=None):
        for c in range(nsub):
            p.dma_s(wld[:, 0:width], dram_ap[:, c * width:(c + 1) * width],
                    wait=[(p.el, p.nel)])
            tgt = p.nqs

            def absf(a, c=c):
                a.activation(wabs[:, 0:width], wld[:, 0:width], AF.Abs)
            p.act(absf, wait_dma=[(p.qs, tgt)])

            def red(v, c=c):
                if c == 0:
                    v.reduce_sum(accv[:, 0:1], wabs[:, 0:width], axis=AX)
                else:
                    v.reduce_sum(accv[:, 1:2], wabs[:, 0:width], axis=AX)
                    v.tensor_add(accv[:, 0:1], accv[:, 0:1], accv[:, 1:2])
            p.vec(red)

        def pt(t):
            return t.matmul(p.misc1[0:1, 0:1], accv[:, 0:1], onesc[:, :],
                            start=True, stop=True)
        p.tensor(pt)

        def scale_f(v):
            v.tensor_scalar_mul(sc11[:, :], p.misc1[0:1, 0:1], 1.0 / n_elem)
            v.tensor_scalar_max(sc11[:, :], sc11[:, :], 1e-8)
            v.reciprocal(sc11[:, :], sc11[:, :])
        p.vec(scale_f)

        def bcm(t):
            return t.matmul(p.misc2[:, 0:1], onesr[:, :], sc11[:, :],
                            start=True, stop=True)
        p.tensor(bcm)
        p.vec(lambda v: v.tensor_copy(scbc[:, :], p.misc2[:, 0:1]))

        for c in range(nsub):
            p.dma_s(wld[:, 0:width], dram_ap[:, c * width:(c + 1) * width],
                    wait=[(p.el, p.nel)])
            tgt = p.nqs
            if out_hbm is None:
                dst = out_tile[:, c * width:(c + 1) * width]
            else:
                dst = stage[:, (c % 2) * width:(c % 2 + 1) * width]

            def tern(v, c=c, dst=dst):
                MAGIC = 12582912.0
                v.tensor_scalar_mul(wabs[:, 0:width], wld[:, 0:width], scbc[:, 0:1])
                v.tensor_scalar_min(wabs[:, 0:width], wabs[:, 0:width], 2.5)
                v.tensor_scalar_max(wabs[:, 0:width], wabs[:, 0:width], -2.5)
                v.tensor_scalar_add(wabs[:, 0:width], wabs[:, 0:width], MAGIC)
                v.tensor_scalar_add(wabs[:, 0:width], wabs[:, 0:width], -MAGIC)
                v.tensor_scalar_min(wabs[:, 0:width], wabs[:, 0:width], 1.0)
                v.tensor_scalar_max(wabs[:, 0:width], wabs[:, 0:width], -1.0)
                v.tensor_copy(dst, wabs[:, 0:width])
            p.vec(tern, wait_dma=[(p.qs, tgt)])
            if out_hbm is not None:
                p.dma_s(out_hbm[:, c * width:(c + 1) * width], dst,
                        wait=[(p.el, p.nel)])

    ternarize(din["W1T"], 32, 1024, W1t, 4096 * 1024)
    ternarize(din["W2T"], 32, 1024, None, 4096 * 1024,
              out_hbm=w2t_hbm.ap(), stage=w2stage)

    for s in range(nslab2):
        col0 = s * WF
        for c in range(8):
            p.dma_s(xs[:, c * WF:(c + 1) * WF],
                    x1_hbm.ap()[:, c * T + col0: c * T + col0 + WF],
                    wait=([(p.el, p.nel)] if c == 0 else None))
        tgt = p.nqs
        p.el_barrier([(p.qs, tgt)])

        emit_ln(p, lambda c: xs[:, c * WF:(c + 1) * WF],
                  lambda c: xn[:, c * WF:(c + 1) * WF], WF, lnsc,
                  x_full=xs[:, 0:8 * WF], out_full=xn[:, 0:8 * WF])

        for m in range(32):
            ps = p.psum()

            def m1(t, m=m, ps=ps):
                last = None
                for k in range(8):
                    last = t.matmul(
                        ps[:, 0:WF],
                        W1t[:, k * 4096 + m * 128: k * 4096 + m * 128 + 128],
                        xn[:, k * WF:(k + 1) * WF], start=(k == 0), stop=(k == 7))
                return last
            p.tensor(m1)

            def ge(a, m=m, ps=ps):
                a.activation(h1[:, m * WF:(m + 1) * WF], ps[:, 0:WF], AF.Gelu,
                             bias=b1c[:, m:m + 1])
            p.act(ge)

        psm_all = [p.ps[0], p.ps[1], p.ps[2], p.ps[3],
                   p.pst[0], p.pst[1], p.misc1, p.misc2]
        m2_pe = {}
        for k in range(32):
            buf = w2b[:, (k % 2) * 1024:(k % 2 + 1) * 1024]
            wlk = [(p.qs, p.nqs)]
            if k >= 2:
                wlk.append((p.pe, m2_pe[k - 2]))
            else:
                wlk.append((p.el, p.nel))
            p.dma_s(buf, w2t_hbm.ap()[:, k * 1024:(k + 1) * 1024], wait=wlk)
            wt = p.nqs

            def m2k(t, k=k, buf=buf, wt=wt):
                t.wait_ge(p.qs, wt)
                last = None
                for m in range(8):
                    last = t.matmul(psm_all[m][:, 0:WF],
                                    buf[:, m * 128:(m + 1) * 128],
                                    h1[:, k * WF:(k + 1) * WF],
                                    start=(k == 0), stop=(k == 31))
                return last
            p.tensor(m2k, wait_el=(k == 0))
            m2_pe[k] = p.npe

        def bia(v):
            for m in range(8):
                v.tensor_scalar_add(fo[:, m * WF:(m + 1) * WF],
                                    psm_all[m][:, 0:WF], b2c[:, m:m + 1])
        p.vec(bia)

        emit_ln(p, lambda c: fo[:, c * WF:(c + 1) * WF],
                  lambda c: x2s[:, c * WF:(c + 1) * WF], WF, lnsc,
                  x_full=fo[:, 0:8 * WF], out_full=x2s[:, 0:8 * WF])

        def resid2(v):
            v.tensor_add(x2s[:, 0:8 * WF], x2s[:, 0:8 * WF], xs[:, 0:8 * WF])
        p.vec(resid2)

        for c in range(8):
            p.dma_s(x2T_out[:, c * T + col0: c * T + col0 + WF],
                    x2s[:, c * WF:(c + 1) * WF],
                    wait=([(p.el, p.nel)] if c == 0 else None))

    p.el_barrier([(p.qs, p.nqs), (p.qg, p.nqg)])
    nc.leave_named_scope("p2_ffn", _sc2, False)
    p.stack = outer_stack
    l0_stack.close()
    return None


F32 = mybir.dt.float32
BF16 = mybir.dt.bfloat16
I32 = mybir.dt.int32
AF = mybir.ActivationFunctionType
ALU = mybir.AluOpType
AX = mybir.AxisListType.X

WS = 512
RMS_EPS = 1.1920929e-07
SCALE = 64.0 ** -0.5
LN_EPS_IN = 1e-12


def build_F(T=8192, leaf_iters=26, polish=6):
    TD = T // 4
    nslab = T // WS
    nheads = 16
    ntb = T // 128
    nc = bass.Bass("TRN2", target_bir_lowering=False)
    din = {}
    for nm, shp in AB_INPUTS:
        if shp is None:
            shp = [128, 8 * T]
        din[nm] = nc.dram_tensor(nm, shp, F32, kind="ExternalInput").ap()
    for nm, shp, dt in [
        ("dqc", [128, 4 * 64], F32), ("uqt", [128, 4 * 64], F32),
        ("dkvc", [128, 8 * 64], F32), ("ukvt", [128, 8 * 128], F32),
        ("qrt", [64, 64], F32), ("krt", [64, 64], F32), ("wot", [64, 64], F32),
        ("qnw", [64, 1], F32), ("knw", [64, 1], F32),
        ("masks16", [128, 16 * 512], F32), ("rmask", [128, 4], F32),
        ("wrc", [128, 8 * 4], F32), ("w1e", [128, 4 * 32 * 1024], F32),
        ("b1cD", [128, 4 * 32], F32), ("woc", [128, 32 * 1024], F32),
        ("boc", [128, 8], F32), ("gam", [128, 8], F32), ("bet", [128, 8], F32),
    ]:
        din[nm] = nc.dram_tensor(nm, shp, dt, kind="ExternalInput").ap()
    outT = nc.dram_tensor("outT", [128, 8 * TD], F32, kind="ExternalOutput").ap()
    x2_hbm = nc.dram_tensor("x2_scratch", [128, 8 * T], F32)
    x2own_hbm = nc.dram_tensor("x2own_scratch", [128, 8 * TD], F32)
    w1bf_hbm = nc.dram_tensor("w1bf_scratch", [128, 4 * 32 * 1024], BF16)
    wocbf_hbm = nc.dram_tensor("wocbf_scratch", [128, 32 * 1024], BF16)
    kv_hbm = nc.dram_tensor("kv_scratch", [128, nheads * T], BF16)
    vt_hbm = nc.dram_tensor("vt_scratch", [128, nheads * ntb * 65], BF16)
    ao_hbm = nc.dram_tensor("ao_scratch", [128, 8 * TD], BF16)

    stack = contextlib.ExitStack()
    block = stack.enter_context(nc.Block())
    p = Prog(nc, block, stack)
    p.alloc_psum(n_big=4)

    emit_L0(nc, stack, block, p, din, x2_hbm.ap(), T, leaf_iters, polish)

    srm = p.sb("srmask", [128, 4], F32)
    p.dma_s(srm[:, :], din["rmask"], wait=[(p.el, p.nel)])

    p3 = contextlib.ExitStack()
    p.stack = p3

    def p3sb(name, shape, dtype=F32):
        return p3.enter_context(nc.sbuf_tensor(name, shape, dtype))

    onescb2 = p3sb("onescb2", [128, 1], BF16)
    qnw = p3sb("sqnw", [64, 1], F32)
    knw = p3sb("sknw", [64, 1], F32)
    identf = p3sb("sidentf", [128, 128], F32)
    masks = p3sb("smasks16", [128, 16 * 512], BF16)
    wsm = {}
    for nm, shp in [("dqc", [128, 4 * 64]), ("uqt", [128, 4 * 64]),
                    ("dkvc", [128, 8 * 64]), ("ukvt", [128, 8 * 128]),
                    ("qrt", [64, 64]), ("krt", [64, 64]), ("wot", [64, 64])]:
        wsm[nm] = p3sb("s_" + nm, shp, BF16)
    p.dma_s(qnw[:, :], din["qnw"], wait=[(p.el, p.nel)])
    p.dma_s(knw[:, :], din["knw"])
    p.dma_s(identf[:, :], din["ident"])
    p.dma_g(masks[:, :], din["masks16"], wait=[(p.el, p.nel)])
    for nm in ["dqc", "uqt", "dkvc", "ukvt", "qrt", "krt", "wot"]:
        p.dma_g(wsm[nm][:, :], din[nm])
    nq3s, nq3g = p.nqs, p.nqg

    xs3 = p3sb("t_xs3", [128, 8 * WS], F32)
    xns = p3sb("t_xns", [128, 8 * WS], BF16)
    xh0 = p3sb("t_xh0", [64, WS], BF16)
    kvs = p3sb("t_kvs", [128, WS], F32)
    kpiece = p3sb("t_kp", [128, WS], BF16)
    vtp = p3sb("t_vtp", [128, 4 * 65], BF16)
    rbc = p3sb("t_rbc", [128, WS], F32)
    att = p3sb("t_att", [128, WS], F32)
    attb = p3sb("t_attb", [128, WS], BF16)
    qct = p3sb("t_qct", [64, 64], BF16)
    kvct = p3sb("t_kvct", [64, 128], BF16)
    sonesc3 = p3sb("sonesc3", [128, 1], F32)
    sonesr3 = p3sb("sonesr3", [1, 128], F32)
    lnsc3 = make_ln_scratch(p, WS, tag="3")
    lnsc3["ones_col"] = sonesc3[:, :]
    lnsc3["ones_col_b"] = onescb2[:, :]
    lnsc3["ones_row"] = sonesr3[:, :]
    p.dma_s(lnsc3["ones_col"], din["onesc"])
    p.dma_s(lnsc3["ones_row"], din["onesr"])
    nq3s = p.nqs

    p.el_barrier([(p.qs, nq3s), (p.qg, nq3g)])
    p.vec(lambda v: v.memset(onescb2[:, :], 1.0))

    def qc_fn(t):
        last = None
        for k in range(4):
            last = t.matmul(p.misc1[0:64, 0:64], wsm["dqc"][:, k * 64:(k + 1) * 64],
                            wsm["uqt"][:, k * 64:(k + 1) * 64],
                            start=(k == 0), stop=(k == 3))
        return last
    p.tensor(qc_fn)
    p.vec(lambda v: v.tensor_copy(qct[:, :], p.misc1[0:64, 0:64]))

    def kvc_fn(t):
        last = None
        for k in range(8):
            last = t.matmul(p.misc2[0:64, 0:128], wsm["dkvc"][:, k * 64:(k + 1) * 64],
                            wsm["ukvt"][:, k * 128:(k + 1) * 128],
                            start=(k == 0), stop=(k == 7))
        return last
    p.tensor(kvc_fn)
    p.vec(lambda v: v.tensor_copy(kvct[:, :], p.misc2[0:64, 0:128]))

    _sc3a = nc.enter_named_scope("p3a_kv", False)[0]
    for s in range(nslab):
        col0 = s * WS
        for c in range(8):
            p.dma_s(xs3[:, c * WS:(c + 1) * WS],
                    x2_hbm.ap()[:, c * T + col0: c * T + col0 + WS],
                    wait=([(p.el, p.nel)] if c == 0 else None))
        tgt = p.nqs
        p.el_barrier([(p.qs, tgt)])
        emit_ln(p, lambda c: xs3[:, c * WS:(c + 1) * WS],
                  lambda c: xns[:, c * WS:(c + 1) * WS], WS, lnsc3,
                  x_full=xs3[:, 0:8 * WS], out_full=xns[:, 0:8 * WS])

        for h in range(nheads):
            hc, hr = h // 2, (h % 2) * 64

            def cph(v, hc=hc, hr=hr):
                v.tensor_copy(xh0[:, :], xns[hr:hr + 64, hc * WS:(hc + 1) * WS])
            p.vec(cph, wait_dma=[(p.qg, p.nqg)])

            ps = p.psum()

            def km(t, ps=ps):
                return t.matmul(ps[0:64, 0:WS], kvct[:, 0:64], xh0[:, :],
                                start=True, stop=True)
            p.tensor(km)
            p.vec(lambda v, ps=ps: v.tensor_copy(kvs[0:64, :], ps[0:64, 0:WS]))

            def sqk(a):
                a.activation(att[0:64, :], kvs[0:64, :], AF.Square)
            p.act(sqk)

            def msk(t):
                return t.matmul(p.misc1[0:1, 0:WS], lnsc3["ones_col"][0:64, 0:1],
                                att[0:64, :], start=True, stop=True)
            p.tensor(msk)

            def kr1(v):
                v.tensor_scalar(rbc[0:1, :], p.misc1[0:1, 0:WS], 1.0 / 64,
                                RMS_EPS, ALU.mult, ALU.add)
            p.vec(kr1)

            def kr2(a):
                a.activation(att[0:1, :], rbc[0:1, :], AF.Ln)
                a.activation(rbc[0:1, :], att[0:1, :], AF.Exp, scale=-0.5)
            p.act(kr2)

            def kr3(t):
                return t.matmul(p.misc1[:, 0:WS], lnsc3["ones_row"], rbc[0:1, :],
                                start=True, stop=True)
            p.tensor(kr3)

            def kn(v):
                v.tensor_copy(rbc[:, :], p.misc1[:, 0:WS])
                v.tensor_mul(att[0:64, :], kvs[0:64, :], rbc[0:64, :])
                v.tensor_scalar(kpiece[0:64, :], att[0:64, :], knw[:, 0:1], 1.0,
                                ALU.mult, ALU.mult)
            p.vec(kn)

            ps2 = p.psum()

            def krm(t, ps=ps2):
                return t.matmul(ps[0:64, 0:WS], wsm["krt"][:, :], xh0[:, :],
                                start=True, stop=True)
            p.tensor(krm)
            p.vec(lambda v, ps=ps2: v.tensor_copy(kpiece[64:128, :], ps[0:64, 0:WS]))
            p.dma_g(kv_hbm.ap()[:, h * T + col0: h * T + col0 + WS], kpiece[:, :],
                    wait=[(p.el, p.nel)])

            def vo(v):
                v.memset(vtp[:, :].rearrange("p (b o) -> p b o", o=65)[:, :, 64:65], 1.0)
            p.vec(vo)
            for i in range(4):
                ps3 = p.psum()

                def vm(t, i=i, ps=ps3):
                    return t.matmul(ps[0:128, 0:64], xh0[:, i * 128:(i + 1) * 128],
                                    kvct[:, 64:128], start=True, stop=True)
                p.tensor(vm)
                p.vec(lambda v, i=i, ps=ps3:
                      v.tensor_copy(vtp[:, i * 65: i * 65 + 64], ps[0:128, 0:64]))
            p.dma_g(vt_hbm.ap()[:, h * ntb * 65 + 4 * s * 65: h * ntb * 65 + 4 * s * 65 + 260],
                    vtp[:, :], wait=[(p.el, p.nel)])

    nc.leave_named_scope("p3a_kv", _sc3a, False)
    _sc3b = nc.enter_named_scope("p3b_attn", False)[0]
    oxn = p3sb("t_oxn", [128, 8 * TD], BF16)
    Kh = p3sb("t_Kh", [128, T], BF16)
    vTh = p3sb("t_vTh", [128, ntb * 65], BF16)
    Qown = p3sb("t_Qo", [128, TD], BF16)
    qcs = p3sb("t_qcs", [64, WS], F32)
    rm = p3sb("t_rm", [128, TD // 128 + 1], F32)
    rmrow = p3sb("t_rmrow", [1, 128], F32)
    nmbc = p3sb("t_nmbc", [128, WS], F32)
    ctxb = p3sb("t_ctxb", [64, WS], BF16)
    ao = p3sb("t_ao", [128, 8 * TD], BF16)
    n_own_slab = TD // WS
    n_qt = TD // 128
    n_g = TD // 512

    xg = p3sb("t_xg", [128, 2 * 2048], F32)
    sel_done = {}
    nqs_writes = 0
    for s4 in range(n_own_slab):
        for c in range(8):
            k = s4 * 8 + c
            buf = xg[:, (k % 2) * 2048:(k % 2) * 2048 + 2048]
            src = x2_hbm.ap()[:, c * T + 2048 * s4: c * T + 2048 * s4 + 2048] \
                .rearrange("p (g r i) -> p r g i", g=4, r=4)
            for j in range(4):
                wl0 = None
                if j == 0:
                    wl0 = [(p.qg, p.nqg)]
                    wl0.append((p.el, sel_done[k - 2]) if k >= 2
                               else (p.el, p.nel))
                p.dma_g(buf[:, j * WS:(j + 1) * WS], src[:, j], wait=wl0)
            wt = p.nqg
            wl = [(p.qg, wt)]
            if c == 0 and s4 > 0:
                wl.append((p.qs, nqs_writes))

            def sel(v, c=c, buf=buf):
                d = xs3[:, c * WS:(c + 1) * WS]
                v.tensor_scalar(d, buf[:, 0:WS], srm[:, 0:1], None, ALU.mult)
                for j in range(1, 4):
                    v.scalar_tensor_tensor(d, buf[:, j * WS:(j + 1) * WS],
                                           srm[:, j:j + 1], d, ALU.mult, ALU.add)
            p.vec(sel, wait_dma=wl)
            sel_done[k] = p.nel
        for c in range(8):
            p.dma_s(x2own_hbm.ap()[:, c * TD + s4 * WS: c * TD + s4 * WS + WS],
                    xs3[:, c * WS:(c + 1) * WS],
                    wait=([(p.el, p.nel)] if c == 0 else None))
        nqs_writes = p.nqs
        emit_ln(p, lambda c: xs3[:, c * WS:(c + 1) * WS],
                  lambda c: oxn[:, c * TD + s4 * WS: c * TD + s4 * WS + WS],
                  WS, lnsc3, x_full=xs3[:, 0:8 * WS])

    for i in range(64):
        p.dma_g(w1bf_hbm.ap()[:, i * 2048:(i + 1) * 2048],
                din["w1e"][:, i * 2048:(i + 1) * 2048],
                wait=([(p.el, p.nel)] if i == 0 else None))
    for i in range(16):
        p.dma_g(wocbf_hbm.ap()[:, i * 2048:(i + 1) * 2048],
                din["woc"][:, i * 2048:(i + 1) * 2048])
    nqg_cast = p.nqg

    for h in range(nheads):
        hc, hr = h // 2, (h % 2) * 64
        p.dma_s(Kh[:, :], kv_hbm.ap()[:, h * T:(h + 1) * T],
                wait=[(p.el, p.nel)])
        p.dma_s(vTh[:, :], vt_hbm.ap()[:, h * ntb * 65:(h + 1) * ntb * 65])
        nkv = p.nqs
        p.el_barrier([(p.qs, nkv)])

        for nt in range(n_own_slab):
            sl = slice(nt * WS, (nt + 1) * WS)

            def cpo(v, sl=sl, hc=hc, hr=hr):
                v.tensor_copy(xh0[:, :], oxn[hr:hr + 64, hc * TD + nt * WS: hc * TD + nt * WS + WS])
            p.vec(cpo)
            ps = p.psum()

            def qm(t, ps=ps):
                return t.matmul(ps[0:64, 0:WS], qct[:, :], xh0[:, :],
                                start=True, stop=True)
            p.tensor(qm)
            p.vec(lambda v, ps=ps: v.tensor_copy(qcs[:, :], ps[0:64, 0:WS]))

            def sqq(a):
                a.activation(att[0:64, :], qcs[:, :], AF.Square)
            p.act(sqq)

            def msq(t):
                return t.matmul(p.misc1[0:1, 0:WS], lnsc3["ones_col"][0:64, 0:1],
                                att[0:64, :], start=True, stop=True)
            p.tensor(msq)

            def qr1(v):
                v.tensor_scalar(rbc[0:1, :], p.misc1[0:1, 0:WS], 1.0 / 64,
                                RMS_EPS, ALU.mult, ALU.add)
            p.vec(qr1)

            def qr2(a):
                a.activation(att[0:1, :], rbc[0:1, :], AF.Ln)
                a.activation(rbc[0:1, :], att[0:1, :], AF.Exp, scale=-0.5)
            p.act(qr2)

            def qr3(t):
                return t.matmul(p.misc1[:, 0:WS], lnsc3["ones_row"], rbc[0:1, :],
                                start=True, stop=True)
            p.tensor(qr3)

            def qn(v, sl=sl):
                v.tensor_copy(rbc[:, :], p.misc1[:, 0:WS])
                v.tensor_mul(qcs[:, :], qcs[:, :], rbc[0:64, :])
                v.tensor_scalar(Qown[0:64, sl], qcs[:, :], qnw[:, 0:1], SCALE,
                                ALU.mult, ALU.mult)
            p.vec(qn)
            ps2 = p.psum()

            def qrm(t, ps=ps2, sl=sl):
                return t.matmul(ps[0:64, 0:WS], wsm["qrt"][:, :], Qown[0:64, sl],
                                start=True, stop=True)
            p.tensor(qrm)
            p.vec(lambda v, ps=ps2, sl=sl:
                  v.tensor_copy(Qown[64:128, sl], ps[0:64, 0:WS]))

        for i in range(n_qt):
            for kb in range(i + 1):
                ps = p.psum()

                def sA(t, ps=ps, i=i, kb=kb):
                    return t.matmul(ps[:, 0:WS], Qown[:, i * 128:(i + 1) * 128],
                                    Kh[:, kb * 512:(kb + 1) * 512],
                                    start=True, stop=True)
                p.tensor(sA)

                def mx(v, ps=ps, i=i, kb=kb):
                    if kb == 0:
                        v.reduce_max(rm[:, i:i + 1], ps[:, 0:WS], axis=AX,
                                     negate=True)
                    else:
                        v.reduce_max(rm[:, n_qt:n_qt + 1], ps[:, 0:WS], axis=AX,
                                     negate=True)
                        v.tensor_tensor(rm[:, i:i + 1], rm[:, i:i + 1],
                                        rm[:, n_qt:n_qt + 1], ALU.min)
                p.vec(mx)

        for g in range(n_g):
            for j in range(4):
                def trj(t, g=g, j=j):
                    return t.transpose(p.pst[1][0:1, 0:128],
                                       rm[:, 4 * g + j:4 * g + j + 1], identf[:, :])
                p.tensor(trj)
                p.vec(lambda v: v.tensor_copy(rmrow[:, :], p.pst[1][0:1, 0:128]))

                def bcj(t, j=j):
                    return t.matmul(p.misc1[:, j * 128:(j + 1) * 128],
                                    lnsc3["ones_row"], rmrow[:, :],
                                    start=True, stop=True)
                p.tensor(bcj)
            p.vec(lambda v: v.tensor_copy(nmbc[:, :], p.misc1[:, 0:512]))

            for ck in range(16 * g + 16):
                d = ck - 16 * g
                ps = p.psum()

                def sB(t, ps=ps, g=g, ck=ck):
                    return t.matmul(ps[:, 0:WS], Kh[:, ck * 128:(ck + 1) * 128],
                                    Qown[:, g * 512:(g + 1) * 512],
                                    start=True, stop=True)
                p.tensor(sB)

                def sub(v, ps=ps, d=d):
                    v.tensor_add(att[:, :], ps[:, 0:WS], nmbc[:, :])
                    if d >= 0:
                        v.tensor_add(att[:, :], att[:, :],
                                     masks[:, d * 512:(d + 1) * 512])
                p.vec(sub)

                def ex(a):
                    a.activation(attb[:, :], att[:, :], AF.Exp)
                p.act(ex)

                def av(t, g=g, ck=ck):
                    return t.matmul(p.pst[0][0:65, 0:WS],
                                    vTh[:, ck * 65:(ck + 1) * 65], attb[:, :],
                                    start=(ck == 0), stop=(ck == 16 * g + 15))
                p.tensor(av)

            def nrm(v):
                v.reciprocal(rbc[0:1, :], p.pst[0][64:65, 0:WS])
            p.vec(nrm)

            def nbc(t):
                return t.matmul(p.misc2[0:64, 0:WS], lnsc3["ones_row"][:, 0:64],
                                rbc[0:1, :], start=True, stop=True)
            p.tensor(nbc)

            def cmul(v):
                v.tensor_copy(rbc[0:64, :], p.misc2[0:64, 0:WS])
                v.tensor_mul(att[0:64, :], p.pst[0][0:64, 0:WS], rbc[0:64, :])
                v.tensor_copy(ctxb[:, :], att[0:64, :])
            p.vec(cmul)
            ps5 = p.psum()

            def wof(t, ps=ps5):
                return t.matmul(ps[0:64, 0:WS], wsm["wot"][:, :], ctxb[:, :],
                                start=True, stop=True)
            p.tensor(wof)
            p.vec(lambda v, ps=ps5, g=g, hc=hc, hr=hr:
                  v.tensor_copy(ao[hr:hr + 64, hc * TD + g * 512: hc * TD + g * 512 + WS],
                                ps[0:64, 0:WS]))

    p.el_barrier([])
    for c in range(8):
        p.dma_s(ao_hbm.ap()[:, c * TD:(c + 1) * TD], ao[:, c * TD:(c + 1) * TD],
                wait=[(p.el, p.nel)])
    p.el_barrier([(p.qs, p.nqs)])
    nc.leave_named_scope("p3b_attn", _sc3b, False)
    p.stack = stack
    p3.close()

    _sc4 = nc.enter_named_scope("p4_moe", False)[0]
    emit_moe(nc, stack, block, p, din, outT, x2own_hbm, ao_hbm, TD, T,
             w1bf_hbm, wocbf_hbm, nqg_cast)
    nc.leave_named_scope("p4_moe", _sc4, False)

    @block.sync
    def _(s):
        s.wait_ge(p.qs, p.nqs)

    @block.gpsimd
    def _(g):
        g.wait_ge(p.qg, p.nqg)

    nc._stack = stack
    return nc


def emit_moe(nc, stack, block, p, din, outT, x2own_hbm, ao_hbm, TD, T,
             w1bf_hbm, wocbf_hbm, nqg_cast):
    W = 512
    nslab = TD // W
    p4 = contextlib.ExitStack()
    outer = p.stack
    p.stack = p4

    def sb4(name, shape, dtype=F32):
        return p4.enter_context(nc.sbuf_tensor(name, shape, dtype))

    onesc = sb4("onesc4", [128, 1], F32)
    onesr = sb4("onesr4", [1, 128], F32)
    onescb = sb4("onescb4", [128, 1], BF16)
    wrc = sb4("s_wrc", [128, 8 * 4], BF16)
    b1c = sb4("s_b1cD", [128, 4 * 32], F32)
    woc = sb4("s_woc", [128, 32 * 1024], BF16)
    boc = sb4("s_boc", [128, 8], F32)
    gam = sb4("s_gam", [128, 8], F32)
    bet = sb4("s_bet", [128, 8], F32)
    p.dma_s(onesc[:, :], din["onesc"], wait=[(p.el, p.nel)])
    p.dma_s(onesr[:, :], din["onesr"])
    p.dma_s(b1c[:, :], din["b1cD"])
    p.dma_s(boc[:, :], din["boc"])
    p.dma_s(gam[:, :], din["gam"])
    p.dma_s(bet[:, :], din["bet"])
    p.dma_g(wrc[:, :], din["wrc"], wait=[(p.el, p.nel)])
    p.dma_s(woc[:, :], wocbf_hbm.ap(), wait=[(p.qg, p.nqg), (p.el, p.nel)])
    nq_s, nq_g = p.nqs, p.nqg

    x2s = sb4("t_x2_4", [128, 8 * W], F32)
    ats = sb4("t_at_4", [128, 8 * W], BF16)
    x3s = sb4("t_x3_4", [128, 8 * W], F32)
    xn3 = sb4("t_xn3_4", [128, 8 * W], BF16)
    inter = sb4("t_inter4", [128, 32 * W], BF16)
    hs = sb4("t_hs4", [128, 2 * W], BF16)
    ys = x2s
    lgr = sb4("t_lgr4", [1, 4 * W], F32)
    mxr = sb4("t_mxr4", [1, W], F32)
    mrow = sb4("t_mrow4", [1, 4 * W], F32)
    accm = sb4("t_accm4", [1, W], F32)
    mbc = sb4("t_mbc4", [128, W], BF16)
    w1b = [sb4(f"t_w1b4{i}", [128, 1024], BF16) for i in range(3)]
    lnsc = make_ln_scratch(p, W, tag="4")
    lnsc["ones_col"] = onesc[:, :]
    lnsc["ones_col_b"] = onescb[:, :]
    lnsc["ones_row"] = onesr[:, :]

    def ch(tile, c, w=W):
        return tile[:, c * w:(c + 1) * w]

    p.el_barrier([(p.qs, nq_s), (p.qg, nq_g)])
    p.vec(lambda v: v.memset(onescb[:, :], 1.0))

    for s in range(nslab):
        col0 = s * W
        for c in range(8):
            p.dma_s(ch(x2s, c),
                    x2own_hbm.ap()[:, c * TD + col0: c * TD + col0 + W],
                    wait=([(p.el, p.nel)] if c == 0 else None))
        for c in range(8):
            p.dma_s(ch(ats, c), ao_hbm.ap()[:, c * TD + col0: c * TD + col0 + W],
                    wait=([(p.el, p.nel)] if c == 0 else None))
        p.el_barrier([(p.qs, p.nqs), (p.qg, p.nqg)])

        emit_ln(p, lambda c: ch(ats, c), lambda c: ch(x3s, c), W, lnsc,
                  x_full=ats[:, 0:8 * W], out_full=x3s[:, 0:8 * W])

        def r1(v):
            v.tensor_add(x3s[:, 0:8 * W], x3s[:, 0:8 * W], x2s[:, 0:8 * W])
        p.vec(r1)

        emit_ln(p, lambda c: ch(x3s, c), lambda c: ch(xn3, c), W, lnsc,
                  x_full=x3s[:, 0:8 * W], out_full=xn3[:, 0:8 * W])

        for e in range(4):
            def lg(t, e=e):
                last = None
                for k in range(8):
                    last = t.matmul(p.misc1[0:1, 0:W],
                                    wrc[:, k * 4 + e: k * 4 + e + 1],
                                    ch(xn3, k), start=(k == 0), stop=(k == 7))
                return last
            p.tensor(lg)
            p.vec(lambda v, e=e: v.tensor_copy(lgr[:, e * W:(e + 1) * W],
                                               p.misc1[0:1, 0:W]))

        def am(v):
            v.tensor_tensor(mxr[:, :], lgr[:, 0:W], lgr[:, W:2 * W], ALU.max)
            v.tensor_tensor(mxr[:, :], mxr[:, :], lgr[:, 2 * W:3 * W], ALU.max)
            v.tensor_tensor(mxr[:, :], mxr[:, :], lgr[:, 3 * W:4 * W], ALU.max)
            v.memset(accm[:, :], 0.0)
            for e in range(4):
                sl = slice(e * W, (e + 1) * W)
                v.tensor_tensor(mrow[:, sl], lgr[:, sl], mxr[:, :], ALU.is_ge)
                v.scalar_tensor_tensor(mrow[:, sl], accm[:, :], -1.0,
                                       mrow[:, sl], ALU.mult, ALU.add)
                v.tensor_scalar_max(mrow[:, sl], mrow[:, sl], 0.0)
                v.tensor_add(accm[:, :], accm[:, :], mrow[:, sl])
        p.vec(am)
        ge_done = {}
        nel_preexp = p.nel

        for e in range(4):
            def mb(t, e=e):
                return t.matmul(p.misc2[:, 0:W], onesr[:, :],
                                mrow[:, e * W:(e + 1) * W], start=True, stop=True)
            p.tensor(mb)
            p.vec(lambda v: v.tensor_copy(mbc[:, :], p.misc2[:, 0:W]))

            for i in range(32):
                j = e * 32 + i
                buf = w1b[j % 3]
                wl = [(p.qs, p.nqs), (p.el, ge_done.get(j - 3, nel_preexp))]
                if j == 0 and s == 0:
                    wl = [(p.qg, p.nqg), (p.qs, p.nqs), (p.el, p.nel)]
                p.dma_s(buf[:, :],
                        w1bf_hbm.ap()[:, j * 1024:(j + 1) * 1024], wait=wl)
                wt = p.nqs
                ps = p.psum()
                wl2 = [(p.qs, wt), (p.el, ge_done.get(j - 2, nel_preexp))]

                def mm1(t, buf=buf, ps=ps, wl2=wl2):
                    for sem, cnt in wl2:
                        t.wait_ge(sem, cnt)
                    last = None
                    for k in range(8):
                        last = t.matmul(ps[:, 0:W], buf[:, k * 128:(k + 1) * 128],
                                        ch(xn3, k), start=(k == 0), stop=(k == 7))
                    return last
                p.tensor(mm1, wait_el=False)

                def ge(a, e=e, i=i, ps=ps):
                    a.activation(hs[:, (i % 2) * W:(i % 2 + 1) * W], ps[:, 0:W],
                                 AF.Gelu, bias=b1c[:, e * 32 + i: e * 32 + i + 1])
                p.act(ge)

                def acc(v, e=e, i=i):
                    h1 = hs[:, (i % 2) * W:(i % 2 + 1) * W]
                    if e == 0:
                        v.tensor_mul(ch(inter, i), h1, mbc[:, :])
                    else:
                        v.tensor_mul(h1, h1, mbc[:, :])
                        v.tensor_add(ch(inter, i), ch(inter, i), h1)
                p.vec(acc)
                ge_done[j] = p.nel

        for m in range(8):
            ps = p.psum()

            def mo(t, m=m, ps=ps):
                last = None
                for k in range(32):
                    last = t.matmul(ps[:, 0:W],
                                    woc[:, k * 1024 + m * 128: k * 1024 + m * 128 + 128],
                                    ch(inter, k), start=(k == 0), stop=(k == 31))
                return last
            p.tensor(mo)

            def ye(v, m=m, ps=ps):
                v.tensor_scalar_add(ch(ys, m), ps[:, 0:W], boc[:, m:m + 1])
                v.tensor_add(ch(ys, m), ch(ys, m), ch(xn3, m))
            p.vec(ye)

        y2 = inter[:, 0:16 * W].bitcast(F32)
        emit_ln(p, lambda c: ch(ys, c), lambda c: ch(y2, c), W, lnsc,
                  eps=LN_EPS_IN, x_full=ys[:, 0:8 * W], out_full=y2[:, 0:8 * W])

        def gb(v):
            y3 = y2[:, 0:8 * W].rearrange("p (c w) -> p c w", c=8)
            g3 = gam[:, 0:8].unsqueeze(2).broadcast_to([128, 8, W])
            b3 = bet[:, 0:8].unsqueeze(2).broadcast_to([128, 8, W])
            v.tensor_tensor(y3, y3, g3, ALU.mult)
            v.tensor_tensor(y3, y3, b3, ALU.add)
        p.vec(gb)

        emit_ln(p, lambda c: ch(y2, c), lambda c: ch(y2, c), W, lnsc,
                  x_full=y2[:, 0:8 * W], out_full=y2[:, 0:8 * W])

        def r2(v):
            v.tensor_add(y2[:, 0:8 * W], y2[:, 0:8 * W], x3s[:, 0:8 * W])
        p.vec(r2)

        for c in range(8):
            p.dma_s(outT[:, c * TD + col0: c * TD + col0 + W], ch(y2, c),
                    wait=([(p.el, p.nel)] if c == 0 else None))
    p.stack = outer
    p4.close()



_BUILT = {}


def _get(name, fn):
    if name not in _BUILT:
        _BUILT[name] = fn()
    return _BUILT[name]


def kernel(**inputs):
    d = {k: np.asarray(v) for k, v in inputs.items()}
    x = np.asarray(d["L0_attn_x"], np.float32)
    B, T, DM = x.shape
    ncF = _get("F", lambda: build_F(T=T))
    in_maps = [f_inputs(x[c // 4], d, T, c % 4) for c in range(8)]
    res = run_bass_kernel_spmd(ncF, in_maps, core_ids=list(range(8)))
    outs = [res.results[c]["outT"] for c in range(8)]
    return f_reassemble(outs, T)


# version 05
# speedup vs baseline: 1.3451x; 1.1207x over previous
import sys
import contextlib
import numpy as np
import concourse.bass as bass
import concourse.mybir as mybir
from concourse.bass_utils import run_bass_kernel_spmd



F32 = mybir.dt.float32
BF16 = mybir.dt.bfloat16
I32 = mybir.dt.int32
AF = mybir.ActivationFunctionType
ALU = mybir.AluOpType
AX = mybir.AxisListType.X


class Prog:
    def __init__(self, nc, block, stack):
        self.nc, self.block, self.stack = nc, block, stack
        self.pe = stack.enter_context(nc.semaphore("pe"))
        self.el = stack.enter_context(nc.semaphore("el"))
        self.qs = stack.enter_context(nc.semaphore("qs"))
        self.qg = stack.enter_context(nc.semaphore("qg"))
        self.npe = 0
        self.nel = 0
        self.nqs = 0
        self.nqg = 0
        self.ps = []
        self.ps_small = []

    def alloc_psum(self, n_big=4):
        self.ps = [self.stack.enter_context(
            self.nc.psum_tensor(f"ps{i}", [128, 512], F32)) for i in range(n_big)]
        self.pst = [self.stack.enter_context(
            self.nc.psum_tensor(f"pst{i}", [128, 512], F32)) for i in range(2)]
        self.misc1 = self.stack.enter_context(
            self.nc.psum_tensor("psm1", [128, 512], F32))
        self.misc2 = self.stack.enter_context(
            self.nc.psum_tensor("psm2", [128, 512], F32))
        self._pi = 0
        self.bar = self.stack.enter_context(
            self.nc.sbuf_tensor("elbar", [1, 1], F32))

    def psum(self):
        t = self.ps[self._pi % len(self.ps)]
        self._pi += 1
        return t

    def el_barrier(self, waits):
        @self.block.vector
        def _(v):
            for sem, cnt in waits:
                v.wait_ge(sem, cnt)
            r = v.memset(self.bar[:, :], 0.0)
            r._wait_ge(self.el, self.nel)
            r.then_inc(self.el, 1)
        self.nel += 1

    def sb(self, name, shape, dtype=F32):
        return self.stack.enter_context(self.nc.sbuf_tensor(name, shape, dtype))

    class _E:
        def __init__(self, p, eng, pending):
            object.__setattr__(self, "_p", p)
            object.__setattr__(self, "_e", eng)
            object.__setattr__(self, "_pend", pending)

        def __getattr__(self, name):
            f = getattr(self._e, name)
            if not callable(f) or name.startswith("wait"):
                return f
            p = self._p

            def wrapped(*a, **k):
                for sem, cnt in self._pend:
                    self._e.wait_ge(sem, cnt)
                del self._pend[:]
                r = f(*a, **k)
                r._wait_ge(p.el, p.nel)
                r.then_inc(p.el, 1)
                p.nel += 1
                return r
            return wrapped

    def _esec(self, blockdec, fn, wait_pe, wait_dma):
        pending = []
        if wait_pe:
            pending.append((self.pe, self.npe))
        if wait_dma:
            pending.extend(wait_dma)

        @blockdec
        def _(e):
            fn(Prog._E(self, e, pending))

    def vec(self, fn, wait_pe=True, wait_dma=None):
        self._esec(self.block.vector, fn, wait_pe, wait_dma)

    def act(self, fn, wait_pe=True, wait_dma=None):
        self._esec(self.block.scalar, fn, wait_pe, wait_dma)

    def pool(self, fn, wait_pe=True, wait_dma=None):
        self._esec(self.block.gpsimd, fn, wait_pe, wait_dma)

    def tensor(self, fn, wait_el=True, wait_dma=None, extra_el=None):
        prog = self

        class _T:
            def __init__(self):
                self.first = True
                self.front = []

            def _pre(self):
                if self.first:
                    self.first = False
                    if wait_dma:
                        for sem, cnt in wait_dma:
                            self._t.wait_ge(sem, cnt)
                    if wait_el:
                        self.front.append((prog.el, prog.nel))

            def _att(self, r):
                for sem, cnt in self.front[1:]:
                    pass
                if self.front:
                    r._wait_ge(*self.front[-1])
                    del self.front[:]
                return r

            def matmul(self, *a, **k):
                self._pre()
                for sem, cnt in self.front[:-1]:
                    self._t.wait_ge(sem, cnt)
                    self.front.remove((sem, cnt))
                return self._att(self._t.matmul(*a, **k))

            def transpose(self, *a, **k):
                self._pre()
                for sem, cnt in self.front[:-1]:
                    self._t.wait_ge(sem, cnt)
                    self.front.remove((sem, cnt))
                return self._att(self._t.transpose(*a, **k))

            def wait_ge(self, sem, cnt):
                self.front.append((sem, cnt))
                return None

        @self.block.tensor
        def _(t):
            w = _T()
            w._t = t
            last = fn(w)
            last.then_inc(self.pe, 1)
        self.npe += 1

    def dma_s(self, out_ap, in_ap, wait=None):
        @self.block.sync
        def _(s):
            if wait:
                for sem, cnt in wait[:-1]:
                    s.wait_ge(sem, cnt)
            r = s.dma_start(out=out_ap, in_=in_ap)
            if wait:
                r._wait_ge(*wait[-1])
            r.then_inc(self.qs, 16)
        self.nqs += 16

    def dma_g(self, out_ap, in_ap, wait=None):
        @self.block.gpsimd
        def _(g):
            if wait:
                for sem, cnt in wait[:-1]:
                    g.wait_ge(sem, cnt)
            r = g.dma_start(out=out_ap, in_=in_ap)
            if wait:
                r._wait_ge(*wait[-1])
            r.then_inc(self.qg, 16)
        self.nqg += 16


def emit_ln(p: Prog, x_of, out_of, W, sc, eps=1e-5, nch=8, nfeat=None,
            out_chunks=None, extra_mul=None, x_full=None, out_full=None):
    nfeat = nfeat or nch * 128
    out_chunks = out_chunks if out_chunks is not None else range(nch)

    x_dt = x_of(0).dtype
    sc = dict(sc)
    sc["psA"] = sc["psm1"][0:1, 0:W]
    sc["psB"] = sc["psm2"][0:1, 0:W]
    sq = sc["sq"]
    assert sq.shape[1] == 8 * W, (sq.shape, W)
    sqb = sq[:, 0:4 * W].bitcast(mybir.dt.bfloat16)
    xbb = sq[:, 4 * W:8 * W].bitcast(mybir.dt.bfloat16)

    def sq_fn(a):
        if x_full is not None:
            a.activation(sqb, x_full, AF.Square)
        else:
            for c in range(nch):
                a.activation(sqb[:, c * W:(c + 1) * W], x_of(c), AF.Square)
    p.act(sq_fn)

    if x_dt == mybir.dt.bfloat16:
        xb_of = x_of
    else:
        def cast_fn(v):
            if x_full is not None:
                v.tensor_copy(xbb, x_full)
            else:
                for c in range(nch):
                    v.tensor_copy(xbb[:, c * W:(c + 1) * W], x_of(c))
        p.vec(cast_fn)
        xb_of = lambda c: xbb[:, c * W:(c + 1) * W]

    onesb = sc["ones_col_b"]

    def mm_fn(t):
        for c in range(nch):
            t.matmul(sc["psA"], onesb, sqb[:, c * W:(c + 1) * W],
                     start=(c == 0), stop=(c == nch - 1))
        last = None
        for c in range(nch):
            last = t.matmul(sc["psB"], onesb, xb_of(c),
                            start=(c == 0), stop=(c == nch - 1))
        return last
    p.tensor(mm_fn)

    inv_n = 1.0 / nfeat

    def stats(v):
        v.tensor_scalar_mul(sc["r1"][:, 0:W], sc["psB"], inv_n)
        v.tensor_mul(sc["r2"][:, 0:W], sc["r1"][:, 0:W], sc["r1"][:, 0:W])
        v.tensor_scalar(sc["r3"][:, 0:W], sc["psA"], inv_n, eps,
                        ALU.mult, ALU.add)
        v.tensor_sub(sc["r3"][:, 0:W], sc["r3"][:, 0:W], sc["r2"][:, 0:W])
    p.vec(stats)

    def rstd(a):
        a.activation(sc["r2"][:, 0:W], sc["r3"][:, 0:W], AF.Ln)
        a.activation(sc["r3"][:, 0:W], sc["r2"][:, 0:W], AF.Exp, scale=-0.5)
    p.act(rstd)

    p.vec(lambda v: v.tensor_mul(sc["r1"][:, 0:W], sc["r1"][:, 0:W],
                                 sc["r3"][:, 0:W]))

    psC1 = p.psum()
    psC2 = p.psum()

    def bc(t):
        t.matmul(psC1[:, 0:W], sc["ones_row"], sc["r3"][:, 0:W],
                 start=True, stop=True)
        return t.matmul(psC2[:, 0:W], sc["ones_row"], sc["r1"][:, 0:W],
                        start=True, stop=True)
    p.tensor(bc)

    def evacbc(v):
        v.tensor_copy(sc["b1"][:, 0:W], psC1[:, 0:W])
        v.tensor_copy(sc["b2"][:, 0:W], psC2[:, 0:W])
        if x_full is not None and out_full is not None:
            x3 = x_full.rearrange("p (c w) -> p c w", c=nch)
            o3 = out_full.rearrange("p (c w) -> p c w", c=nch)
            t3 = sq[:, :].rearrange("p (c w) -> p c w", c=nch)
            b1b = sc["b1"][:, 0:W].unsqueeze(1).broadcast_to([128, nch, W])
            b2b = sc["b2"][:, 0:W].unsqueeze(1).broadcast_to([128, nch, W])
            v.tensor_tensor(t3, x3, b1b, ALU.mult)
            v.tensor_tensor(o3, t3, b2b, ALU.subtract)
        else:
            for c in out_chunks:
                v.tensor_mul(sq[:, c * W:(c + 1) * W], x_of(c), sc["b1"][:, 0:W])
                v.tensor_sub(out_of(c), sq[:, c * W:(c + 1) * W], sc["b2"][:, 0:W])
    p.vec(evacbc)


def make_ln_scratch(p: Prog, W, tag=""):
    sc = {}
    sc["sq"] = p.sb("lnsq" + tag, [128, 8 * W], F32)[:, :]
    sc["r1"] = p.sb("lnr1" + tag, [1, W], F32)[:, :]
    sc["r2"] = p.sb("lnr2" + tag, [1, W], F32)[:, :]
    sc["r3"] = p.sb("lnr3" + tag, [1, W], F32)[:, :]
    sc["b1"] = p.sb("lnb1" + tag, [128, W], F32)[:, :]
    sc["b2"] = p.sb("lnb2" + tag, [128, W], F32)[:, :]
    sc["psm1"] = p.misc1
    sc["psm2"] = p.misc2
    return sc


def emit_mm(p: Prog, out_of, lhsT_of, rhs_of, kch, mch, W, evac,
            nsplit=1):
    nel_ev = {-2: p.nel, -1: p.nel}
    for m in range(mch):
        ps = p.psum()

        def mm_fn(t, m=m, ps=ps):
            t.wait_ge(p.el, nel_ev[m - 2])
            last = None
            for k in range(kch):
                last = t.matmul(ps[:, 0:W], lhsT_of(k, m), rhs_of(k),
                                start=(k == 0), stop=(k == kch - 1))
            return last
        p.tensor(mm_fn, wait_el=False)
        evac(m, ps)
        nel_ev[m] = p.nel


F32 = mybir.dt.float32
BF16 = mybir.dt.bfloat16
ALU = mybir.AluOpType


class CM:

    def __init__(self, re, im, nim=None, n=0):
        self.re, self.im, self.nim, self.n = re, im, nim, n


class Inv:
    def __init__(self, nc, block, stack, ident128_f32, psum_from=None):
        self.nc, self.block, self.stack = nc, block, stack
        self.ident = ident128_f32
        self.pe = stack.enter_context(nc.semaphore("ipe"))
        self.dv = stack.enter_context(nc.semaphore("idv"))
        self.npe = 0
        self.ndv = 0
        if psum_from is None:
            self.ps = [stack.enter_context(nc.psum_tensor(f"ips{i}", [128, 512], F32))
                       for i in range(4)]
            self.pst = [stack.enter_context(nc.psum_tensor(f"ipst{i}", [128, 512], F32))
                        for i in range(2)]
        else:
            self.ps = list(psum_from.ps)
            self.pst = list(psum_from.pst)
        self._pi = 0

    def alloc(self, name, n, nim=False):
        nch = max(1, n // 128)
        re = self.stack.enter_context(self.nc.sbuf_tensor(name + "r", [128, nch * n], F32))
        im = self.stack.enter_context(self.nc.sbuf_tensor(name + "i", [128, nch * n], F32))
        ni = self.stack.enter_context(self.nc.sbuf_tensor(name + "n", [128, nch * n], F32)) if nim else None
        return CM(re, im, ni, n)

    def _psum2(self):
        a = self.ps[self._pi % 4]
        b = self.ps[(self._pi + 1) % 4]
        self._pi += 2
        return a, b

    class _SyncV:

        def __init__(self, inv, v, pe_wait):
            object.__setattr__(self, "_inv", inv)
            object.__setattr__(self, "_v", v)
            object.__setattr__(self, "_pw", [pe_wait])

        def __getattr__(self, name):
            f = getattr(self._v, name)
            if not callable(f) or name.startswith("wait"):
                return f
            inv = self._inv

            def wrapped(*a, **k):
                if self._pw[0] is not None:
                    self._v.wait_ge(inv.pe, self._pw[0])
                    self._pw[0] = None
                r = f(*a, **k)
                r._wait_ge(inv.dv, inv.ndv)
                r.then_inc(inv.dv, 1)
                inv.ndv += 1
                return r
            return wrapped

    def dve(self, fn, k=None):
        @self.block.vector
        def _(v):
            fn(Inv._SyncV(self, v, self.npe))

    def make_nim(self, M: CM):
        self.dve(lambda v: v.tensor_scalar_mul(M.nim[:, :], M.im[:, :], -1.0))

    def cmm(self, out: CM, A: CM, B: CM, n, a_r0=0, a_c0=0, b_r0=0, b_c0=0,
            o_r0=0, o_c0=0, post=None, nA=None, nB=None, nO=None):
        nA = nA or A.n
        nB = nB or B.n
        nO = nO or out.n
        nch = n // 128
        for mc in range(nch):
            psr, psi = self._psum2()
            m0 = a_c0 + mc * 128

            @self.block.tensor
            def _(t, psr=psr, psi=psi, m0=m0):
                first = [True]
                _mm = t.matmul

                def matmul_w(*a, **k):
                    r = _mm(*a, **k)
                    if first[0]:
                        first[0] = False
                        r._wait_ge(self.dv, self.ndv)
                    return r
                t = type("TW", (), {"matmul": staticmethod(matmul_w)})()

                def aps(kc):
                    ac = (a_r0 + kc * 128) // 128
                    bc = (b_r0 + kc * 128) // 128
                    return (A.re[:, ac * nA + m0: ac * nA + m0 + 128],
                            A.im[:, ac * nA + m0: ac * nA + m0 + 128],
                            A.nim[:, ac * nA + m0: ac * nA + m0 + 128],
                            B.re[:, bc * nB + b_c0: bc * nB + b_c0 + n],
                            B.im[:, bc * nB + b_c0: bc * nB + b_c0 + n])
                last = None
                for kc in range(nch):
                    a_re, a_im, a_ni, b_re, b_im = aps(kc)
                    t.matmul(psr[:, 0:n], a_re, b_re, start=(kc == 0), stop=False)
                for kc in range(nch):
                    a_re, a_im, a_ni, b_re, b_im = aps(kc)
                    t.matmul(psr[:, 0:n], a_ni, b_im, start=False,
                             stop=(kc == nch - 1))
                for kc in range(nch):
                    a_re, a_im, a_ni, b_re, b_im = aps(kc)
                    t.matmul(psi[:, 0:n], a_re, b_im, start=(kc == 0), stop=False)
                for kc in range(nch):
                    a_re, a_im, a_ni, b_re, b_im = aps(kc)
                    last = t.matmul(psi[:, 0:n], a_im, b_re, start=False,
                                    stop=(kc == nch - 1))
                last.then_inc(self.pe, 1)
            self.npe += 1

            oc = (o_r0 // 128) + mc
            o_re = out.re[:, oc * nO + o_c0: oc * nO + o_c0 + n]
            o_im = out.im[:, oc * nO + o_c0: oc * nO + o_c0 + n]

            if post is None:
                def post_(v, psr, psi, o_re, o_im, mc):
                    v.tensor_copy(o_re, psr)
                    return v.tensor_copy(o_im, psi)
            else:
                post_ = post
            self.dve(lambda v, psr=psr, psi=psi, o_re=o_re, o_im=o_im, mc=mc:
                     post_(v, psr[:, 0:n], psi[:, 0:n], o_re, o_im, mc))

    def ctrans(self, out: CM, src: CM, n, s_r0=0, s_c0=0, o_r0=0, o_c0=0,
               nS=None, nO=None, negate_to_nim=True):
        nS = nS or src.n
        nO = nO or out.n
        nch = n // 128
        for rc in range(nch):
            for cc in range(nch):
                sc = (s_r0 + rc * 128) // 128
                s_re = src.re[:, sc * nS + s_c0 + cc * 128: sc * nS + s_c0 + cc * 128 + 128]
                s_im = src.im[:, sc * nS + s_c0 + cc * 128: sc * nS + s_c0 + cc * 128 + 128]
                pr = self.pst[0]
                pi = self.pst[1]

                @self.block.tensor
                def _(t, s_re=s_re, s_im=s_im, pr=pr, pi=pi):
                    r = t.transpose(pr[:, 0:128], s_re, self.ident)
                    r._wait_ge(self.dv, self.ndv)
                    t.transpose(pi[:, 0:128], s_im, self.ident).then_inc(self.pe, 1)
                self.npe += 1

                oc = (o_r0 + cc * 128) // 128
                o_re = out.re[:, oc * nO + o_c0 + rc * 128: oc * nO + o_c0 + rc * 128 + 128]
                o_im = out.im[:, oc * nO + o_c0 + rc * 128: oc * nO + o_c0 + rc * 128 + 128]
                if negate_to_nim and out.nim is not None:
                    o_ni = out.nim[:, oc * nO + o_c0 + rc * 128: oc * nO + o_c0 + rc * 128 + 128]

                    def post(v, o_re=o_re, o_im=o_im, o_ni=o_ni, pr=pr, pi=pi):
                        v.tensor_copy(o_re, pr[:, 0:128])
                        v.tensor_copy(o_im, pi[:, 0:128])
                        return v.tensor_scalar_mul(o_ni, pi[:, 0:128], -1.0)
                else:
                    def post(v, o_re=o_re, o_im=o_im, pr=pr, pi=pi):
                        v.tensor_copy(o_re, pr[:, 0:128])
                        return v.tensor_copy(o_im, pi[:, 0:128])
                self.dve(post)

    def newton(self, Wt: CM, X: CM, Xs: CM, Xt: CM, M: CM, twoI,
               n, iters, w_r0=0, alpha_bc=None):
        nch = n // 128
        if alpha_bc is not None:
            for c in range(nch):
                wc = (w_r0 + c * 128) // 128
                src_re = Wt.re[:, wc * Wt.n + w_r0: wc * Wt.n + w_r0 + n]
                src_im = Wt.im[:, wc * Wt.n + w_r0: wc * Wt.n + w_r0 + n]
                dst_re = X.re[:, c * n: c * n + n]
                dst_im = X.im[:, c * n: c * n + n]

                def init(v, src_re=src_re, src_im=src_im, dst_re=dst_re, dst_im=dst_im):
                    v.tensor_scalar_mul(dst_re, src_re, alpha_bc)
                    return v.tensor_scalar(dst_im, src_im, alpha_bc, -1.0,
                                           ALU.mult, ALU.mult)
                self.dve(init)

        cur, other = X, Xs
        for it in range(iters):
            def post_m(v, psr, psi, o_re, o_im, mc):
                v.scalar_tensor_tensor(o_re, psr, -1.0,
                                       twoI[:, mc * n:(mc + 1) * n], ALU.mult, ALU.add)
                return v.tensor_scalar_mul(o_im, psi, -1.0)
            self.cmm(M, Wt, cur, n, a_r0=w_r0, a_c0=w_r0, post=post_m,
                     nA=Wt.n, nB=n, nO=n)
            self.ctrans(Xt, cur, n, nS=n, nO=n, negate_to_nim=True)
            self.cmm(other, Xt, M, n, nA=n, nB=n, nO=n)
            cur, other = other, cur
        return cur


def emit_alpha(inv: Inv, M: CM, r0, c0, n, nM, alpha_bc, scratch, ones_col,
               ones_row, tiny11, ps11, ps128):
    nch = n // 128
    acc = scratch

    def emit(v):
        first = True
        last = None
        for plane in (M.re, M.im):
            for c in range(nch):
                mc = (r0 + c * 128) // 128
                blk = plane[:, mc * nM + c0: mc * nM + c0 + n]
                v.tensor_mul(scratch[:, 2:2 + n], blk, blk)
                if first:
                    last = v.reduce_sum(scratch[:, 0:1], scratch[:, 2:2 + n],
                                        axis=mybir.AxisListType.X)
                    first = False
                else:
                    v.reduce_sum(scratch[:, 1:2], scratch[:, 2:2 + n],
                                 axis=mybir.AxisListType.X)
                    last = v.tensor_add(scratch[:, 0:1], scratch[:, 0:1],
                                        scratch[:, 1:2])
        return last
    inv.dve(emit)

    @inv.block.tensor
    def _(t):
        r = t.matmul(ps11[:, :], scratch[:, 0:1], ones_col, start=True, stop=True)
        r._wait_ge(inv.dv, inv.ndv)
        r.then_inc(inv.pe, 1)
    inv.npe += 1

    def emit2(v):
        return v.reciprocal(tiny11[:, :], ps11[:, :])
    inv.dve(emit2)

    @inv.block.tensor
    def _(t):
        r = t.matmul(ps128[:, :], ones_row, tiny11[:, :], start=True, stop=True)
        r._wait_ge(inv.dv, inv.ndv)
        r.then_inc(inv.pe, 1)
    inv.npe += 1

    inv.dve(lambda v: v.tensor_copy(alpha_bc, ps128[:, :]))


def emit_inv256(inv: Inv, N: CM, Nt: CM, r0, out: CM, outT: CM, bufs, consts,
                leaf_iters=30):
    XA, XS = bufs["X128a"], bufs["X128b"]
    Xs, Xt, M128 = bufs["Xs128"], bufs["Xt128"], bufs["M128"]
    P_, Pt, Q, S, St, SiQ = bufs["P128"], bufs["Pt128"], bufs["Q128"], \
        bufs["S128"], bufs["St128"], bufs["SiQ128"]
    a0, d0 = r0, r0 + 128

    emit_alpha(inv, Nt, a0, a0, 128, Nt.n, bufs["alpha"], bufs["scr"],
               consts["ones_col"], consts["ones_row"], bufs["t11"],
               bufs["ps11"], bufs["ps128"])
    Ai = inv.newton(Nt, XA, Xs, Xt, M128, consts["twoI128"], 128, leaf_iters,
                    w_r0=a0, alpha_bc=bufs["alpha"][:, 0:1])
    if Ai is not XA:
        inv.dve(lambda v: (v.tensor_copy(XA.re[:, :], Ai.re[:, :]),
                           v.tensor_copy(XA.im[:, :], Ai.im[:, :]))[-1])
    AiT = bufs["AiT128"]
    inv.ctrans(AiT, XA, 128)
    inv.cmm(P_, AiT, N, 128, b_r0=a0, b_c0=d0, nB=N.n)
    Dre = lambda mc: N.re[:, (d0 // 128) * N.n + d0: (d0 // 128) * N.n + d0 + 128]
    Dim = lambda mc: N.im[:, (d0 // 128) * N.n + d0: (d0 // 128) * N.n + d0 + 128]

    def post_s(v, psr, psi, o_re, o_im, mc):
        v.scalar_tensor_tensor(o_re, psr, -1.0, Dre(mc), ALU.mult, ALU.add)
        return v.scalar_tensor_tensor(o_im, psi, -1.0, Dim(mc), ALU.mult, ALU.add)
    inv.cmm(S, Nt, P_, 128, a_r0=a0, a_c0=d0, post=post_s, nA=Nt.n, nB=128, nO=128)
    inv.ctrans(St, S, 128)
    emit_alpha(inv, St, 0, 0, 128, 128, bufs["alpha"], bufs["scr"],
               consts["ones_col"], consts["ones_row"], bufs["t11"],
               bufs["ps11"], bufs["ps128"])
    Si = inv.newton(St, XS, Xs, Xt, M128, consts["twoI128"], 128, leaf_iters,
                    w_r0=0, alpha_bc=bufs["alpha"][:, 0:1])
    if Si is not XS:
        inv.dve(lambda v: (v.tensor_copy(XS.re[:, :], Si.re[:, :]),
                           v.tensor_copy(XS.im[:, :], Si.im[:, :]))[-1])
    SiT = bufs["SiT128"]
    inv.ctrans(SiT, XS, 128)
    inv.cmm(Q, Nt, XA, 128, a_r0=a0, a_c0=d0, nA=Nt.n, nB=128, nO=128)
    inv.cmm(SiQ, SiT, Q, 128, nA=128, nB=128, nO=128)
    inv.ctrans(Pt, P_, 128)
    def post_tl(v, psr, psi, o_re, o_im, mc):
        v.tensor_add(o_re, psr, XA.re[:, 0:128])
        return v.tensor_add(o_im, psi, XA.im[:, 0:128])
    inv.cmm(out, Pt, SiQ, 128, o_r0=0, o_c0=0, post=post_tl, nA=128, nB=128)
    def post_neg(v, psr, psi, o_re, o_im, mc):
        v.tensor_scalar_mul(o_re, psr, -1.0)
        return v.tensor_scalar_mul(o_im, psi, -1.0)
    inv.cmm(out, Pt, XS, 128, o_r0=0, o_c0=128, post=post_neg, nA=128, nB=128)
    oc = 1
    def emit_bl(v):
        v.tensor_scalar_mul(out.re[:, oc * out.n + 0: oc * out.n + 128], SiQ.re[:, :], -1.0)
        v.tensor_scalar_mul(out.im[:, oc * out.n + 0: oc * out.n + 128], SiQ.im[:, :], -1.0)
        v.tensor_copy(out.re[:, oc * out.n + 128: oc * out.n + 256], XS.re[:, :])
        return v.tensor_copy(out.im[:, oc * out.n + 128: oc * out.n + 256], XS.im[:, :])
    inv.dve(emit_bl)
    inv.ctrans(outT, out, 256)


def emit_invert512(inv: Inv, W: CM, Wt: CM, Wi: CM, bufs, consts,
                   leaf_iters=30, polish=6):
    Ai, AiT = bufs["Ai256"], bufs["AiT256"]
    emit_inv256(inv, W, Wt, 0, Ai, AiT, bufs, consts, leaf_iters)
    P_, Pt, Q, S, St, SiQ = bufs["P256"], bufs["Pt256"], bufs["Q256"], \
        bufs["S256"], bufs["St256"], bufs["SiQ256"]
    inv.cmm(P_, AiT, W, 256, b_r0=0, b_c0=256, nB=W.n)
    def post_s(v, psr, psi, o_re, o_im, mc):
        dc = (256 + mc * 128) // 128
        v.scalar_tensor_tensor(o_re, psr, -1.0,
                               W.re[:, dc * W.n + 256: dc * W.n + 512], ALU.mult, ALU.add)
        return v.scalar_tensor_tensor(o_im, psi, -1.0,
                                      W.im[:, dc * W.n + 256: dc * W.n + 512], ALU.mult, ALU.add)
    inv.cmm(S, Wt, P_, 256, a_r0=0, a_c0=256, post=post_s, nA=Wt.n, nB=256, nO=256)
    inv.ctrans(St, S, 256)
    Si, SiT = bufs["Si256"], bufs["SiT256"]
    emit_inv256(inv, S, St, 0, Si, SiT, bufs, consts, leaf_iters)
    inv.cmm(Q, Wt, Ai, 256, a_r0=0, a_c0=256, nA=Wt.n, nB=256, nO=256)
    inv.cmm(SiQ, SiT, Q, 256, nA=256, nB=256, nO=256)
    inv.ctrans(Pt, P_, 256)
    def post_tl(v, psr, psi, o_re, o_im, mc):
        v.tensor_add(o_re, psr, Ai.re[:, mc * 256: mc * 256 + 256])
        return v.tensor_add(o_im, psi, Ai.im[:, mc * 256: mc * 256 + 256])
    inv.cmm(Wi, Pt, SiQ, 256, o_r0=0, o_c0=0, post=post_tl, nA=256, nB=256)
    def post_neg(v, psr, psi, o_re, o_im, mc):
        v.tensor_scalar_mul(o_re, psr, -1.0)
        return v.tensor_scalar_mul(o_im, psi, -1.0)
    inv.cmm(Wi, Pt, Si, 256, o_r0=0, o_c0=256, post=post_neg, nA=256, nB=256)
    def emit_b(v):
        last = None
        for mc in range(2):
            oc = 2 + mc
            last = v.tensor_scalar_mul(Wi.re[:, oc * Wi.n + 0: oc * Wi.n + 256],
                                       SiQ.re[:, mc * 256: mc * 256 + 256], -1.0)
            last = v.tensor_scalar_mul(Wi.im[:, oc * Wi.n + 0: oc * Wi.n + 256],
                                       SiQ.im[:, mc * 256: mc * 256 + 256], -1.0)
            last = v.tensor_copy(Wi.re[:, oc * Wi.n + 256: oc * Wi.n + 512],
                                 Si.re[:, mc * 256: mc * 256 + 256])
            last = v.tensor_copy(Wi.im[:, oc * Wi.n + 256: oc * Wi.n + 512],
                                 Si.im[:, mc * 256: mc * 256 + 256])
        return last
    inv.dve(emit_b)
    if polish > 0:
        res = inv.newton(Wt, Wi, bufs["Xs512"], bufs["Xt512"], bufs["M512"],
                         consts["twoI512"], 512, polish, w_r0=0, alpha_bc=None)
        if res is not Wi:
            inv.dve(lambda v: (v.tensor_copy(Wi.re[:, :], res.re[:, :]),
                               v.tensor_copy(Wi.im[:, :], res.im[:, :]))[-1])


F32 = np.float32


def chunked(M):
    r, c = M.shape
    nch = r // 128
    out = np.empty((128, nch * c), M.dtype)
    for k in range(nch):
        out[:, k * c:(k + 1) * c] = M[k * 128:(k + 1) * 128, :]
    return np.ascontiguousarray(out)


def unchunk(T, rows, cols):
    nch = rows // 128
    M = np.empty((rows, cols), T.dtype)
    for k in range(nch):
        M[k * 128:(k + 1) * 128, :] = T[:, k * cols:(k + 1) * cols]
    return M


def twoI_tile(n):
    nch = n // 128
    t = np.zeros((128, nch * n), np.float32)
    for c in range(nch):
        for p in range(128):
            t[p, c * n + c * 128 + p] = 2.0
    return t


def vec_chunk(v):
    n = v.shape[0]
    return np.ascontiguousarray(v.reshape(n // 128, 128).T.astype(v.dtype))


PERM_REIM = np.concatenate([np.arange(0, 1024, 2), np.arange(1, 1024, 2)])


def ab_inputs(xb, d, T):
    f = lambda a: np.ascontiguousarray(np.asarray(a, np.float32))
    w_a, w_x = f(d["L0_attn_w_a"]), f(d["L0_attn_w_x"])
    V_r, V_i = f(d["L0_attn_mat_v_real"]), f(d["L0_attn_mat_v_imag"])
    out = dict(
        xT=chunked(f(xb).T.copy()),
        Vt_r=chunked(V_r.T.copy()), Vt_i=chunked(V_i.T.copy()),
        Vn_r=chunked(V_r), Vn_i=chunked(V_i),
        waT=chunked(np.ascontiguousarray(w_a.T[:, PERM_REIM])),
        wxT=chunked(np.ascontiguousarray(w_x.T[:, PERM_REIM])),
        woT=chunked(f(d["L0_attn_w_out"]).T.copy()),
        h0r=vec_chunk(f(d["L0_attn_last_hidden_real"])),
        h0i=vec_chunk(f(d["L0_attn_last_hidden_imag"])),
        W1T=chunked(f(d["L0_ffn_W1"]).T.copy()),
        b1c=vec_chunk(f(d["L0_ffn_b1"])),
        W2T=chunked(f(d["L0_ffn_W2"]).T.copy()),
        b2c=vec_chunk(f(d["L0_ffn_b2"])),
        ident=np.eye(128, dtype=np.float32),
        twoI512=twoI_tile(512), twoI128=twoI_tile(128),
        onesc=np.ones((128, 1), np.float32), onesr=np.ones((1, 128), np.float32),
        onesW=np.ones((128, 512), np.float32),
    )
    return out


def c_inputs(x2b_perm, d, T):
    f = lambda a: np.ascontiguousarray(np.asarray(a, np.float32))
    masks = np.zeros((128, 4 * 512), np.float32)
    ii = np.arange(128)[:, None]
    jj = np.arange(512)[None, :]
    for p in range(4):
        masks[:, p * 512:(p + 1) * 512] = np.where(ii > jj - 128 * p, -1e30, 0.0)
    return dict(
        x2T=chunked(f(x2b_perm).T.copy()),
        dqc=chunked(f(d["L1_attn_W_DQ"])),
        uqt=chunked(f(d["L1_attn_W_UQ"]).T.copy()),
        dkvc=chunked(f(d["L1_attn_W_DKV"])),
        ukvt=chunked(f(d["L1_attn_W_UKV"]).T.copy()),
        qrt=f(d["L1_attn_W_QR"]).T.copy(),
        krt=f(d["L1_attn_W_KR"]).T.copy(),
        wot=f(d["L1_attn_W_O"]).T.copy(),
        qnw=f(d["L1_attn_q_norm_w"]).reshape(64, 1),
        knw=f(d["L1_attn_k_norm_w"]).reshape(64, 1),
        identb=np.eye(64, dtype=np.float32),
        identf=np.eye(128, dtype=np.float32),
        masks=masks,
        onesc=np.ones((128, 1), np.float32), onesr=np.ones((1, 128), np.float32),
    )


def head_perm(j):
    mine = np.arange(256 * j, 256 * j + 256)
    rest = np.concatenate([np.arange(0, 256 * j), np.arange(256 * j + 256, 1024)])
    return np.concatenate([mine, rest])


def f_inputs(xb, d, T, r):
    f = lambda a: np.ascontiguousarray(np.asarray(a, np.float32))
    out = ab_inputs(xb, d, T)
    W1 = f(d["L1_ffn_W1"])
    w1e = np.empty((128, 4 * 32 * 1024), np.float32)
    for e in range(4):
        ck = chunked(W1[e])
        for i in range(32):
            for k in range(8):
                w1e[:, (e * 32 + i) * 1024 + k * 128:(e * 32 + i) * 1024 + (k + 1) * 128] = \
                    ck[:, k * 4096 + i * 128: k * 4096 + (i + 1) * 128]
    masks = np.zeros((128, 16 * 512), np.float32)
    ii = np.arange(128)[:, None]
    jj = np.arange(512)[None, :]
    qpos = 128 * (4 * (jj // 128) + r) + (jj % 128)
    for dd in range(16):
        masks[:, dd * 512:(dd + 1) * 512] = np.where(128 * dd + ii <= qpos, 0.0, -1e30)
    out.update(
        dqc=chunked(f(d["L1_attn_W_DQ"])),
        uqt=chunked(f(d["L1_attn_W_UQ"]).T.copy()),
        dkvc=chunked(f(d["L1_attn_W_DKV"])),
        ukvt=chunked(f(d["L1_attn_W_UKV"]).T.copy()),
        qrt=f(d["L1_attn_W_QR"]).T.copy(),
        krt=f(d["L1_attn_W_KR"]).T.copy(),
        wot=f(d["L1_attn_W_O"]).T.copy(),
        qnw=f(d["L1_attn_q_norm_w"]).reshape(64, 1),
        knw=f(d["L1_attn_k_norm_w"]).reshape(64, 1),
        masks16=masks,
        rmask=np.ascontiguousarray(
            np.broadcast_to((np.arange(4) == r).astype(np.float32), (128, 4))),
        wrc=chunked(f(d["L1_ffn_W_router"])),
        w1e=w1e,
        b1cD=np.ascontiguousarray(f(d["L1_ffn_b1"]).reshape(4, 32, 128)
                                  .transpose(2, 0, 1).reshape(128, 128)),
        woc=chunked(f(d["L1_ffn_Wo"])),
        boc=vec_chunk(f(d["L1_ffn_bo"])),
        gam=vec_chunk(f(d["L1_ffn_ln_gamma"])),
        bet=vec_chunk(f(d["L1_ffn_ln_beta"])),
    )
    return out


def f_reassemble(outs, T):
    TD = T // 4
    res = np.empty((2, T, 1024), np.float32)
    for c in range(8):
        b, r = c // 4, c % 4
        o = unchunk(outs[c], 1024, TD)
        for blk in range(TD // 128):
            g = 4 * blk + r
            res[b, 128 * g:128 * g + 128, :] = o[:, 128 * blk:128 * blk + 128].T
    return res




F32 = mybir.dt.float32
BF16 = mybir.dt.bfloat16
I32 = mybir.dt.int32
AF = mybir.ActivationFunctionType
ALU = mybir.AluOpType
AX = mybir.AxisListType.X
PI = float(np.pi)

W = 512
W2W = 512


AB_INPUTS = [
    ("xT", None), ("Vt_r", [128, 2048]), ("Vt_i", [128, 2048]),
    ("Vn_r", [128, 2048]), ("Vn_i", [128, 2048]),
    ("waT", [128, 8 * 1024]), ("wxT", [128, 8 * 1024]),
    ("woT", [128, 4 * 1024]), ("h0r", [128, 4]), ("h0i", [128, 4]),
    ("W1T", [128, 8 * 4096]), ("b1c", [128, 32]),
    ("W2T", [128, 32 * 1024]), ("b2c", [128, 8]),
    ("ident", [128, 128]), ("twoI512", [128, 2048]), ("twoI128", [128, 128]),
    ("onesc", [128, 1]), ("onesr", [1, 128]), ("onesW", [128, 512]),
]


def build_AB(T=8192, leaf_iters=26, polish=6):
    nc = bass.Bass("TRN2", target_bir_lowering=False)
    din = {}
    for nm, shp in AB_INPUTS:
        if shp is None:
            shp = [128, 8 * T]
        din[nm] = nc.dram_tensor(nm, shp, F32, kind="ExternalInput").ap()
    x2T_out = nc.dram_tensor("x2T", [128, 8 * T], F32, kind="ExternalOutput").ap()
    stack = contextlib.ExitStack()
    block = stack.enter_context(nc.Block())
    p = Prog(nc, block, stack)
    p.alloc_psum(n_big=4)
    emit_L0(nc, stack, block, p, din, x2T_out, T, leaf_iters, polish)

    @block.sync
    def _(s):
        s.wait_ge(p.qs, p.nqs)

    nc._stack = stack
    return nc


def emit_L0(nc, stack, block, p, din, x2T_out, T, leaf_iters=26, polish=6):
    nslab = T // W
    x1_hbm = nc.dram_tensor("x1_scratch", [128, 8 * T], F32)
    w2t_hbm = nc.dram_tensor("w2t_scratch", [128, 32 * 1024], BF16)
    outer_stack = p.stack
    l0_stack = contextlib.ExitStack()
    p.stack = l0_stack

    sident = p.sb("sident", [128, 128], F32)
    s2I128 = p.sb("s2I128", [128, 128], F32)
    onesc = p.sb("sonesc", [128, 1], F32)
    onesr = p.sb("sonesr", [1, 128], F32)
    onesW = p.sb("sonesW", [128, 512], F32)
    h0r = p.sb("sh0r", [128, 4], F32)
    h0i = p.sb("sh0i", [128, 4], F32)
    xs = p.sb("t_x", [128, 8 * W], F32)
    xn = p.sb("t_xn", [128, 8 * W], BF16)
    onescb = p.sb("sonescb", [128, 1], BF16)
    lnsc = make_ln_scratch(p, W)
    lnsc["ones_col"] = onesc[:, :]
    lnsc["ones_col_b"] = onescb[:, :]
    lnsc["ones_row"] = onesr[:, :]

    for nm, tile in [("ident", sident), ("twoI128", s2I128), ("onesc", onesc),
                     ("onesr", onesr), ("onesW", onesW), ("h0r", h0r), ("h0i", h0i)]:
        p.dma_s(tile[:, :], din[nm])

    _sc0 = nc.enter_named_scope("p0_inv", False)[0]
    p1_stack = contextlib.ExitStack()

    def p1sb(name, shape, dtype=F32):
        return p1_stack.enter_context(nc.sbuf_tensor(name, shape, dtype))

    Wi_bf_r = p1sb("Wi_bf_r", [128, 2048], BF16)
    Wi_bf_i = p1sb("Wi_bf_i", [128, 2048], BF16)
    Wi_bf_ni = p1sb("Wi_bf_ni", [128, 2048], BF16)
    Vt_bf_r = p1sb("Vt_bf_r", [128, 2048], BF16)
    Vt_bf_ni = p1sb("Vt_bf_ni", [128, 2048], BF16)

    inv_stack = contextlib.ExitStack()
    inv = Inv(nc, block, inv_stack, sident[:, :], psum_from=p)
    s2I512 = inv_stack.enter_context(nc.sbuf_tensor("s2I512", [128, 2048], F32))

    Wm = inv.alloc("Wm", 512)
    Wt = inv.alloc("Wt", 512, nim=True)
    Wi = inv.alloc("Wi", 512)
    consts_inv = {"twoI512": s2I512[:, :], "twoI128": s2I128[:, :],
                  "ones_col": onesc[:, :], "ones_row": onesr[:, :]}
    sscr0 = inv_stack.enter_context(nc.sbuf_tensor("sscr0", [1, 1], F32))
    glue_stack = contextlib.ExitStack()
    inv.stack = glue_stack
    bufs = {}
    for nm, n, nim in [
        ("X128a", 128, False), ("X128b", 128, False), ("Xs128", 128, False),
        ("Xt128", 128, True), ("M128", 128, False), ("P128", 128, False),
        ("Pt128", 128, True), ("Q128", 128, False), ("S128", 128, False),
        ("St128", 128, True), ("SiQ128", 128, False), ("AiT128", 128, True),
        ("SiT128", 128, True),
        ("Ai256", 256, False), ("AiT256", 256, True), ("P256", 256, False),
        ("Pt256", 256, True), ("Q256", 256, False), ("S256", 256, False),
        ("St256", 256, True), ("SiQ256", 256, False), ("Si256", 256, False),
        ("SiT256", 256, True),
    ]:
        bufs[nm] = inv.alloc(nm, n, nim=nim)
    salpha = glue_stack.enter_context(nc.sbuf_tensor("salpha", [128, 1], F32))
    sscr = glue_stack.enter_context(nc.sbuf_tensor("sscr", [128, 516], F32))
    st11 = glue_stack.enter_context(nc.sbuf_tensor("st11", [1, 1], F32))
    bufs["alpha"] = salpha[:, 0:1]
    bufs["scr"] = sscr[:, :]
    bufs["t11"] = st11[:, :]
    bufs["ps11"] = p.misc1[0:1, 0:1]
    bufs["ps128"] = p.misc2[:, 0:1]

    p.dma_s(s2I512[:, :], din["twoI512"])
    p.dma_s(Wm.re[:, :], din["Vt_r"])
    p.dma_s(Wm.im[:, :], din["Vt_i"])
    p.dma_s(Wt.re[:, :], din["Vn_r"])
    p.dma_s(Wt.im[:, :], din["Vn_i"])

    @block.vector
    def _(v):
        v.wait_ge(p.qs, p.nqs)
        v.memset(sscr0[:, 0:1], 0.0).then_inc(inv.dv, 1)
    inv.ndv += 1

    inv.make_nim(Wt)
    emit_invert512(inv, Wm, Wt, Wi, bufs, consts_inv,
                      leaf_iters=leaf_iters, polish=0)
    glue_stack.close()
    pol_stack = contextlib.ExitStack()
    inv.stack = pol_stack
    Xs512 = inv.alloc("Xs512", 512)
    Xt512 = inv.alloc("Xt512", 512, nim=True)
    M512 = inv.alloc("M512", 512)
    res = inv.newton(Wt, Wi, Xs512, Xt512, M512, s2I512[:, :], 512, polish)
    if res is not Wi:
        inv.dve(lambda v: (v.tensor_copy(Wi.re[:, :], res.re[:, :]),
                           v.tensor_copy(Wi.im[:, :], res.im[:, :]))[-1])
    pol_stack.close()
    inv.stack = inv_stack

    def cast(v):
        v.tensor_copy(Wi_bf_r[:, :], Wi.re[:, :])
        v.tensor_copy(Wi_bf_i[:, :], Wi.im[:, :])
        v.tensor_scalar_mul(Wi_bf_ni[:, :], Wi.im[:, :], -1.0)
        v.tensor_copy(Vt_bf_r[:, :], Wm.re[:, :])
        v.tensor_scalar_mul(Vt_bf_ni[:, :], Wm.im[:, :], -1.0)
    inv.dve(cast)

    @block.vector
    def _(v):
        v.wait_ge(inv.dv, inv.ndv)
        v.memset(p.bar[:, :], 0.0).then_inc(p.el, 1)
    p.nel += 1
    inv_stack.close()

    nc.leave_named_scope("p0_inv", _sc0, False)
    _sc1 = nc.enter_named_scope("p1_mixer", False)[0]
    waT = p1sb("s_waT", [128, 8 * 1024], BF16)
    wxT = p1sb("s_wxT", [128, 8 * 1024], BF16)
    woT = p1sb("s_woT", [128, 4 * 1024], BF16)
    p.dma_g(waT[:, :], din["waT"], wait=[(p.el, p.nel)])
    p.dma_g(wxT[:, :], din["wxT"])
    p.dma_g(woT[:, :], din["woT"])
    n_wdma = p.nqg

    ar = p1sb("t_ar", [128, 4 * W], BF16)
    ai = p1sb("t_ai", [128, 4 * W], BF16)
    xr = p1sb("t_xr", [128, 4 * W], BF16)
    xi = p1sb("t_xi", [128, 4 * W], BF16)
    fbig = p1sb("t_fbig", [128, 12 * W], F32)
    f1 = fbig[:, 0 * W:4 * W]
    f2 = fbig[:, 4 * W:8 * W]
    f3 = fbig[:, 8 * W:12 * W]
    x1s = fbig[:, 0:8 * W]
    rho = p1sb("t_rho", [128, 4 * W], F32)
    sinT = p1sb("t_sin", [128, 4 * W], BF16)
    cosT = p1sb("t_cos", [128, 4 * W], BF16)
    gar = p1sb("t_gar", [128, 4 * W], BF16)
    gai = p1sb("t_gai", [128, 4 * W], BF16)
    zr = p1sb("t_zr", [128, 4 * W], F32)
    zi = p1sb("t_zi", [128, 4 * W], F32)
    thh = zr
    Th = zi
    WN = W + 1
    winr = p1sb("t_winr", [128, 4 * WN], BF16)
    wini = p1sb("t_wini", [128, 4 * WN], BF16)
    hrt = p1sb("t_hr", [128, 4 * W], BF16)
    ur, ui = xr, xi
    btr, bti = ar, ai
    mo = xn
    carT = p1sb("c_T", [128, 4], F32)
    carZr = p1sb("c_zr", [128, 4], F32)
    carZi = p1sb("c_zi", [128, 4], F32)

    def ch(tile, c, w=W):
        return tile[:, c * w:(c + 1) * w]

    def init_car(v):
        v.memset(onescb[:, :], 1.0)
        v.memset(carT[:, :], 0.0)
        v.memset(carZr[:, :], 0.0)
        v.memset(carZi[:, :], 0.0)
        for c in range(4):
            v.tensor_copy(winr[:, c * WN: c * WN + 1], h0r[:, c:c + 1])
            v.tensor_copy(wini[:, c * WN: c * WN + 1], h0i[:, c:c + 1])
    p.vec(init_car)

    for s in range(nslab):
        col0 = s * W
        for c in range(8):
            p.dma_s(ch(xs, c), din["xT"][:, c * T + col0: c * T + col0 + W],
                    wait=([(p.el, p.nel)] if c == 0 else None))
        dma_tgt = p.nqs
        p.el_barrier([(p.qs, dma_tgt), (p.qg, n_wdma)])

        emit_ln(p, lambda c: ch(xs, c), lambda c: ch(xn, c), W, lnsc,
                  x_full=xs[:, 0:8 * W], out_full=xn[:, 0:8 * W])

        def evac_a(m, ps):
            dst = ar if m < 4 else ai
            p.vec(lambda v, dst=dst, mm=m % 4, ps=ps:
                  v.tensor_copy(ch(dst, mm), ps[:, 0:W]))
        emit_mm(p, None,
                  lambda k, m: waT[:, k * 1024 + m * 128: k * 1024 + m * 128 + 128],
                  lambda k: ch(xn, k), 8, 8, W, evac_a)

        def evac_x(m, ps):
            dst = xr if m < 4 else xi
            p.vec(lambda v, dst=dst, mm=m % 4, ps=ps:
                  v.tensor_copy(ch(dst, mm), ps[:, 0:W]))
        emit_mm(p, None,
                  lambda k, m: wxT[:, k * 1024 + m * 128: k * 1024 + m * 128 + 128],
                  lambda k: ch(xn, k), 8, 8, W, evac_x)

        FW = slice(0, 4 * W)

        def xcf(v):
            v.tensor_mul(f1[:, FW], xr[:, FW], xr[:, FW])
            v.tensor_mul(f2[:, FW], xi[:, FW], xi[:, FW])
            v.tensor_add(f1[:, FW], f1[:, FW], f2[:, FW])
        p.vec(xcf)

        def sqxf(a):
            a.activation(f2[:, FW], f1[:, FW], AF.Sqrt)
        p.act(sqxf)

        def xc2(v):
            for c in range(4):
                v.tensor_mul(winr[:, c * WN + 1: c * WN + 1 + W], ch(xr, c), ch(f2, c))
                v.tensor_mul(wini[:, c * WN + 1: c * WN + 1 + W], ch(xi, c), ch(f2, c))
        p.vec(xc2)

        def gate(v):
            v.tensor_mul(f1[:, FW], ar[:, FW], ar[:, FW])
            v.tensor_mul(f2[:, FW], ai[:, FW], ai[:, FW])
            v.tensor_add(f1[:, FW], f1[:, FW], f2[:, FW])
        p.vec(gate)

        def sqf(a):
            a.activation(f2[:, FW], f1[:, FW], AF.Sqrt)
        p.act(sqf)

        def gate2(v):
            v.tensor_scalar_add(f3[:, FW], f1[:, FW], 1.0)
            v.reciprocal(f3[:, FW], f3[:, FW])
            v.tensor_mul(rho[:, FW], f1[:, FW], f3[:, FW])
            v.tensor_mul(f3[:, FW], f2[:, FW], f3[:, FW])
            v.tensor_mul(gar[:, FW], f3[:, FW], ar[:, FW])
            v.tensor_mul(gai[:, FW], f3[:, FW], ai[:, FW])
            v.tensor_add(f2[:, FW], f2[:, FW], ar[:, FW])
            v.tensor_mul(f3[:, FW], f2[:, FW], f2[:, FW])
            v.tensor_mul(f1[:, FW], ai[:, FW], ai[:, FW])
            v.tensor_add(f3[:, FW], f3[:, FW], f1[:, FW])
        p.vec(gate2)

        def sq2f(a):
            a.activation(f1[:, FW], f3[:, FW], AF.Sqrt)
        p.act(sq2f)

        def gate3(v):
            v.tensor_add(f1[:, FW], f1[:, FW], f2[:, FW])
            v.tensor_scalar_max(f1[:, FW], f1[:, FW], 1e-30)
            v.reciprocal(f1[:, FW], f1[:, FW])
            v.tensor_mul(f2[:, FW], f1[:, FW], ai[:, FW])
        p.vec(gate3)

        def atanf(a):
            a.activation(thh[:, FW], f2[:, FW], AF.Arctan)
        p.act(atanf)

        for mch in range(4):
            ps = p.psum()

            def ufn(t, mch=mch, ps=ps):
                for k in range(4):
                    t.matmul(ps[:, 0:W],
                             Wi_bf_r[:, k * 512 + mch * 128: k * 512 + mch * 128 + 128],
                             winr[:, k * WN: k * WN + W], start=(k == 0), stop=False)
                last = None
                for k in range(4):
                    last = t.matmul(
                        ps[:, 0:W],
                        Wi_bf_ni[:, k * 512 + mch * 128: k * 512 + mch * 128 + 128],
                        wini[:, k * WN: k * WN + W], start=False, stop=(k == 3))
                return last
            p.tensor(ufn)
            p.vec(lambda v, mch=mch, ps=ps: v.tensor_copy(ch(ur, mch), ps[:, 0:W]))
            ps2 = p.psum()

            def ufn2(t, mch=mch, ps=ps2):
                for k in range(4):
                    t.matmul(ps[:, 0:W],
                             Wi_bf_i[:, k * 512 + mch * 128: k * 512 + mch * 128 + 128],
                             winr[:, k * WN: k * WN + W], start=(k == 0), stop=False)
                last = None
                for k in range(4):
                    last = t.matmul(
                        ps[:, 0:W],
                        Wi_bf_r[:, k * 512 + mch * 128: k * 512 + mch * 128 + 128],
                        wini[:, k * WN: k * WN + W], start=False, stop=(k == 3))
                return last
            p.tensor(ufn2)
            p.vec(lambda v, mch=mch, ps=ps2: v.tensor_copy(ch(ui, mch), ps[:, 0:W]))

        def scans1(v):
            for c in range(4):
                v.tensor_tensor_scan(ch(Th, c), onesW[:, 0:W], ch(thh, c),
                                     carT[:, c:c + 1], ALU.mult, ALU.add)
            for c in range(4):
                v.tensor_copy(carT[:, c:c + 1], Th[:, c * W + W - 1: c * W + W])
        p.vec(scans1)

        MAGIC = 12582912.0

        def rr1(v):
            H = Th[:, FW]
            v.tensor_scalar_mul(f1[:, FW], H, 2.0 / PI)
            v.tensor_scalar_add(f1[:, FW], f1[:, FW], MAGIC)
            v.tensor_scalar_add(f1[:, FW], f1[:, FW], -MAGIC)
            v.scalar_tensor_tensor(f1[:, FW], f1[:, FW], -PI / 2, H,
                                   ALU.mult, ALU.add)
            v.tensor_scalar(f2[:, FW], H, 2.0 / PI, 0.25, ALU.mult, ALU.add)
            v.tensor_scalar_add(f2[:, FW], f2[:, FW], MAGIC)
            v.tensor_scalar_add(f2[:, FW], f2[:, FW], -MAGIC)
            v.tensor_scalar_mul(f2[:, FW], f2[:, FW], -PI / 2)
            v.tensor_add(f2[:, FW], f2[:, FW], H)
            v.tensor_scalar_add(f2[:, FW], f2[:, FW], PI / 8)
            for t_ in (f1[:, FW], f2[:, FW]):
                v.tensor_scalar_min(t_, t_, 0.785396)
                v.tensor_scalar_max(t_, t_, -0.785396)
        p.vec(rr1)

        def sincos(a):
            a.activation(sinT[:, FW], f1[:, FW], AF.Sin, scale=4.0)
            a.activation(cosT[:, FW], f2[:, FW], AF.Sin, scale=4.0)
        p.act(sincos)

        def bmix(v):
            v.tensor_mul(f1[:, FW], gar[:, FW], ur[:, FW])
            v.tensor_mul(f2[:, FW], gai[:, FW], ui[:, FW])
            v.tensor_sub(f1[:, FW], f1[:, FW], f2[:, FW])
            v.tensor_mul(f2[:, FW], gar[:, FW], ui[:, FW])
            v.tensor_mul(f3[:, FW], gai[:, FW], ur[:, FW])
            v.tensor_add(f2[:, FW], f2[:, FW], f3[:, FW])
            v.tensor_mul(f3[:, FW], cosT[:, FW], f1[:, FW])
            v.tensor_mul(btr[:, FW], sinT[:, FW], f2[:, FW])
            v.tensor_add(btr[:, FW], btr[:, FW], f3[:, FW])
            v.tensor_mul(f3[:, FW], cosT[:, FW], f2[:, FW])
            v.tensor_mul(f2[:, FW], sinT[:, FW], f1[:, FW])
            v.tensor_sub(bti[:, FW], f3[:, FW], f2[:, FW])
        p.vec(bmix)

        def scans2(v):
            for c in range(4):
                v.tensor_tensor_scan(ch(zr, c), ch(rho, c), ch(btr, c),
                                     carZr[:, c:c + 1], ALU.mult, ALU.add)
                v.tensor_tensor_scan(ch(zi, c), ch(rho, c), ch(bti, c),
                                     carZi[:, c:c + 1], ALU.mult, ALU.add)
            for c in range(4):
                v.tensor_copy(carZr[:, c:c + 1], zr[:, c * W + W - 1: c * W + W])
                v.tensor_copy(carZi[:, c:c + 1], zi[:, c * W + W - 1: c * W + W])
        p.vec(scans2)

        sr, si = gar, gai

        def smix(v):
            v.tensor_mul(f1[:, FW], cosT[:, FW], zr[:, FW])
            v.tensor_mul(f2[:, FW], sinT[:, FW], zi[:, FW])
            v.tensor_sub(sr[:, FW], f1[:, FW], f2[:, FW])
            v.tensor_mul(f1[:, FW], cosT[:, FW], zi[:, FW])
            v.tensor_mul(f2[:, FW], sinT[:, FW], zr[:, FW])
            v.tensor_add(si[:, FW], f1[:, FW], f2[:, FW])
        p.vec(smix)

        for mcH in range(4):
            ps = p.psum()

            def hfn(t, mcH=mcH, ps=ps):
                for k in range(4):
                    t.matmul(ps[:, 0:W],
                             Vt_bf_r[:, k * 512 + mcH * 128: k * 512 + mcH * 128 + 128],
                             ch(sr, k), start=(k == 0), stop=False)
                last = None
                for k in range(4):
                    last = t.matmul(
                        ps[:, 0:W],
                        Vt_bf_ni[:, k * 512 + mcH * 128: k * 512 + mcH * 128 + 128],
                        ch(si, k), start=False, stop=(k == 3))
                return last
            p.tensor(hfn)
            p.vec(lambda v, mcH=mcH, ps=ps:
                  v.tensor_add(ch(hrt, mcH), ps[:, 0:W],
                               winr[:, mcH * WN + 1: mcH * WN + 1 + W]))

        def wincar(v):
            for c in range(4):
                v.tensor_copy(winr[:, c * WN: c * WN + 1],
                              winr[:, c * WN + W: c * WN + W + 1])
                v.tensor_copy(wini[:, c * WN: c * WN + 1],
                              wini[:, c * WN + W: c * WN + W + 1])
        p.vec(wincar)

        def evac_mo(m, ps):
            p.vec(lambda v, m=m, ps=ps: v.tensor_copy(ch(mo, m), ps[:, 0:W]))
        emit_mm(p, None,
                  lambda k, m: woT[:, k * 1024 + m * 128: k * 1024 + m * 128 + 128],
                  lambda k: ch(hrt, k), 4, 8, W, evac_mo)

        emit_ln(p, lambda c: ch(mo, c), lambda c: ch(x1s, c), W, lnsc,
                  x_full=mo[:, 0:8 * W], out_full=x1s[:, 0:8 * W])

        def resid1(v):
            for c in range(8):
                v.tensor_add(ch(x1s, c), ch(x1s, c), ch(xs, c))
        p.vec(resid1)

        for c in range(8):
            p.dma_s(x1_hbm.ap()[:, c * T + col0: c * T + col0 + W], ch(x1s, c),
                    wait=([(p.el, p.nel)] if c == 0 else None))

    p.el_barrier([(p.qs, p.nqs)])
    p1_stack.close()

    nc.leave_named_scope("p1_mixer", _sc1, False)
    _sc2 = nc.enter_named_scope("p2_ffn", False)[0]
    WF = W2W
    nslab2 = T // WF
    W1t = p.sb("s_W1t", [128, 8 * 4096], BF16)
    w2b = p.sb("t_w2b", [128, 2 * 1024], BF16)
    w2stage = p.sb("t_w2stage", [128, 2 * 1024], BF16)
    b1c = p.sb("s_b1c", [128, 32], F32)
    b2c = p.sb("s_b2c", [128, 8], F32)
    p.dma_s(b1c[:, :], din["b1c"], wait=[(p.el, p.nel)])
    p.dma_s(b2c[:, :], din["b2c"])
    h1 = p.sb("t_h1", [128, 32 * WF], BF16)
    wld = h1[:, 0:2048].bitcast(F32)
    wabs = h1[:, 2048:4096].bitcast(F32)
    accv = p.sb("t_acc", [128, 2], F32)
    sc11 = p.sb("t_sc11", [1, 1], F32)
    scbc = p.sb("t_scbc", [128, 1], F32)
    x2s = lnsc["sq"]
    fo = xn

    def ternarize(dram_ap, nsub, width, out_tile, n_elem, out_hbm=None,
                  stage=None):
        for c in range(nsub):
            p.dma_s(wld[:, 0:width], dram_ap[:, c * width:(c + 1) * width],
                    wait=[(p.el, p.nel)])
            tgt = p.nqs

            def absf(a, c=c):
                a.activation(wabs[:, 0:width], wld[:, 0:width], AF.Abs)
            p.act(absf, wait_dma=[(p.qs, tgt)])

            def red(v, c=c):
                if c == 0:
                    v.reduce_sum(accv[:, 0:1], wabs[:, 0:width], axis=AX)
                else:
                    v.reduce_sum(accv[:, 1:2], wabs[:, 0:width], axis=AX)
                    v.tensor_add(accv[:, 0:1], accv[:, 0:1], accv[:, 1:2])
            p.vec(red)

        def pt(t):
            return t.matmul(p.misc1[0:1, 0:1], accv[:, 0:1], onesc[:, :],
                            start=True, stop=True)
        p.tensor(pt)

        def scale_f(v):
            v.tensor_scalar_mul(sc11[:, :], p.misc1[0:1, 0:1], 1.0 / n_elem)
            v.tensor_scalar_max(sc11[:, :], sc11[:, :], 1e-8)
            v.reciprocal(sc11[:, :], sc11[:, :])
        p.vec(scale_f)

        def bcm(t):
            return t.matmul(p.misc2[:, 0:1], onesr[:, :], sc11[:, :],
                            start=True, stop=True)
        p.tensor(bcm)
        p.vec(lambda v: v.tensor_copy(scbc[:, :], p.misc2[:, 0:1]))

        for c in range(nsub):
            p.dma_s(wld[:, 0:width], dram_ap[:, c * width:(c + 1) * width],
                    wait=[(p.el, p.nel)])
            tgt = p.nqs
            if out_hbm is None:
                dst = out_tile[:, c * width:(c + 1) * width]
            else:
                dst = stage[:, (c % 2) * width:(c % 2 + 1) * width]

            def tern(v, c=c, dst=dst):
                MAGIC = 12582912.0
                v.tensor_scalar_mul(wabs[:, 0:width], wld[:, 0:width], scbc[:, 0:1])
                v.tensor_scalar_min(wabs[:, 0:width], wabs[:, 0:width], 2.5)
                v.tensor_scalar_max(wabs[:, 0:width], wabs[:, 0:width], -2.5)
                v.tensor_scalar_add(wabs[:, 0:width], wabs[:, 0:width], MAGIC)
                v.tensor_scalar_add(wabs[:, 0:width], wabs[:, 0:width], -MAGIC)
                v.tensor_scalar_min(wabs[:, 0:width], wabs[:, 0:width], 1.0)
                v.tensor_scalar_max(wabs[:, 0:width], wabs[:, 0:width], -1.0)
                v.tensor_copy(dst, wabs[:, 0:width])
            p.vec(tern, wait_dma=[(p.qs, tgt)])
            if out_hbm is not None:
                p.dma_s(out_hbm[:, c * width:(c + 1) * width], dst,
                        wait=[(p.el, p.nel)])

    ternarize(din["W1T"], 32, 1024, W1t, 4096 * 1024)
    ternarize(din["W2T"], 32, 1024, None, 4096 * 1024,
              out_hbm=w2t_hbm.ap(), stage=w2stage)

    for s in range(nslab2):
        col0 = s * WF
        for c in range(8):
            p.dma_s(xs[:, c * WF:(c + 1) * WF],
                    x1_hbm.ap()[:, c * T + col0: c * T + col0 + WF],
                    wait=([(p.el, p.nel)] if c == 0 else None))
        tgt = p.nqs
        p.el_barrier([(p.qs, tgt)])

        emit_ln(p, lambda c: xs[:, c * WF:(c + 1) * WF],
                  lambda c: xn[:, c * WF:(c + 1) * WF], WF, lnsc,
                  x_full=xs[:, 0:8 * WF], out_full=xn[:, 0:8 * WF])

        nel_ge = {-2: p.nel, -1: p.nel}
        for m in range(32):
            ps = p.psum()

            def m1(t, m=m, ps=ps):
                t.wait_ge(p.el, nel_ge[m - 2])
                last = None
                for k in range(8):
                    last = t.matmul(
                        ps[:, 0:WF],
                        W1t[:, k * 4096 + m * 128: k * 4096 + m * 128 + 128],
                        xn[:, k * WF:(k + 1) * WF], start=(k == 0), stop=(k == 7))
                return last
            p.tensor(m1, wait_el=False)

            def ge(a, m=m, ps=ps):
                a.activation(h1[:, m * WF:(m + 1) * WF], ps[:, 0:WF], AF.Gelu,
                             bias=b1c[:, m:m + 1])
            p.act(ge)
            nel_ge[m] = p.nel

        psm_all = [p.ps[0], p.ps[1], p.ps[2], p.ps[3],
                   p.pst[0], p.pst[1], p.misc1, p.misc2]
        m2_pe = {}
        for k in range(32):
            buf = w2b[:, (k % 2) * 1024:(k % 2 + 1) * 1024]
            wlk = [(p.qs, p.nqs)]
            if k >= 2:
                wlk.append((p.pe, m2_pe[k - 2]))
            else:
                wlk.append((p.el, p.nel))
            p.dma_s(buf, w2t_hbm.ap()[:, k * 1024:(k + 1) * 1024], wait=wlk)
            wt = p.nqs

            def m2k(t, k=k, buf=buf, wt=wt):
                t.wait_ge(p.qs, wt)
                last = None
                for m in range(8):
                    last = t.matmul(psm_all[m][:, 0:WF],
                                    buf[:, m * 128:(m + 1) * 128],
                                    h1[:, k * WF:(k + 1) * WF],
                                    start=(k == 0), stop=(k == 31))
                return last
            p.tensor(m2k, wait_el=(k == 0))
            m2_pe[k] = p.npe

        def bia(v):
            for m in range(8):
                v.tensor_scalar_add(fo[:, m * WF:(m + 1) * WF],
                                    psm_all[m][:, 0:WF], b2c[:, m:m + 1])
        p.vec(bia)

        emit_ln(p, lambda c: fo[:, c * WF:(c + 1) * WF],
                  lambda c: x2s[:, c * WF:(c + 1) * WF], WF, lnsc,
                  x_full=fo[:, 0:8 * WF], out_full=x2s[:, 0:8 * WF])

        def resid2(v):
            v.tensor_add(x2s[:, 0:8 * WF], x2s[:, 0:8 * WF], xs[:, 0:8 * WF])
        p.vec(resid2)

        for c in range(8):
            p.dma_s(x2T_out[:, c * T + col0: c * T + col0 + WF],
                    x2s[:, c * WF:(c + 1) * WF],
                    wait=([(p.el, p.nel)] if c == 0 else None))

    p.el_barrier([(p.qs, p.nqs), (p.qg, p.nqg)])
    nc.leave_named_scope("p2_ffn", _sc2, False)
    p.stack = outer_stack
    l0_stack.close()
    return None


F32 = mybir.dt.float32
BF16 = mybir.dt.bfloat16
I32 = mybir.dt.int32
AF = mybir.ActivationFunctionType
ALU = mybir.AluOpType
AX = mybir.AxisListType.X

WS = 512
RMS_EPS = 1.1920929e-07
SCALE = 64.0 ** -0.5
LN_EPS_IN = 1e-12


def build_F(T=8192, leaf_iters=26, polish=6):
    TD = T // 4
    nslab = T // WS
    nheads = 16
    ntb = T // 128
    nc = bass.Bass("TRN2", target_bir_lowering=False)
    din = {}
    for nm, shp in AB_INPUTS:
        if shp is None:
            shp = [128, 8 * T]
        din[nm] = nc.dram_tensor(nm, shp, F32, kind="ExternalInput").ap()
    for nm, shp, dt in [
        ("dqc", [128, 4 * 64], F32), ("uqt", [128, 4 * 64], F32),
        ("dkvc", [128, 8 * 64], F32), ("ukvt", [128, 8 * 128], F32),
        ("qrt", [64, 64], F32), ("krt", [64, 64], F32), ("wot", [64, 64], F32),
        ("qnw", [64, 1], F32), ("knw", [64, 1], F32),
        ("masks16", [128, 16 * 512], F32), ("rmask", [128, 4], F32),
        ("wrc", [128, 8 * 4], F32), ("w1e", [128, 4 * 32 * 1024], F32),
        ("b1cD", [128, 4 * 32], F32), ("woc", [128, 32 * 1024], F32),
        ("boc", [128, 8], F32), ("gam", [128, 8], F32), ("bet", [128, 8], F32),
    ]:
        din[nm] = nc.dram_tensor(nm, shp, dt, kind="ExternalInput").ap()
    outT = nc.dram_tensor("outT", [128, 8 * TD], F32, kind="ExternalOutput").ap()
    x2_hbm = nc.dram_tensor("x2_scratch", [128, 8 * T], F32)
    x2own_hbm = nc.dram_tensor("x2own_scratch", [128, 8 * TD], F32)
    w1bf_hbm = nc.dram_tensor("w1bf_scratch", [128, 4 * 32 * 1024], BF16)
    wocbf_hbm = nc.dram_tensor("wocbf_scratch", [128, 32 * 1024], BF16)
    kv_hbm = nc.dram_tensor("kv_scratch", [128, nheads * T], BF16)
    vt_hbm = nc.dram_tensor("vt_scratch", [128, nheads * ntb * 65], BF16)
    ao_hbm = nc.dram_tensor("ao_scratch", [128, 8 * TD], BF16)

    stack = contextlib.ExitStack()
    block = stack.enter_context(nc.Block())
    p = Prog(nc, block, stack)
    p.alloc_psum(n_big=4)

    emit_L0(nc, stack, block, p, din, x2_hbm.ap(), T, leaf_iters, polish)

    srm = p.sb("srmask", [128, 4], F32)
    p.dma_s(srm[:, :], din["rmask"], wait=[(p.el, p.nel)])

    p3 = contextlib.ExitStack()
    p.stack = p3

    def p3sb(name, shape, dtype=F32):
        return p3.enter_context(nc.sbuf_tensor(name, shape, dtype))

    onescb2 = p3sb("onescb2", [128, 1], BF16)
    qnw = p3sb("sqnw", [64, 1], F32)
    knw = p3sb("sknw", [64, 1], F32)
    identf = p3sb("sidentf", [128, 128], F32)
    masks = p3sb("smasks16", [128, 16 * 512], BF16)
    wsm = {}
    for nm, shp in [("dqc", [128, 4 * 64]), ("uqt", [128, 4 * 64]),
                    ("dkvc", [128, 8 * 64]), ("ukvt", [128, 8 * 128]),
                    ("qrt", [64, 64]), ("krt", [64, 64]), ("wot", [64, 64])]:
        wsm[nm] = p3sb("s_" + nm, shp, BF16)
    p.dma_s(qnw[:, :], din["qnw"], wait=[(p.el, p.nel)])
    p.dma_s(knw[:, :], din["knw"])
    p.dma_s(identf[:, :], din["ident"])
    p.dma_g(masks[:, :], din["masks16"], wait=[(p.el, p.nel)])
    for nm in ["dqc", "uqt", "dkvc", "ukvt", "qrt", "krt", "wot"]:
        p.dma_g(wsm[nm][:, :], din[nm])
    nq3s, nq3g = p.nqs, p.nqg

    xs3 = p3sb("t_xs3", [128, 8 * WS], F32)
    xns = p3sb("t_xns", [128, 8 * WS], BF16)
    xh0 = p3sb("t_xh0", [64, WS], BF16)
    kvs = p3sb("t_kvs", [128, WS], F32)
    kpiece = p3sb("t_kp", [128, WS], BF16)
    vtp = p3sb("t_vtp", [128, 4 * 65], BF16)
    rbc = p3sb("t_rbc", [128, WS], F32)
    att = p3sb("t_att", [128, 2 * WS], F32)
    attb = p3sb("t_attb", [128, 2 * WS], BF16)
    qct = p3sb("t_qct", [64, 64], BF16)
    kvct = p3sb("t_kvct", [64, 128], BF16)
    sonesc3 = p3sb("sonesc3", [128, 1], F32)
    sonesr3 = p3sb("sonesr3", [1, 128], F32)
    lnsc3 = make_ln_scratch(p, WS, tag="3")
    lnsc3["ones_col"] = sonesc3[:, :]
    lnsc3["ones_col_b"] = onescb2[:, :]
    lnsc3["ones_row"] = sonesr3[:, :]
    p.dma_s(lnsc3["ones_col"], din["onesc"])
    p.dma_s(lnsc3["ones_row"], din["onesr"])
    nq3s = p.nqs

    p.el_barrier([(p.qs, nq3s), (p.qg, nq3g)])
    p.vec(lambda v: v.memset(onescb2[:, :], 1.0))

    def qc_fn(t):
        last = None
        for k in range(4):
            last = t.matmul(p.misc1[0:64, 0:64], wsm["dqc"][:, k * 64:(k + 1) * 64],
                            wsm["uqt"][:, k * 64:(k + 1) * 64],
                            start=(k == 0), stop=(k == 3))
        return last
    p.tensor(qc_fn)
    p.vec(lambda v: v.tensor_copy(qct[:, :], p.misc1[0:64, 0:64]))

    def kvc_fn(t):
        last = None
        for k in range(8):
            last = t.matmul(p.misc2[0:64, 0:128], wsm["dkvc"][:, k * 64:(k + 1) * 64],
                            wsm["ukvt"][:, k * 128:(k + 1) * 128],
                            start=(k == 0), stop=(k == 7))
        return last
    p.tensor(kvc_fn)
    p.vec(lambda v: v.tensor_copy(kvct[:, :], p.misc2[0:64, 0:128]))

    _sc3a = nc.enter_named_scope("p3a_kv", False)[0]
    for s in range(nslab):
        col0 = s * WS
        for c in range(8):
            p.dma_s(xs3[:, c * WS:(c + 1) * WS],
                    x2_hbm.ap()[:, c * T + col0: c * T + col0 + WS],
                    wait=([(p.el, p.nel)] if c == 0 else None))
        tgt = p.nqs
        p.el_barrier([(p.qs, tgt)])
        emit_ln(p, lambda c: xs3[:, c * WS:(c + 1) * WS],
                  lambda c: xns[:, c * WS:(c + 1) * WS], WS, lnsc3,
                  x_full=xs3[:, 0:8 * WS], out_full=xns[:, 0:8 * WS])

        for h in range(nheads):
            hc, hr = h // 2, (h % 2) * 64

            def cph(v, hc=hc, hr=hr):
                v.tensor_copy(xh0[:, :], xns[hr:hr + 64, hc * WS:(hc + 1) * WS])
            p.vec(cph, wait_dma=[(p.qg, p.nqg)])

            ps = p.psum()

            def km(t, ps=ps):
                return t.matmul(ps[0:64, 0:WS], kvct[:, 0:64], xh0[:, :],
                                start=True, stop=True)
            p.tensor(km)
            p.vec(lambda v, ps=ps: v.tensor_copy(kvs[0:64, :], ps[0:64, 0:WS]))

            def sqk(a):
                a.activation(att[0:64, 0:WS], kvs[0:64, :], AF.Square)
            p.act(sqk)

            def msk(t):
                return t.matmul(p.misc1[0:1, 0:WS], lnsc3["ones_col"][0:64, 0:1],
                                att[0:64, 0:WS], start=True, stop=True)
            p.tensor(msk)

            def kr1(v):
                v.tensor_scalar(rbc[0:1, :], p.misc1[0:1, 0:WS], 1.0 / 64,
                                RMS_EPS, ALU.mult, ALU.add)
            p.vec(kr1)

            def kr2(a):
                a.activation(att[0:1, 0:WS], rbc[0:1, :], AF.Ln)
                a.activation(rbc[0:1, :], att[0:1, 0:WS], AF.Exp, scale=-0.5)
            p.act(kr2)

            def kr3(t):
                return t.matmul(p.misc1[:, 0:WS], lnsc3["ones_row"], rbc[0:1, :],
                                start=True, stop=True)
            p.tensor(kr3)

            def kn(v):
                v.tensor_copy(rbc[:, :], p.misc1[:, 0:WS])
                v.tensor_mul(att[0:64, 0:WS], kvs[0:64, :], rbc[0:64, :])
                v.tensor_scalar(kpiece[0:64, :], att[0:64, 0:WS], knw[:, 0:1], 1.0,
                                ALU.mult, ALU.mult)
            p.vec(kn)

            ps2 = p.psum()

            def krm(t, ps=ps2):
                return t.matmul(ps[0:64, 0:WS], wsm["krt"][:, :], xh0[:, :],
                                start=True, stop=True)
            p.tensor(krm)
            p.vec(lambda v, ps=ps2: v.tensor_copy(kpiece[64:128, :], ps[0:64, 0:WS]))
            p.dma_g(kv_hbm.ap()[:, h * T + col0: h * T + col0 + WS], kpiece[:, :],
                    wait=[(p.el, p.nel)])

            def vo(v):
                v.memset(vtp[:, :].rearrange("p (b o) -> p b o", o=65)[:, :, 64:65], 1.0)
            p.vec(vo)
            for i in range(4):
                ps3 = p.psum()

                def vm(t, i=i, ps=ps3):
                    return t.matmul(ps[0:128, 0:64], xh0[:, i * 128:(i + 1) * 128],
                                    kvct[:, 64:128], start=True, stop=True)
                p.tensor(vm)
                p.vec(lambda v, i=i, ps=ps3:
                      v.tensor_copy(vtp[:, i * 65: i * 65 + 64], ps[0:128, 0:64]))
            p.dma_g(vt_hbm.ap()[:, h * ntb * 65 + 4 * s * 65: h * ntb * 65 + 4 * s * 65 + 260],
                    vtp[:, :], wait=[(p.el, p.nel)])

    nc.leave_named_scope("p3a_kv", _sc3a, False)
    _sc3b = nc.enter_named_scope("p3b_attn", False)[0]
    oxn = p3sb("t_oxn", [128, 8 * TD], BF16)
    Kh = p3sb("t_Kh", [128, T], BF16)
    vTh = p3sb("t_vTh", [128, ntb * 65], BF16)
    Qown = p3sb("t_Qo", [128, TD], BF16)
    qcs = p3sb("t_qcs", [64, WS], F32)
    rm = p3sb("t_rm", [128, TD // 128 + 1], F32)
    rmrow = p3sb("t_rmrow", [1, 128], F32)
    nmbc = p3sb("t_nmbc", [128, WS], F32)
    ctxb = p3sb("t_ctxb", [64, WS], BF16)
    ao = p3sb("t_ao", [128, 8 * TD], BF16)
    n_own_slab = TD // WS
    n_qt = TD // 128
    n_g = TD // 512

    xg = p3sb("t_xg", [128, 2 * 2048], F32)
    sel_done = {}
    nqs_writes = 0
    for s4 in range(n_own_slab):
        for c in range(8):
            k = s4 * 8 + c
            buf = xg[:, (k % 2) * 2048:(k % 2) * 2048 + 2048]
            src = x2_hbm.ap()[:, c * T + 2048 * s4: c * T + 2048 * s4 + 2048] \
                .rearrange("p (g r i) -> p r g i", g=4, r=4)
            for j in range(4):
                wl0 = None
                if j == 0:
                    wl0 = [(p.qg, p.nqg)]
                    wl0.append((p.el, sel_done[k - 2]) if k >= 2
                               else (p.el, p.nel))
                p.dma_g(buf[:, j * WS:(j + 1) * WS], src[:, j], wait=wl0)
            wt = p.nqg
            wl = [(p.qg, wt)]
            if c == 0 and s4 > 0:
                wl.append((p.qs, nqs_writes))

            def sel(v, c=c, buf=buf):
                d = xs3[:, c * WS:(c + 1) * WS]
                v.tensor_scalar(d, buf[:, 0:WS], srm[:, 0:1], None, ALU.mult)
                for j in range(1, 4):
                    v.scalar_tensor_tensor(d, buf[:, j * WS:(j + 1) * WS],
                                           srm[:, j:j + 1], d, ALU.mult, ALU.add)
            p.vec(sel, wait_dma=wl)
            sel_done[k] = p.nel
        for c in range(8):
            p.dma_s(x2own_hbm.ap()[:, c * TD + s4 * WS: c * TD + s4 * WS + WS],
                    xs3[:, c * WS:(c + 1) * WS],
                    wait=([(p.el, p.nel)] if c == 0 else None))
        nqs_writes = p.nqs
        emit_ln(p, lambda c: xs3[:, c * WS:(c + 1) * WS],
                  lambda c: oxn[:, c * TD + s4 * WS: c * TD + s4 * WS + WS],
                  WS, lnsc3, x_full=xs3[:, 0:8 * WS])

    for i in range(64):
        p.dma_g(w1bf_hbm.ap()[:, i * 2048:(i + 1) * 2048],
                din["w1e"][:, i * 2048:(i + 1) * 2048],
                wait=([(p.el, p.nel)] if i == 0 else None))
    for i in range(16):
        p.dma_g(wocbf_hbm.ap()[:, i * 2048:(i + 1) * 2048],
                din["woc"][:, i * 2048:(i + 1) * 2048])
    nqg_cast = p.nqg

    for h in range(nheads):
        hc, hr = h // 2, (h % 2) * 64
        p.dma_s(Kh[:, :], kv_hbm.ap()[:, h * T:(h + 1) * T],
                wait=[(p.el, p.nel)])
        p.dma_s(vTh[:, :], vt_hbm.ap()[:, h * ntb * 65:(h + 1) * ntb * 65])
        nkv = p.nqs
        p.el_barrier([(p.qs, nkv)])

        for nt in range(n_own_slab):
            sl = slice(nt * WS, (nt + 1) * WS)

            def cpo(v, sl=sl, hc=hc, hr=hr):
                v.tensor_copy(xh0[:, :], oxn[hr:hr + 64, hc * TD + nt * WS: hc * TD + nt * WS + WS])
            p.vec(cpo)
            ps = p.psum()

            def qm(t, ps=ps):
                return t.matmul(ps[0:64, 0:WS], qct[:, :], xh0[:, :],
                                start=True, stop=True)
            p.tensor(qm)
            p.vec(lambda v, ps=ps: v.tensor_copy(qcs[:, :], ps[0:64, 0:WS]))

            def sqq(a):
                a.activation(att[0:64, 0:WS], qcs[:, :], AF.Square)
            p.act(sqq)

            def msq(t):
                return t.matmul(p.misc1[0:1, 0:WS], lnsc3["ones_col"][0:64, 0:1],
                                att[0:64, 0:WS], start=True, stop=True)
            p.tensor(msq)

            def qr1(v):
                v.tensor_scalar(rbc[0:1, :], p.misc1[0:1, 0:WS], 1.0 / 64,
                                RMS_EPS, ALU.mult, ALU.add)
            p.vec(qr1)

            def qr2(a):
                a.activation(att[0:1, 0:WS], rbc[0:1, :], AF.Ln)
                a.activation(rbc[0:1, :], att[0:1, 0:WS], AF.Exp, scale=-0.5)
            p.act(qr2)

            def qr3(t):
                return t.matmul(p.misc1[:, 0:WS], lnsc3["ones_row"], rbc[0:1, :],
                                start=True, stop=True)
            p.tensor(qr3)

            def qn(v, sl=sl):
                v.tensor_copy(rbc[:, :], p.misc1[:, 0:WS])
                v.tensor_mul(qcs[:, :], qcs[:, :], rbc[0:64, :])
                v.tensor_scalar(Qown[0:64, sl], qcs[:, :], qnw[:, 0:1], SCALE,
                                ALU.mult, ALU.mult)
            p.vec(qn)
            ps2 = p.psum()

            def qrm(t, ps=ps2, sl=sl):
                return t.matmul(ps[0:64, 0:WS], wsm["qrt"][:, :], Qown[0:64, sl],
                                start=True, stop=True)
            p.tensor(qrm)
            p.vec(lambda v, ps=ps2, sl=sl:
                  v.tensor_copy(Qown[64:128, sl], ps[0:64, 0:WS]))

        idx = 0
        pe_sA = {}
        nel_mx = {}
        for i in range(n_qt):
            for kb in range(i + 1):
                bank = p.ps[2 + (idx % 2)]

                def sA(t, bank=bank, i=i, kb=kb, idx=idx):
                    t.wait_ge(p.el, nel_mx[idx - 2] if idx >= 2 else p.nel)
                    return t.matmul(bank[:, 0:WS], Qown[:, i * 128:(i + 1) * 128],
                                    Kh[:, kb * 512:(kb + 1) * 512],
                                    start=True, stop=True)
                p.tensor(sA, wait_el=False)
                pe_sA[idx] = p.npe

                def mx(v, bank=bank, i=i, kb=kb):
                    if kb == 0:
                        v.reduce_max(rm[:, i:i + 1], bank[:, 0:WS], axis=AX,
                                     negate=True)
                    else:
                        v.reduce_max(rm[:, n_qt:n_qt + 1], bank[:, 0:WS], axis=AX,
                                     negate=True)
                        v.tensor_tensor(rm[:, i:i + 1], rm[:, i:i + 1],
                                        rm[:, n_qt:n_qt + 1], ALU.min)
                p.vec(mx, wait_pe=False, wait_dma=[(p.pe, pe_sA[idx])])
                nel_mx[idx] = p.nel
                idx += 1

        for g in range(n_g):
            for j in range(4):
                def trj(t, g=g, j=j):
                    return t.transpose(p.pst[1][0:1, 0:128],
                                       rm[:, 4 * g + j:4 * g + j + 1], identf[:, :])
                p.tensor(trj)
                p.vec(lambda v: v.tensor_copy(rmrow[:, :], p.pst[1][0:1, 0:128]))

                def bcj(t, j=j):
                    return t.matmul(p.misc1[:, j * 128:(j + 1) * 128],
                                    lnsc3["ones_row"], rmrow[:, :],
                                    start=True, stop=True)
                p.tensor(bcj)
            p.vec(lambda v: v.tensor_copy(nmbc[:, :], p.misc1[:, 0:512]))

            pe_sB = {}
            pe_av = {}
            nel_sub = {}
            for ck in range(16 * g + 16):
                d = ck - 16 * g
                bank = p.ps[ck % 2]
                atts = att[:, (ck % 2) * WS:(ck % 2 + 1) * WS]
                attbs = attb[:, (ck % 2) * WS:(ck % 2 + 1) * WS]

                def sB(t, bank=bank, g=g, ck=ck):
                    t.wait_ge(p.el, nel_sub[ck - 2] if ck >= 2 else p.nel)
                    return t.matmul(bank[:, 0:WS], Kh[:, ck * 128:(ck + 1) * 128],
                                    Qown[:, g * 512:(g + 1) * 512],
                                    start=True, stop=True)
                p.tensor(sB, wait_el=False)
                pe_sB[ck] = p.npe

                def sub(v, bank=bank, d=d, atts=atts):
                    v.tensor_add(atts, bank[:, 0:WS], nmbc[:, :])
                    if d >= 0:
                        v.tensor_add(atts, atts,
                                     masks[:, d * 512:(d + 1) * 512])
                p.vec(sub, wait_pe=False, wait_dma=[(p.pe, pe_sB[ck])])
                nel_sub[ck] = p.nel

                def ex(a, atts=atts, attbs=attbs):
                    a.activation(attbs, atts, AF.Exp)
                p.act(ex, wait_pe=False,
                      wait_dma=([(p.pe, pe_av[ck - 2])] if ck >= 2 else None))

                def av(t, g=g, ck=ck, attbs=attbs):
                    return t.matmul(p.pst[0][0:65, 0:WS],
                                    vTh[:, ck * 65:(ck + 1) * 65], attbs,
                                    start=(ck == 0), stop=(ck == 16 * g + 15))
                p.tensor(av)
                pe_av[ck] = p.npe

            def nrm(v):
                v.reciprocal(rbc[0:1, :], p.pst[0][64:65, 0:WS])
            p.vec(nrm)

            def nbc(t):
                return t.matmul(p.misc2[0:64, 0:WS], lnsc3["ones_row"][:, 0:64],
                                rbc[0:1, :], start=True, stop=True)
            p.tensor(nbc)

            def cmul(v):
                v.tensor_copy(rbc[0:64, :], p.misc2[0:64, 0:WS])
                v.tensor_mul(att[0:64, 0:WS], p.pst[0][0:64, 0:WS], rbc[0:64, :])
                v.tensor_copy(ctxb[:, :], att[0:64, 0:WS])
            p.vec(cmul)
            ps5 = p.psum()

            def wof(t, ps=ps5):
                return t.matmul(ps[0:64, 0:WS], wsm["wot"][:, :], ctxb[:, :],
                                start=True, stop=True)
            p.tensor(wof)
            p.vec(lambda v, ps=ps5, g=g, hc=hc, hr=hr:
                  v.tensor_copy(ao[hr:hr + 64, hc * TD + g * 512: hc * TD + g * 512 + WS],
                                ps[0:64, 0:WS]))

    p.el_barrier([])
    for c in range(8):
        p.dma_s(ao_hbm.ap()[:, c * TD:(c + 1) * TD], ao[:, c * TD:(c + 1) * TD],
                wait=[(p.el, p.nel)])
    p.el_barrier([(p.qs, p.nqs)])
    nc.leave_named_scope("p3b_attn", _sc3b, False)
    p.stack = stack
    p3.close()

    _sc4 = nc.enter_named_scope("p4_moe", False)[0]
    emit_moe(nc, stack, block, p, din, outT, x2own_hbm, ao_hbm, TD, T,
             w1bf_hbm, wocbf_hbm, nqg_cast)
    nc.leave_named_scope("p4_moe", _sc4, False)

    @block.sync
    def _(s):
        s.wait_ge(p.qs, p.nqs)

    @block.gpsimd
    def _(g):
        g.wait_ge(p.qg, p.nqg)

    nc._stack = stack
    return nc


def emit_moe(nc, stack, block, p, din, outT, x2own_hbm, ao_hbm, TD, T,
             w1bf_hbm, wocbf_hbm, nqg_cast):
    W = 512
    nslab = TD // W
    p4 = contextlib.ExitStack()
    outer = p.stack
    p.stack = p4

    def sb4(name, shape, dtype=F32):
        return p4.enter_context(nc.sbuf_tensor(name, shape, dtype))

    onesc = sb4("onesc4", [128, 1], F32)
    onesr = sb4("onesr4", [1, 128], F32)
    onescb = sb4("onescb4", [128, 1], BF16)
    wrc = sb4("s_wrc", [128, 8 * 4], BF16)
    b1c = sb4("s_b1cD", [128, 4 * 32], F32)
    woc = sb4("s_woc", [128, 32 * 1024], BF16)
    boc = sb4("s_boc", [128, 8], F32)
    gam = sb4("s_gam", [128, 8], F32)
    bet = sb4("s_bet", [128, 8], F32)
    p.dma_s(onesc[:, :], din["onesc"], wait=[(p.el, p.nel)])
    p.dma_s(onesr[:, :], din["onesr"])
    p.dma_s(b1c[:, :], din["b1cD"])
    p.dma_s(boc[:, :], din["boc"])
    p.dma_s(gam[:, :], din["gam"])
    p.dma_s(bet[:, :], din["bet"])
    p.dma_g(wrc[:, :], din["wrc"], wait=[(p.el, p.nel)])
    p.dma_s(woc[:, :], wocbf_hbm.ap(), wait=[(p.qg, p.nqg), (p.el, p.nel)])
    nq_s, nq_g = p.nqs, p.nqg

    x2s = sb4("t_x2_4", [128, 8 * W], F32)
    ats = sb4("t_at_4", [128, 8 * W], BF16)
    x3s = sb4("t_x3_4", [128, 8 * W], F32)
    xn3 = sb4("t_xn3_4", [128, 8 * W], BF16)
    inter = sb4("t_inter4", [128, 32 * W], BF16)
    hs = sb4("t_hs4", [128, 2 * W], BF16)
    ys = x2s
    lgr = sb4("t_lgr4", [1, 4 * W], F32)
    mxr = sb4("t_mxr4", [1, W], F32)
    mrow = sb4("t_mrow4", [1, 4 * W], F32)
    accm = sb4("t_accm4", [1, W], F32)
    mbc = sb4("t_mbc4", [128, W], BF16)
    w1b = [sb4(f"t_w1b4{i}", [128, 1024], BF16) for i in range(3)]
    lnsc = make_ln_scratch(p, W, tag="4")
    lnsc["ones_col"] = onesc[:, :]
    lnsc["ones_col_b"] = onescb[:, :]
    lnsc["ones_row"] = onesr[:, :]

    def ch(tile, c, w=W):
        return tile[:, c * w:(c + 1) * w]

    p.el_barrier([(p.qs, nq_s), (p.qg, nq_g)])
    p.vec(lambda v: v.memset(onescb[:, :], 1.0))

    for s in range(nslab):
        col0 = s * W
        for c in range(8):
            p.dma_s(ch(x2s, c),
                    x2own_hbm.ap()[:, c * TD + col0: c * TD + col0 + W],
                    wait=([(p.el, p.nel)] if c == 0 else None))
        for c in range(8):
            p.dma_s(ch(ats, c), ao_hbm.ap()[:, c * TD + col0: c * TD + col0 + W],
                    wait=([(p.el, p.nel)] if c == 0 else None))
        p.el_barrier([(p.qs, p.nqs), (p.qg, p.nqg)])

        emit_ln(p, lambda c: ch(ats, c), lambda c: ch(x3s, c), W, lnsc,
                  x_full=ats[:, 0:8 * W], out_full=x3s[:, 0:8 * W])

        def r1(v):
            v.tensor_add(x3s[:, 0:8 * W], x3s[:, 0:8 * W], x2s[:, 0:8 * W])
        p.vec(r1)

        emit_ln(p, lambda c: ch(x3s, c), lambda c: ch(xn3, c), W, lnsc,
                  x_full=x3s[:, 0:8 * W], out_full=xn3[:, 0:8 * W])

        for e in range(4):
            def lg(t, e=e):
                last = None
                for k in range(8):
                    last = t.matmul(p.misc1[0:1, 0:W],
                                    wrc[:, k * 4 + e: k * 4 + e + 1],
                                    ch(xn3, k), start=(k == 0), stop=(k == 7))
                return last
            p.tensor(lg)
            p.vec(lambda v, e=e: v.tensor_copy(lgr[:, e * W:(e + 1) * W],
                                               p.misc1[0:1, 0:W]))

        def am(v):
            v.tensor_tensor(mxr[:, :], lgr[:, 0:W], lgr[:, W:2 * W], ALU.max)
            v.tensor_tensor(mxr[:, :], mxr[:, :], lgr[:, 2 * W:3 * W], ALU.max)
            v.tensor_tensor(mxr[:, :], mxr[:, :], lgr[:, 3 * W:4 * W], ALU.max)
            v.memset(accm[:, :], 0.0)
            for e in range(4):
                sl = slice(e * W, (e + 1) * W)
                v.tensor_tensor(mrow[:, sl], lgr[:, sl], mxr[:, :], ALU.is_ge)
                v.scalar_tensor_tensor(mrow[:, sl], accm[:, :], -1.0,
                                       mrow[:, sl], ALU.mult, ALU.add)
                v.tensor_scalar_max(mrow[:, sl], mrow[:, sl], 0.0)
                v.tensor_add(accm[:, :], accm[:, :], mrow[:, sl])
        p.vec(am)
        ge_done = {}
        nel_preexp = p.nel

        for e in range(4):
            def mb(t, e=e):
                return t.matmul(p.misc2[:, 0:W], onesr[:, :],
                                mrow[:, e * W:(e + 1) * W], start=True, stop=True)
            p.tensor(mb)
            p.vec(lambda v: v.tensor_copy(mbc[:, :], p.misc2[:, 0:W]))

            for i in range(32):
                j = e * 32 + i
                buf = w1b[j % 3]
                wl = [(p.qs, p.nqs), (p.el, ge_done.get(j - 3, nel_preexp))]
                if j == 0 and s == 0:
                    wl = [(p.qg, p.nqg), (p.qs, p.nqs), (p.el, p.nel)]
                p.dma_s(buf[:, :],
                        w1bf_hbm.ap()[:, j * 1024:(j + 1) * 1024], wait=wl)
                wt = p.nqs
                ps = p.psum()
                wl2 = [(p.qs, wt), (p.el, ge_done.get(j - 2, nel_preexp))]

                def mm1(t, buf=buf, ps=ps, wl2=wl2):
                    for sem, cnt in wl2:
                        t.wait_ge(sem, cnt)
                    last = None
                    for k in range(8):
                        last = t.matmul(ps[:, 0:W], buf[:, k * 128:(k + 1) * 128],
                                        ch(xn3, k), start=(k == 0), stop=(k == 7))
                    return last
                p.tensor(mm1, wait_el=False)

                def ge(a, e=e, i=i, ps=ps):
                    a.activation(hs[:, (i % 2) * W:(i % 2 + 1) * W], ps[:, 0:W],
                                 AF.Gelu, bias=b1c[:, e * 32 + i: e * 32 + i + 1])
                p.act(ge)

                def acc(v, e=e, i=i):
                    h1 = hs[:, (i % 2) * W:(i % 2 + 1) * W]
                    if e == 0:
                        v.tensor_mul(ch(inter, i), h1, mbc[:, :])
                    else:
                        v.tensor_mul(h1, h1, mbc[:, :])
                        v.tensor_add(ch(inter, i), ch(inter, i), h1)
                p.vec(acc)
                ge_done[j] = p.nel

        for m in range(8):
            ps = p.psum()

            def mo(t, m=m, ps=ps):
                last = None
                for k in range(32):
                    last = t.matmul(ps[:, 0:W],
                                    woc[:, k * 1024 + m * 128: k * 1024 + m * 128 + 128],
                                    ch(inter, k), start=(k == 0), stop=(k == 31))
                return last
            p.tensor(mo)

            def ye(v, m=m, ps=ps):
                v.tensor_scalar_add(ch(ys, m), ps[:, 0:W], boc[:, m:m + 1])
                v.tensor_add(ch(ys, m), ch(ys, m), ch(xn3, m))
            p.vec(ye)

        y2 = inter[:, 0:16 * W].bitcast(F32)
        emit_ln(p, lambda c: ch(ys, c), lambda c: ch(y2, c), W, lnsc,
                  eps=LN_EPS_IN, x_full=ys[:, 0:8 * W], out_full=y2[:, 0:8 * W])

        def gb(v):
            y3 = y2[:, 0:8 * W].rearrange("p (c w) -> p c w", c=8)
            g3 = gam[:, 0:8].unsqueeze(2).broadcast_to([128, 8, W])
            b3 = bet[:, 0:8].unsqueeze(2).broadcast_to([128, 8, W])
            v.tensor_tensor(y3, y3, g3, ALU.mult)
            v.tensor_tensor(y3, y3, b3, ALU.add)
        p.vec(gb)

        emit_ln(p, lambda c: ch(y2, c), lambda c: ch(y2, c), W, lnsc,
                  x_full=y2[:, 0:8 * W], out_full=y2[:, 0:8 * W])

        def r2(v):
            v.tensor_add(y2[:, 0:8 * W], y2[:, 0:8 * W], x3s[:, 0:8 * W])
        p.vec(r2)

        for c in range(8):
            p.dma_s(outT[:, c * TD + col0: c * TD + col0 + W], ch(y2, c),
                    wait=([(p.el, p.nel)] if c == 0 else None))
    p.stack = outer
    p4.close()



_BUILT = {}


def _get(name, fn):
    if name not in _BUILT:
        _BUILT[name] = fn()
    return _BUILT[name]


def kernel(**inputs):
    d = {k: np.asarray(v) for k, v in inputs.items()}
    x = np.asarray(d["L0_attn_x"], np.float32)
    B, T, DM = x.shape
    ncF = _get("F", lambda: build_F(T=T))
    in_maps = [f_inputs(x[c // 4], d, T, c % 4) for c in range(8)]
    res = run_bass_kernel_spmd(ncF, in_maps, core_ids=list(range(8)))
    outs = [res.results[c]["outT"] for c in range(8)]
    return f_reassemble(outs, T)


# version 06
# speedup vs baseline: 1.5789x; 1.1739x over previous
import sys
import contextlib
import numpy as np
import concourse.bass as bass
import concourse.mybir as mybir
from concourse.bass_utils import run_bass_kernel_spmd



F32 = mybir.dt.float32
BF16 = mybir.dt.bfloat16
I32 = mybir.dt.int32
AF = mybir.ActivationFunctionType
ALU = mybir.AluOpType
AX = mybir.AxisListType.X


class Prog:
    def __init__(self, nc, block, stack):
        self.nc, self.block, self.stack = nc, block, stack
        self.pe = stack.enter_context(nc.semaphore("pe"))
        self.el = stack.enter_context(nc.semaphore("el"))
        self.qs = stack.enter_context(nc.semaphore("qs"))
        self.qg = stack.enter_context(nc.semaphore("qg"))
        self.ea = stack.enter_context(nc.semaphore("ea"))
        self.nea = 0
        self.npe = 0
        self.nel = 0
        self.nqs = 0
        self.nqg = 0
        self.ps = []
        self.ps_small = []

    def alloc_psum(self, n_big=4):
        self.ps = [self.stack.enter_context(
            self.nc.psum_tensor(f"ps{i}", [128, 512], F32)) for i in range(n_big)]
        self.pst = [self.stack.enter_context(
            self.nc.psum_tensor(f"pst{i}", [128, 512], F32)) for i in range(2)]
        self.misc1 = self.stack.enter_context(
            self.nc.psum_tensor("psm1", [128, 512], F32))
        self.misc2 = self.stack.enter_context(
            self.nc.psum_tensor("psm2", [128, 512], F32))
        self._pi = 0
        self.bar = self.stack.enter_context(
            self.nc.sbuf_tensor("elbar", [1, 1], F32))

    def psum(self):
        t = self.ps[self._pi % len(self.ps)]
        self._pi += 1
        return t

    def el_barrier(self, waits):
        @self.block.vector
        def _(v):
            for sem, cnt in waits:
                v.wait_ge(sem, cnt)
            r = v.memset(self.bar[:, :], 0.0)
            r._wait_ge(self.el, self.nel)
            r.then_inc(self.el, 1)
        self.nel += 1

    def sb(self, name, shape, dtype=F32):
        return self.stack.enter_context(self.nc.sbuf_tensor(name, shape, dtype))

    class _E:
        def __init__(self, p, eng, pending):
            object.__setattr__(self, "_p", p)
            object.__setattr__(self, "_e", eng)
            object.__setattr__(self, "_pend", pending)

        def __getattr__(self, name):
            f = getattr(self._e, name)
            if not callable(f) or name.startswith("wait"):
                return f
            p = self._p

            def wrapped(*a, **k):
                for sem, cnt in self._pend:
                    self._e.wait_ge(sem, cnt)
                del self._pend[:]
                r = f(*a, **k)
                r._wait_ge(p.el, p.nel)
                r.then_inc(p.el, 1)
                p.nel += 1
                return r
            return wrapped

    def _esec(self, blockdec, fn, wait_pe, wait_dma):
        pending = []
        if wait_pe:
            pending.append((self.pe, self.npe))
        if wait_dma:
            pending.extend(wait_dma)

        @blockdec
        def _(e):
            fn(Prog._E(self, e, pending))

    def vec(self, fn, wait_pe=True, wait_dma=None):
        self._esec(self.block.vector, fn, wait_pe, wait_dma)

    def act(self, fn, wait_pe=True, wait_dma=None):
        self._esec(self.block.scalar, fn, wait_pe, wait_dma)

    def pool(self, fn, wait_pe=True, wait_dma=None):
        self._esec(self.block.gpsimd, fn, wait_pe, wait_dma)

    def tensor(self, fn, wait_el=True, wait_dma=None, extra_el=None):
        prog = self

        class _T:
            def __init__(self):
                self.first = True
                self.front = []

            def _pre(self):
                if self.first:
                    self.first = False
                    if wait_dma:
                        for sem, cnt in wait_dma:
                            self._t.wait_ge(sem, cnt)
                    if wait_el:
                        self.front.append((prog.el, prog.nel))

            def _att(self, r):
                for sem, cnt in self.front[1:]:
                    pass
                if self.front:
                    r._wait_ge(*self.front[-1])
                    del self.front[:]
                return r

            def matmul(self, *a, **k):
                self._pre()
                for sem, cnt in self.front[:-1]:
                    self._t.wait_ge(sem, cnt)
                    self.front.remove((sem, cnt))
                return self._att(self._t.matmul(*a, **k))

            def transpose(self, *a, **k):
                self._pre()
                for sem, cnt in self.front[:-1]:
                    self._t.wait_ge(sem, cnt)
                    self.front.remove((sem, cnt))
                return self._att(self._t.transpose(*a, **k))

            def wait_ge(self, sem, cnt):
                self.front.append((sem, cnt))
                return None

        @self.block.tensor
        def _(t):
            w = _T()
            w._t = t
            last = fn(w)
            last.then_inc(self.pe, 1)
        self.npe += 1

    def dma_s(self, out_ap, in_ap, wait=None):
        @self.block.sync
        def _(s):
            if wait:
                for sem, cnt in wait[:-1]:
                    s.wait_ge(sem, cnt)
            r = s.dma_start(out=out_ap, in_=in_ap)
            if wait:
                r._wait_ge(*wait[-1])
            r.then_inc(self.qs, 16)
        self.nqs += 16

    def dma_g(self, out_ap, in_ap, wait=None):
        @self.block.gpsimd
        def _(g):
            if wait:
                for sem, cnt in wait[:-1]:
                    g.wait_ge(sem, cnt)
            r = g.dma_start(out=out_ap, in_=in_ap)
            if wait:
                r._wait_ge(*wait[-1])
            r.then_inc(self.qg, 16)
        self.nqg += 16


def emit_ln(p: Prog, x_of, out_of, W, sc, eps=1e-5, nch=8, nfeat=None,
            out_chunks=None, extra_mul=None, x_full=None, out_full=None):
    nfeat = nfeat or nch * 128
    out_chunks = out_chunks if out_chunks is not None else range(nch)

    x_dt = x_of(0).dtype
    sc = dict(sc)
    sc["psA"] = sc["psm1"][0:1, 0:W]
    sc["psB"] = sc["psm2"][0:1, 0:W]
    sq = sc["sq"]
    assert sq.shape[1] == 8 * W, (sq.shape, W)
    sqb = sq[:, 0:4 * W].bitcast(mybir.dt.bfloat16)
    xbb = sq[:, 4 * W:8 * W].bitcast(mybir.dt.bfloat16)

    def sq_fn(a):
        if x_full is not None:
            a.activation(sqb, x_full, AF.Square)
        else:
            for c in range(nch):
                a.activation(sqb[:, c * W:(c + 1) * W], x_of(c), AF.Square)
    p.act(sq_fn)

    if x_dt == mybir.dt.bfloat16:
        xb_of = x_of
    else:
        def cast_fn(v):
            if x_full is not None:
                v.tensor_copy(xbb, x_full)
            else:
                for c in range(nch):
                    v.tensor_copy(xbb[:, c * W:(c + 1) * W], x_of(c))
        p.vec(cast_fn)
        xb_of = lambda c: xbb[:, c * W:(c + 1) * W]

    onesb = sc["ones_col_b"]

    def mm_fn(t):
        for c in range(nch):
            t.matmul(sc["psA"], onesb, sqb[:, c * W:(c + 1) * W],
                     start=(c == 0), stop=(c == nch - 1))
        last = None
        for c in range(nch):
            last = t.matmul(sc["psB"], onesb, xb_of(c),
                            start=(c == 0), stop=(c == nch - 1))
        return last
    p.tensor(mm_fn)

    inv_n = 1.0 / nfeat

    def stats(v):
        v.tensor_scalar_mul(sc["r1"][:, 0:W], sc["psB"], inv_n)
        v.tensor_mul(sc["r2"][:, 0:W], sc["r1"][:, 0:W], sc["r1"][:, 0:W])
        v.tensor_scalar(sc["r3"][:, 0:W], sc["psA"], inv_n, eps,
                        ALU.mult, ALU.add)
        v.tensor_sub(sc["r3"][:, 0:W], sc["r3"][:, 0:W], sc["r2"][:, 0:W])
    p.vec(stats)

    def rstd(a):
        a.activation(sc["r2"][:, 0:W], sc["r3"][:, 0:W], AF.Ln)
        a.activation(sc["r3"][:, 0:W], sc["r2"][:, 0:W], AF.Exp, scale=-0.5)
    p.act(rstd)

    p.vec(lambda v: v.tensor_mul(sc["r1"][:, 0:W], sc["r1"][:, 0:W],
                                 sc["r3"][:, 0:W]))

    psC1 = p.psum()
    psC2 = p.psum()

    def bc(t):
        t.matmul(psC1[:, 0:W], sc["ones_row"], sc["r3"][:, 0:W],
                 start=True, stop=True)
        return t.matmul(psC2[:, 0:W], sc["ones_row"], sc["r1"][:, 0:W],
                        start=True, stop=True)
    p.tensor(bc)

    def evacbc(v):
        v.tensor_copy(sc["b1"][:, 0:W], psC1[:, 0:W])
        v.tensor_copy(sc["b2"][:, 0:W], psC2[:, 0:W])
        if x_full is not None and out_full is not None:
            x3 = x_full.rearrange("p (c w) -> p c w", c=nch)
            o3 = out_full.rearrange("p (c w) -> p c w", c=nch)
            t3 = sq[:, :].rearrange("p (c w) -> p c w", c=nch)
            b1b = sc["b1"][:, 0:W].unsqueeze(1).broadcast_to([128, nch, W])
            b2b = sc["b2"][:, 0:W].unsqueeze(1).broadcast_to([128, nch, W])
            v.tensor_tensor(t3, x3, b1b, ALU.mult)
            v.tensor_tensor(o3, t3, b2b, ALU.subtract)
        else:
            for c in out_chunks:
                v.tensor_mul(sq[:, c * W:(c + 1) * W], x_of(c), sc["b1"][:, 0:W])
                v.tensor_sub(out_of(c), sq[:, c * W:(c + 1) * W], sc["b2"][:, 0:W])
    p.vec(evacbc)


def make_ln_scratch(p: Prog, W, tag=""):
    sc = {}
    sc["sq"] = p.sb("lnsq" + tag, [128, 8 * W], F32)[:, :]
    sc["r1"] = p.sb("lnr1" + tag, [1, W], F32)[:, :]
    sc["r2"] = p.sb("lnr2" + tag, [1, W], F32)[:, :]
    sc["r3"] = p.sb("lnr3" + tag, [1, W], F32)[:, :]
    sc["b1"] = p.sb("lnb1" + tag, [128, W], F32)[:, :]
    sc["b2"] = p.sb("lnb2" + tag, [128, W], F32)[:, :]
    sc["psm1"] = p.misc1
    sc["psm2"] = p.misc2
    return sc


def emit_mm(p: Prog, out_of, lhsT_of, rhs_of, kch, mch, W, evac,
            nsplit=1):
    nel_ev = {-2: p.nel, -1: p.nel}
    for m in range(mch):
        ps = p.psum()

        def mm_fn(t, m=m, ps=ps):
            t.wait_ge(p.el, nel_ev[m - 2])
            last = None
            for k in range(kch):
                last = t.matmul(ps[:, 0:W], lhsT_of(k, m), rhs_of(k),
                                start=(k == 0), stop=(k == kch - 1))
            return last
        p.tensor(mm_fn, wait_el=False)
        evac(m, ps)
        nel_ev[m] = p.nel


F32 = mybir.dt.float32
BF16 = mybir.dt.bfloat16
ALU = mybir.AluOpType


class CM:

    def __init__(self, re, im, nim=None, n=0):
        self.re, self.im, self.nim, self.n = re, im, nim, n


class Inv:
    def __init__(self, nc, block, stack, ident128_f32, psum_from=None):
        self.nc, self.block, self.stack = nc, block, stack
        self.ident = ident128_f32
        self.pe = stack.enter_context(nc.semaphore("ipe"))
        self.dv = stack.enter_context(nc.semaphore("idv"))
        self.npe = 0
        self.ndv = 0
        if psum_from is None:
            self.ps = [stack.enter_context(nc.psum_tensor(f"ips{i}", [128, 512], F32))
                       for i in range(4)]
            self.pst = [stack.enter_context(nc.psum_tensor(f"ipst{i}", [128, 512], F32))
                        for i in range(2)]
        else:
            self.ps = list(psum_from.ps)
            self.pst = list(psum_from.pst)
        self._pi = 0

    def alloc(self, name, n, nim=False):
        nch = max(1, n // 128)
        re = self.stack.enter_context(self.nc.sbuf_tensor(name + "r", [128, nch * n], F32))
        im = self.stack.enter_context(self.nc.sbuf_tensor(name + "i", [128, nch * n], F32))
        ni = self.stack.enter_context(self.nc.sbuf_tensor(name + "n", [128, nch * n], F32)) if nim else None
        return CM(re, im, ni, n)

    def _psum2(self):
        a = self.ps[self._pi % 4]
        b = self.ps[(self._pi + 1) % 4]
        self._pi += 2
        return a, b

    class _SyncV:

        def __init__(self, inv, v, pe_wait):
            object.__setattr__(self, "_inv", inv)
            object.__setattr__(self, "_v", v)
            object.__setattr__(self, "_pw", [pe_wait])

        def __getattr__(self, name):
            f = getattr(self._v, name)
            if not callable(f) or name.startswith("wait"):
                return f
            inv = self._inv

            def wrapped(*a, **k):
                if self._pw[0] is not None:
                    self._v.wait_ge(inv.pe, self._pw[0])
                    self._pw[0] = None
                r = f(*a, **k)
                r._wait_ge(inv.dv, inv.ndv)
                r.then_inc(inv.dv, 1)
                inv.ndv += 1
                return r
            return wrapped

    def dve(self, fn, k=None):
        @self.block.vector
        def _(v):
            fn(Inv._SyncV(self, v, self.npe))

    def make_nim(self, M: CM):
        self.dve(lambda v: v.tensor_scalar_mul(M.nim[:, :], M.im[:, :], -1.0))

    def cmm(self, out: CM, A: CM, B: CM, n, a_r0=0, a_c0=0, b_r0=0, b_c0=0,
            o_r0=0, o_c0=0, post=None, nA=None, nB=None, nO=None):
        nA = nA or A.n
        nB = nB or B.n
        nO = nO or out.n
        nch = n // 128
        for mc in range(nch):
            psr, psi = self._psum2()
            m0 = a_c0 + mc * 128

            @self.block.tensor
            def _(t, psr=psr, psi=psi, m0=m0):
                first = [True]
                _mm = t.matmul

                def matmul_w(*a, **k):
                    r = _mm(*a, **k)
                    if first[0]:
                        first[0] = False
                        r._wait_ge(self.dv, self.ndv)
                    return r
                t = type("TW", (), {"matmul": staticmethod(matmul_w)})()

                def aps(kc):
                    ac = (a_r0 + kc * 128) // 128
                    bc = (b_r0 + kc * 128) // 128
                    return (A.re[:, ac * nA + m0: ac * nA + m0 + 128],
                            A.im[:, ac * nA + m0: ac * nA + m0 + 128],
                            A.nim[:, ac * nA + m0: ac * nA + m0 + 128],
                            B.re[:, bc * nB + b_c0: bc * nB + b_c0 + n],
                            B.im[:, bc * nB + b_c0: bc * nB + b_c0 + n])
                last = None
                for kc in range(nch):
                    a_re, a_im, a_ni, b_re, b_im = aps(kc)
                    t.matmul(psr[:, 0:n], a_re, b_re, start=(kc == 0), stop=False)
                for kc in range(nch):
                    a_re, a_im, a_ni, b_re, b_im = aps(kc)
                    t.matmul(psr[:, 0:n], a_ni, b_im, start=False,
                             stop=(kc == nch - 1))
                for kc in range(nch):
                    a_re, a_im, a_ni, b_re, b_im = aps(kc)
                    t.matmul(psi[:, 0:n], a_re, b_im, start=(kc == 0), stop=False)
                for kc in range(nch):
                    a_re, a_im, a_ni, b_re, b_im = aps(kc)
                    last = t.matmul(psi[:, 0:n], a_im, b_re, start=False,
                                    stop=(kc == nch - 1))
                last.then_inc(self.pe, 1)
            self.npe += 1

            oc = (o_r0 // 128) + mc
            o_re = out.re[:, oc * nO + o_c0: oc * nO + o_c0 + n]
            o_im = out.im[:, oc * nO + o_c0: oc * nO + o_c0 + n]

            if post is None:
                def post_(v, psr, psi, o_re, o_im, mc):
                    v.tensor_copy(o_re, psr)
                    return v.tensor_copy(o_im, psi)
            else:
                post_ = post
            self.dve(lambda v, psr=psr, psi=psi, o_re=o_re, o_im=o_im, mc=mc:
                     post_(v, psr[:, 0:n], psi[:, 0:n], o_re, o_im, mc))

    def ctrans(self, out: CM, src: CM, n, s_r0=0, s_c0=0, o_r0=0, o_c0=0,
               nS=None, nO=None, negate_to_nim=True):
        nS = nS or src.n
        nO = nO or out.n
        nch = n // 128
        for rc in range(nch):
            for cc in range(nch):
                sc = (s_r0 + rc * 128) // 128
                s_re = src.re[:, sc * nS + s_c0 + cc * 128: sc * nS + s_c0 + cc * 128 + 128]
                s_im = src.im[:, sc * nS + s_c0 + cc * 128: sc * nS + s_c0 + cc * 128 + 128]
                pr = self.pst[0]
                pi = self.pst[1]

                @self.block.tensor
                def _(t, s_re=s_re, s_im=s_im, pr=pr, pi=pi):
                    r = t.transpose(pr[:, 0:128], s_re, self.ident)
                    r._wait_ge(self.dv, self.ndv)
                    t.transpose(pi[:, 0:128], s_im, self.ident).then_inc(self.pe, 1)
                self.npe += 1

                oc = (o_r0 + cc * 128) // 128
                o_re = out.re[:, oc * nO + o_c0 + rc * 128: oc * nO + o_c0 + rc * 128 + 128]
                o_im = out.im[:, oc * nO + o_c0 + rc * 128: oc * nO + o_c0 + rc * 128 + 128]
                if negate_to_nim and out.nim is not None:
                    o_ni = out.nim[:, oc * nO + o_c0 + rc * 128: oc * nO + o_c0 + rc * 128 + 128]

                    def post(v, o_re=o_re, o_im=o_im, o_ni=o_ni, pr=pr, pi=pi):
                        v.tensor_copy(o_re, pr[:, 0:128])
                        v.tensor_copy(o_im, pi[:, 0:128])
                        return v.tensor_scalar_mul(o_ni, pi[:, 0:128], -1.0)
                else:
                    def post(v, o_re=o_re, o_im=o_im, pr=pr, pi=pi):
                        v.tensor_copy(o_re, pr[:, 0:128])
                        return v.tensor_copy(o_im, pi[:, 0:128])
                self.dve(post)

    def newton(self, Wt: CM, X: CM, Xs: CM, Xt: CM, M: CM, twoI,
               n, iters, w_r0=0, alpha_bc=None):
        nch = n // 128
        if alpha_bc is not None:
            for c in range(nch):
                wc = (w_r0 + c * 128) // 128
                src_re = Wt.re[:, wc * Wt.n + w_r0: wc * Wt.n + w_r0 + n]
                src_im = Wt.im[:, wc * Wt.n + w_r0: wc * Wt.n + w_r0 + n]
                dst_re = X.re[:, c * n: c * n + n]
                dst_im = X.im[:, c * n: c * n + n]

                def init(v, src_re=src_re, src_im=src_im, dst_re=dst_re, dst_im=dst_im):
                    v.tensor_scalar_mul(dst_re, src_re, alpha_bc)
                    return v.tensor_scalar(dst_im, src_im, alpha_bc, -1.0,
                                           ALU.mult, ALU.mult)
                self.dve(init)

        cur, other = X, Xs
        for it in range(iters):
            def post_m(v, psr, psi, o_re, o_im, mc):
                v.scalar_tensor_tensor(o_re, psr, -1.0,
                                       twoI[:, mc * n:(mc + 1) * n], ALU.mult, ALU.add)
                return v.tensor_scalar_mul(o_im, psi, -1.0)
            self.cmm(M, Wt, cur, n, a_r0=w_r0, a_c0=w_r0, post=post_m,
                     nA=Wt.n, nB=n, nO=n)
            self.ctrans(Xt, cur, n, nS=n, nO=n, negate_to_nim=True)
            self.cmm(other, Xt, M, n, nA=n, nB=n, nO=n)
            cur, other = other, cur
        return cur


def emit_alpha(inv: Inv, M: CM, r0, c0, n, nM, alpha_bc, scratch, ones_col,
               ones_row, tiny11, ps11, ps128):
    nch = n // 128
    acc = scratch

    def emit(v):
        first = True
        last = None
        for plane in (M.re, M.im):
            for c in range(nch):
                mc = (r0 + c * 128) // 128
                blk = plane[:, mc * nM + c0: mc * nM + c0 + n]
                v.tensor_mul(scratch[:, 2:2 + n], blk, blk)
                if first:
                    last = v.reduce_sum(scratch[:, 0:1], scratch[:, 2:2 + n],
                                        axis=mybir.AxisListType.X)
                    first = False
                else:
                    v.reduce_sum(scratch[:, 1:2], scratch[:, 2:2 + n],
                                 axis=mybir.AxisListType.X)
                    last = v.tensor_add(scratch[:, 0:1], scratch[:, 0:1],
                                        scratch[:, 1:2])
        return last
    inv.dve(emit)

    @inv.block.tensor
    def _(t):
        r = t.matmul(ps11[:, :], scratch[:, 0:1], ones_col, start=True, stop=True)
        r._wait_ge(inv.dv, inv.ndv)
        r.then_inc(inv.pe, 1)
    inv.npe += 1

    def emit2(v):
        return v.reciprocal(tiny11[:, :], ps11[:, :])
    inv.dve(emit2)

    @inv.block.tensor
    def _(t):
        r = t.matmul(ps128[:, :], ones_row, tiny11[:, :], start=True, stop=True)
        r._wait_ge(inv.dv, inv.ndv)
        r.then_inc(inv.pe, 1)
    inv.npe += 1

    inv.dve(lambda v: v.tensor_copy(alpha_bc, ps128[:, :]))


def emit_inv256(inv: Inv, N: CM, Nt: CM, r0, out: CM, outT: CM, bufs, consts,
                leaf_iters=30):
    XA, XS = bufs["X128a"], bufs["X128b"]
    Xs, Xt, M128 = bufs["Xs128"], bufs["Xt128"], bufs["M128"]
    P_, Pt, Q, S, St, SiQ = bufs["P128"], bufs["Pt128"], bufs["Q128"], \
        bufs["S128"], bufs["St128"], bufs["SiQ128"]
    a0, d0 = r0, r0 + 128

    emit_alpha(inv, Nt, a0, a0, 128, Nt.n, bufs["alpha"], bufs["scr"],
               consts["ones_col"], consts["ones_row"], bufs["t11"],
               bufs["ps11"], bufs["ps128"])
    Ai = inv.newton(Nt, XA, Xs, Xt, M128, consts["twoI128"], 128, leaf_iters,
                    w_r0=a0, alpha_bc=bufs["alpha"][:, 0:1])
    if Ai is not XA:
        inv.dve(lambda v: (v.tensor_copy(XA.re[:, :], Ai.re[:, :]),
                           v.tensor_copy(XA.im[:, :], Ai.im[:, :]))[-1])
    AiT = bufs["AiT128"]
    inv.ctrans(AiT, XA, 128)
    inv.cmm(P_, AiT, N, 128, b_r0=a0, b_c0=d0, nB=N.n)
    Dre = lambda mc: N.re[:, (d0 // 128) * N.n + d0: (d0 // 128) * N.n + d0 + 128]
    Dim = lambda mc: N.im[:, (d0 // 128) * N.n + d0: (d0 // 128) * N.n + d0 + 128]

    def post_s(v, psr, psi, o_re, o_im, mc):
        v.scalar_tensor_tensor(o_re, psr, -1.0, Dre(mc), ALU.mult, ALU.add)
        return v.scalar_tensor_tensor(o_im, psi, -1.0, Dim(mc), ALU.mult, ALU.add)
    inv.cmm(S, Nt, P_, 128, a_r0=a0, a_c0=d0, post=post_s, nA=Nt.n, nB=128, nO=128)
    inv.ctrans(St, S, 128)
    emit_alpha(inv, St, 0, 0, 128, 128, bufs["alpha"], bufs["scr"],
               consts["ones_col"], consts["ones_row"], bufs["t11"],
               bufs["ps11"], bufs["ps128"])
    Si = inv.newton(St, XS, Xs, Xt, M128, consts["twoI128"], 128, leaf_iters,
                    w_r0=0, alpha_bc=bufs["alpha"][:, 0:1])
    if Si is not XS:
        inv.dve(lambda v: (v.tensor_copy(XS.re[:, :], Si.re[:, :]),
                           v.tensor_copy(XS.im[:, :], Si.im[:, :]))[-1])
    SiT = bufs["SiT128"]
    inv.ctrans(SiT, XS, 128)
    inv.cmm(Q, Nt, XA, 128, a_r0=a0, a_c0=d0, nA=Nt.n, nB=128, nO=128)
    inv.cmm(SiQ, SiT, Q, 128, nA=128, nB=128, nO=128)
    inv.ctrans(Pt, P_, 128)
    def post_tl(v, psr, psi, o_re, o_im, mc):
        v.tensor_add(o_re, psr, XA.re[:, 0:128])
        return v.tensor_add(o_im, psi, XA.im[:, 0:128])
    inv.cmm(out, Pt, SiQ, 128, o_r0=0, o_c0=0, post=post_tl, nA=128, nB=128)
    def post_neg(v, psr, psi, o_re, o_im, mc):
        v.tensor_scalar_mul(o_re, psr, -1.0)
        return v.tensor_scalar_mul(o_im, psi, -1.0)
    inv.cmm(out, Pt, XS, 128, o_r0=0, o_c0=128, post=post_neg, nA=128, nB=128)
    oc = 1
    def emit_bl(v):
        v.tensor_scalar_mul(out.re[:, oc * out.n + 0: oc * out.n + 128], SiQ.re[:, :], -1.0)
        v.tensor_scalar_mul(out.im[:, oc * out.n + 0: oc * out.n + 128], SiQ.im[:, :], -1.0)
        v.tensor_copy(out.re[:, oc * out.n + 128: oc * out.n + 256], XS.re[:, :])
        return v.tensor_copy(out.im[:, oc * out.n + 128: oc * out.n + 256], XS.im[:, :])
    inv.dve(emit_bl)
    inv.ctrans(outT, out, 256)


def emit_invert512(inv: Inv, W: CM, Wt: CM, Wi: CM, bufs, consts,
                   leaf_iters=30, polish=6):
    Ai, AiT = bufs["Ai256"], bufs["AiT256"]
    emit_inv256(inv, W, Wt, 0, Ai, AiT, bufs, consts, leaf_iters)
    P_, Pt, Q, S, St, SiQ = bufs["P256"], bufs["Pt256"], bufs["Q256"], \
        bufs["S256"], bufs["St256"], bufs["SiQ256"]
    inv.cmm(P_, AiT, W, 256, b_r0=0, b_c0=256, nB=W.n)
    def post_s(v, psr, psi, o_re, o_im, mc):
        dc = (256 + mc * 128) // 128
        v.scalar_tensor_tensor(o_re, psr, -1.0,
                               W.re[:, dc * W.n + 256: dc * W.n + 512], ALU.mult, ALU.add)
        return v.scalar_tensor_tensor(o_im, psi, -1.0,
                                      W.im[:, dc * W.n + 256: dc * W.n + 512], ALU.mult, ALU.add)
    inv.cmm(S, Wt, P_, 256, a_r0=0, a_c0=256, post=post_s, nA=Wt.n, nB=256, nO=256)
    inv.ctrans(St, S, 256)
    Si, SiT = bufs["Si256"], bufs["SiT256"]
    emit_inv256(inv, S, St, 0, Si, SiT, bufs, consts, leaf_iters)
    inv.cmm(Q, Wt, Ai, 256, a_r0=0, a_c0=256, nA=Wt.n, nB=256, nO=256)
    inv.cmm(SiQ, SiT, Q, 256, nA=256, nB=256, nO=256)
    inv.ctrans(Pt, P_, 256)
    def post_tl(v, psr, psi, o_re, o_im, mc):
        v.tensor_add(o_re, psr, Ai.re[:, mc * 256: mc * 256 + 256])
        return v.tensor_add(o_im, psi, Ai.im[:, mc * 256: mc * 256 + 256])
    inv.cmm(Wi, Pt, SiQ, 256, o_r0=0, o_c0=0, post=post_tl, nA=256, nB=256)
    def post_neg(v, psr, psi, o_re, o_im, mc):
        v.tensor_scalar_mul(o_re, psr, -1.0)
        return v.tensor_scalar_mul(o_im, psi, -1.0)
    inv.cmm(Wi, Pt, Si, 256, o_r0=0, o_c0=256, post=post_neg, nA=256, nB=256)
    def emit_b(v):
        last = None
        for mc in range(2):
            oc = 2 + mc
            last = v.tensor_scalar_mul(Wi.re[:, oc * Wi.n + 0: oc * Wi.n + 256],
                                       SiQ.re[:, mc * 256: mc * 256 + 256], -1.0)
            last = v.tensor_scalar_mul(Wi.im[:, oc * Wi.n + 0: oc * Wi.n + 256],
                                       SiQ.im[:, mc * 256: mc * 256 + 256], -1.0)
            last = v.tensor_copy(Wi.re[:, oc * Wi.n + 256: oc * Wi.n + 512],
                                 Si.re[:, mc * 256: mc * 256 + 256])
            last = v.tensor_copy(Wi.im[:, oc * Wi.n + 256: oc * Wi.n + 512],
                                 Si.im[:, mc * 256: mc * 256 + 256])
        return last
    inv.dve(emit_b)
    if polish > 0:
        res = inv.newton(Wt, Wi, bufs["Xs512"], bufs["Xt512"], bufs["M512"],
                         consts["twoI512"], 512, polish, w_r0=0, alpha_bc=None)
        if res is not Wi:
            inv.dve(lambda v: (v.tensor_copy(Wi.re[:, :], res.re[:, :]),
                               v.tensor_copy(Wi.im[:, :], res.im[:, :]))[-1])


F32 = np.float32


def chunked(M):
    r, c = M.shape
    nch = r // 128
    out = np.empty((128, nch * c), M.dtype)
    for k in range(nch):
        out[:, k * c:(k + 1) * c] = M[k * 128:(k + 1) * 128, :]
    return np.ascontiguousarray(out)


def unchunk(T, rows, cols):
    nch = rows // 128
    M = np.empty((rows, cols), T.dtype)
    for k in range(nch):
        M[k * 128:(k + 1) * 128, :] = T[:, k * cols:(k + 1) * cols]
    return M


def twoI_tile(n):
    nch = n // 128
    t = np.zeros((128, nch * n), np.float32)
    for c in range(nch):
        for p in range(128):
            t[p, c * n + c * 128 + p] = 2.0
    return t


def vec_chunk(v):
    n = v.shape[0]
    return np.ascontiguousarray(v.reshape(n // 128, 128).T.astype(v.dtype))


PERM_REIM = np.concatenate([np.arange(0, 1024, 2), np.arange(1, 1024, 2)])


def ab_inputs(xb, d, T):
    f = lambda a: np.ascontiguousarray(np.asarray(a, np.float32))
    w_a, w_x = f(d["L0_attn_w_a"]), f(d["L0_attn_w_x"])
    V_r, V_i = f(d["L0_attn_mat_v_real"]), f(d["L0_attn_mat_v_imag"])
    out = dict(
        xT=chunked(f(xb).T.copy()),
        Vt_r=chunked(V_r.T.copy()), Vt_i=chunked(V_i.T.copy()),
        Vn_r=chunked(V_r), Vn_i=chunked(V_i),
        waT=chunked(np.ascontiguousarray(w_a.T[:, PERM_REIM])),
        wxT=chunked(np.ascontiguousarray(w_x.T[:, PERM_REIM])),
        woT=chunked(f(d["L0_attn_w_out"]).T.copy()),
        h0r=vec_chunk(f(d["L0_attn_last_hidden_real"])),
        h0i=vec_chunk(f(d["L0_attn_last_hidden_imag"])),
        W1T=chunked(f(d["L0_ffn_W1"]).T.copy()),
        b1c=vec_chunk(f(d["L0_ffn_b1"])),
        W2T=chunked(f(d["L0_ffn_W2"]).T.copy()),
        b2c=vec_chunk(f(d["L0_ffn_b2"])),
        ident=np.eye(128, dtype=np.float32),
        twoI512=twoI_tile(512), twoI128=twoI_tile(128),
        onesc=np.ones((128, 1), np.float32), onesr=np.ones((1, 128), np.float32),
        onesW=np.ones((128, 512), np.float32),
    )
    return out


def c_inputs(x2b_perm, d, T):
    f = lambda a: np.ascontiguousarray(np.asarray(a, np.float32))
    masks = np.zeros((128, 4 * 512), np.float32)
    ii = np.arange(128)[:, None]
    jj = np.arange(512)[None, :]
    for p in range(4):
        masks[:, p * 512:(p + 1) * 512] = np.where(ii > jj - 128 * p, -1e30, 0.0)
    return dict(
        x2T=chunked(f(x2b_perm).T.copy()),
        dqc=chunked(f(d["L1_attn_W_DQ"])),
        uqt=chunked(f(d["L1_attn_W_UQ"]).T.copy()),
        dkvc=chunked(f(d["L1_attn_W_DKV"])),
        ukvt=chunked(f(d["L1_attn_W_UKV"]).T.copy()),
        qrt=f(d["L1_attn_W_QR"]).T.copy(),
        krt=f(d["L1_attn_W_KR"]).T.copy(),
        wot=f(d["L1_attn_W_O"]).T.copy(),
        qnw=f(d["L1_attn_q_norm_w"]).reshape(64, 1),
        knw=f(d["L1_attn_k_norm_w"]).reshape(64, 1),
        identb=np.eye(64, dtype=np.float32),
        identf=np.eye(128, dtype=np.float32),
        masks=masks,
        onesc=np.ones((128, 1), np.float32), onesr=np.ones((1, 128), np.float32),
    )


def head_perm(j):
    mine = np.arange(256 * j, 256 * j + 256)
    rest = np.concatenate([np.arange(0, 256 * j), np.arange(256 * j + 256, 1024)])
    return np.concatenate([mine, rest])


def f_inputs(xb, d, T, r):
    f = lambda a: np.ascontiguousarray(np.asarray(a, np.float32))
    out = ab_inputs(xb, d, T)
    W1 = f(d["L1_ffn_W1"])
    w1e = np.empty((128, 4 * 32 * 1024), np.float32)
    for e in range(4):
        ck = chunked(W1[e])
        for i in range(32):
            for k in range(8):
                w1e[:, (e * 32 + i) * 1024 + k * 128:(e * 32 + i) * 1024 + (k + 1) * 128] = \
                    ck[:, k * 4096 + i * 128: k * 4096 + (i + 1) * 128]
    masks = np.zeros((128, 16 * 512), np.float32)
    ii = np.arange(128)[:, None]
    jj = np.arange(512)[None, :]
    qpos = 128 * (4 * (jj // 128) + r) + (jj % 128)
    for dd in range(16):
        masks[:, dd * 512:(dd + 1) * 512] = np.where(128 * dd + ii <= qpos, 0.0, -1e30)
    out.update(
        dqc=chunked(f(d["L1_attn_W_DQ"])),
        uqt=chunked(f(d["L1_attn_W_UQ"]).T.copy()),
        dkvc=chunked(f(d["L1_attn_W_DKV"])),
        ukvt=chunked(f(d["L1_attn_W_UKV"]).T.copy()),
        qrt=f(d["L1_attn_W_QR"]).T.copy(),
        krt=f(d["L1_attn_W_KR"]).T.copy(),
        wot=f(d["L1_attn_W_O"]).T.copy(),
        qnw=f(d["L1_attn_q_norm_w"]).reshape(64, 1),
        knw=f(d["L1_attn_k_norm_w"]).reshape(64, 1),
        masks16=masks,
        rmask=np.ascontiguousarray(
            np.broadcast_to((np.arange(4) == r).astype(np.float32), (128, 4))),
        dup64=np.ascontiguousarray(np.tile(np.eye(64, dtype=np.float32), (1, 2))),
        knw2=np.ascontiguousarray(
            np.concatenate([f(d["L1_attn_k_norm_w"])] * 2).reshape(128, 1)),
        osel=_osel(), esel=_esel(),
        wrc=chunked(f(d["L1_ffn_W_router"])),
        w1e=w1e,
        b1cD=np.ascontiguousarray(f(d["L1_ffn_b1"]).reshape(4, 32, 128)
                                  .transpose(2, 0, 1).reshape(128, 128)),
        woc=chunked(f(d["L1_ffn_Wo"])),
        boc=vec_chunk(f(d["L1_ffn_bo"])),
        gam=vec_chunk(f(d["L1_ffn_ln_gamma"])),
        bet=vec_chunk(f(d["L1_ffn_ln_beta"])),
    )
    return out


def f_reassemble(outs, T):
    TD = T // 4
    res = np.empty((2, T, 1024), np.float32)
    for c in range(8):
        b, r = c // 4, c % 4
        o = unchunk(outs[c], 1024, TD)
        for blk in range(TD // 128):
            g = 4 * blk + r
            res[b, 128 * g:128 * g + 128, :] = o[:, 128 * blk:128 * blk + 128].T
    return res


def _osel():
    o = np.zeros((128, 128), np.float32)
    p = np.arange(128)
    for j in range(8):
        o[p < 64, j * 16 + 2 * j] = 1.0
        o[p >= 64, j * 16 + 2 * j + 1] = 1.0
    return o


def _esel():
    e = np.zeros((16, 8 * 128), np.float32)
    p = np.arange(128)
    for j in range(8):
        e[2 * j, j * 128 + np.arange(64)] = 1.0
        e[2 * j + 1, j * 128 + 64 + np.arange(64)] = 1.0
    return e




F32 = mybir.dt.float32
BF16 = mybir.dt.bfloat16
I32 = mybir.dt.int32
AF = mybir.ActivationFunctionType
ALU = mybir.AluOpType
AX = mybir.AxisListType.X
PI = float(np.pi)

W = 512
W2W = 512


AB_INPUTS = [
    ("xT", None), ("Vt_r", [128, 2048]), ("Vt_i", [128, 2048]),
    ("Vn_r", [128, 2048]), ("Vn_i", [128, 2048]),
    ("waT", [128, 8 * 1024]), ("wxT", [128, 8 * 1024]),
    ("woT", [128, 4 * 1024]), ("h0r", [128, 4]), ("h0i", [128, 4]),
    ("W1T", [128, 8 * 4096]), ("b1c", [128, 32]),
    ("W2T", [128, 32 * 1024]), ("b2c", [128, 8]),
    ("ident", [128, 128]), ("twoI512", [128, 2048]), ("twoI128", [128, 128]),
    ("onesc", [128, 1]), ("onesr", [1, 128]), ("onesW", [128, 512]),
]


def build_AB(T=8192, leaf_iters=26, polish=6):
    nc = bass.Bass("TRN2", target_bir_lowering=False)
    din = {}
    for nm, shp in AB_INPUTS:
        if shp is None:
            shp = [128, 8 * T]
        din[nm] = nc.dram_tensor(nm, shp, F32, kind="ExternalInput").ap()
    x2T_out = nc.dram_tensor("x2T", [128, 8 * T], F32, kind="ExternalOutput").ap()
    stack = contextlib.ExitStack()
    block = stack.enter_context(nc.Block())
    p = Prog(nc, block, stack)
    p.alloc_psum(n_big=4)
    emit_L0(nc, stack, block, p, din, x2T_out, T, leaf_iters, polish)

    @block.sync
    def _(s):
        s.wait_ge(p.qs, p.nqs)

    nc._stack = stack
    return nc


def emit_L0(nc, stack, block, p, din, x2T_out, T, leaf_iters=26, polish=6):
    nslab = T // W
    x1_hbm = nc.dram_tensor("x1_scratch", [128, 8 * T], F32)
    w2t_hbm = nc.dram_tensor("w2t_scratch", [128, 32 * 1024], BF16)
    outer_stack = p.stack
    l0_stack = contextlib.ExitStack()
    p.stack = l0_stack

    sident = p.sb("sident", [128, 128], F32)
    s2I128 = p.sb("s2I128", [128, 128], F32)
    onesc = p.sb("sonesc", [128, 1], F32)
    onesr = p.sb("sonesr", [1, 128], F32)
    onesW = p.sb("sonesW", [128, 512], F32)
    h0r = p.sb("sh0r", [128, 4], F32)
    h0i = p.sb("sh0i", [128, 4], F32)
    xs = p.sb("t_x", [128, 8 * W], F32)
    xn = p.sb("t_xn", [128, 8 * W], BF16)
    onescb = p.sb("sonescb", [128, 1], BF16)
    lnsc = make_ln_scratch(p, W)
    lnsc["ones_col"] = onesc[:, :]
    lnsc["ones_col_b"] = onescb[:, :]
    lnsc["ones_row"] = onesr[:, :]

    for nm, tile in [("ident", sident), ("twoI128", s2I128), ("onesc", onesc),
                     ("onesr", onesr), ("onesW", onesW), ("h0r", h0r), ("h0i", h0i)]:
        p.dma_s(tile[:, :], din[nm])

    _sc0 = nc.enter_named_scope("p0_inv", False)[0]
    p1_stack = contextlib.ExitStack()

    def p1sb(name, shape, dtype=F32):
        return p1_stack.enter_context(nc.sbuf_tensor(name, shape, dtype))

    Wi_bf_r = p1sb("Wi_bf_r", [128, 2048], BF16)
    Wi_bf_i = p1sb("Wi_bf_i", [128, 2048], BF16)
    Wi_bf_ni = p1sb("Wi_bf_ni", [128, 2048], BF16)
    Vt_bf_r = p1sb("Vt_bf_r", [128, 2048], BF16)
    Vt_bf_ni = p1sb("Vt_bf_ni", [128, 2048], BF16)

    inv_stack = contextlib.ExitStack()
    inv = Inv(nc, block, inv_stack, sident[:, :], psum_from=p)
    s2I512 = inv_stack.enter_context(nc.sbuf_tensor("s2I512", [128, 2048], F32))

    Wm = inv.alloc("Wm", 512)
    Wt = inv.alloc("Wt", 512, nim=True)
    Wi = inv.alloc("Wi", 512)
    consts_inv = {"twoI512": s2I512[:, :], "twoI128": s2I128[:, :],
                  "ones_col": onesc[:, :], "ones_row": onesr[:, :]}
    sscr0 = inv_stack.enter_context(nc.sbuf_tensor("sscr0", [1, 1], F32))
    glue_stack = contextlib.ExitStack()
    inv.stack = glue_stack
    bufs = {}
    for nm, n, nim in [
        ("X128a", 128, False), ("X128b", 128, False), ("Xs128", 128, False),
        ("Xt128", 128, True), ("M128", 128, False), ("P128", 128, False),
        ("Pt128", 128, True), ("Q128", 128, False), ("S128", 128, False),
        ("St128", 128, True), ("SiQ128", 128, False), ("AiT128", 128, True),
        ("SiT128", 128, True),
        ("Ai256", 256, False), ("AiT256", 256, True), ("P256", 256, False),
        ("Pt256", 256, True), ("Q256", 256, False), ("S256", 256, False),
        ("St256", 256, True), ("SiQ256", 256, False), ("Si256", 256, False),
        ("SiT256", 256, True),
    ]:
        bufs[nm] = inv.alloc(nm, n, nim=nim)
    salpha = glue_stack.enter_context(nc.sbuf_tensor("salpha", [128, 1], F32))
    sscr = glue_stack.enter_context(nc.sbuf_tensor("sscr", [128, 516], F32))
    st11 = glue_stack.enter_context(nc.sbuf_tensor("st11", [1, 1], F32))
    bufs["alpha"] = salpha[:, 0:1]
    bufs["scr"] = sscr[:, :]
    bufs["t11"] = st11[:, :]
    bufs["ps11"] = p.misc1[0:1, 0:1]
    bufs["ps128"] = p.misc2[:, 0:1]

    p.dma_s(s2I512[:, :], din["twoI512"])
    p.dma_s(Wm.re[:, :], din["Vt_r"])
    p.dma_s(Wm.im[:, :], din["Vt_i"])
    p.dma_s(Wt.re[:, :], din["Vn_r"])
    p.dma_s(Wt.im[:, :], din["Vn_i"])

    @block.vector
    def _(v):
        v.wait_ge(p.qs, p.nqs)
        v.memset(sscr0[:, 0:1], 0.0).then_inc(inv.dv, 1)
    inv.ndv += 1

    inv.make_nim(Wt)
    emit_invert512(inv, Wm, Wt, Wi, bufs, consts_inv,
                      leaf_iters=leaf_iters, polish=0)
    glue_stack.close()
    pol_stack = contextlib.ExitStack()
    inv.stack = pol_stack
    Xs512 = inv.alloc("Xs512", 512)
    Xt512 = inv.alloc("Xt512", 512, nim=True)
    M512 = inv.alloc("M512", 512)
    res = inv.newton(Wt, Wi, Xs512, Xt512, M512, s2I512[:, :], 512, polish)
    if res is not Wi:
        inv.dve(lambda v: (v.tensor_copy(Wi.re[:, :], res.re[:, :]),
                           v.tensor_copy(Wi.im[:, :], res.im[:, :]))[-1])
    pol_stack.close()
    inv.stack = inv_stack

    def cast(v):
        v.tensor_copy(Wi_bf_r[:, :], Wi.re[:, :])
        v.tensor_copy(Wi_bf_i[:, :], Wi.im[:, :])
        v.tensor_scalar_mul(Wi_bf_ni[:, :], Wi.im[:, :], -1.0)
        v.tensor_copy(Vt_bf_r[:, :], Wm.re[:, :])
        v.tensor_scalar_mul(Vt_bf_ni[:, :], Wm.im[:, :], -1.0)
    inv.dve(cast)

    @block.vector
    def _(v):
        v.wait_ge(inv.dv, inv.ndv)
        v.memset(p.bar[:, :], 0.0).then_inc(p.el, 1)
    p.nel += 1
    inv_stack.close()

    nc.leave_named_scope("p0_inv", _sc0, False)
    _sc1 = nc.enter_named_scope("p1_mixer", False)[0]
    waT = p1sb("s_waT", [128, 8 * 1024], BF16)
    wxT = p1sb("s_wxT", [128, 8 * 1024], BF16)
    woT = p1sb("s_woT", [128, 4 * 1024], BF16)
    p.dma_g(waT[:, :], din["waT"], wait=[(p.el, p.nel)])
    p.dma_g(wxT[:, :], din["wxT"])
    p.dma_g(woT[:, :], din["woT"])
    n_wdma = p.nqg

    ar = p1sb("t_ar", [128, 4 * W], BF16)
    ai = p1sb("t_ai", [128, 4 * W], BF16)
    xr = p1sb("t_xr", [128, 4 * W], BF16)
    xi = p1sb("t_xi", [128, 4 * W], BF16)
    fbig = p1sb("t_fbig", [128, 12 * W], F32)
    f1 = fbig[:, 0 * W:4 * W]
    f2 = fbig[:, 4 * W:8 * W]
    f3 = fbig[:, 8 * W:12 * W]
    x1s = fbig[:, 0:8 * W]
    rho = p1sb("t_rho", [128, 4 * W], F32)
    sinT = p1sb("t_sin", [128, 4 * W], BF16)
    cosT = p1sb("t_cos", [128, 4 * W], BF16)
    gar = p1sb("t_gar", [128, 4 * W], BF16)
    gai = p1sb("t_gai", [128, 4 * W], BF16)
    zr = p1sb("t_zr", [128, 4 * W], F32)
    zi = p1sb("t_zi", [128, 4 * W], F32)
    thh = zr
    Th = zi
    WN = W + 1
    winr = p1sb("t_winr", [128, 4 * WN], BF16)
    wini = p1sb("t_wini", [128, 4 * WN], BF16)
    hrt = p1sb("t_hr", [128, 4 * W], BF16)
    ur, ui = xr, xi
    btr, bti = ar, ai
    mo = xn
    carT = p1sb("c_T", [128, 4], F32)
    carZr = p1sb("c_zr", [128, 4], F32)
    carZi = p1sb("c_zi", [128, 4], F32)

    def ch(tile, c, w=W):
        return tile[:, c * w:(c + 1) * w]

    def init_car(v):
        v.memset(onescb[:, :], 1.0)
        v.memset(carT[:, :], 0.0)
        v.memset(carZr[:, :], 0.0)
        v.memset(carZi[:, :], 0.0)
        for c in range(4):
            v.tensor_copy(winr[:, c * WN: c * WN + 1], h0r[:, c:c + 1])
            v.tensor_copy(wini[:, c * WN: c * WN + 1], h0i[:, c:c + 1])
    p.vec(init_car)

    for s in range(nslab):
        col0 = s * W
        for c in range(8):
            p.dma_s(ch(xs, c), din["xT"][:, c * T + col0: c * T + col0 + W],
                    wait=([(p.el, p.nel)] if c == 0 else None))
        dma_tgt = p.nqs
        p.el_barrier([(p.qs, dma_tgt), (p.qg, n_wdma)])

        emit_ln(p, lambda c: ch(xs, c), lambda c: ch(xn, c), W, lnsc,
                  x_full=xs[:, 0:8 * W], out_full=xn[:, 0:8 * W])

        def evac_a(m, ps):
            dst = ar if m < 4 else ai
            p.vec(lambda v, dst=dst, mm=m % 4, ps=ps:
                  v.tensor_copy(ch(dst, mm), ps[:, 0:W]))
        emit_mm(p, None,
                  lambda k, m: waT[:, k * 1024 + m * 128: k * 1024 + m * 128 + 128],
                  lambda k: ch(xn, k), 8, 8, W, evac_a)

        def evac_x(m, ps):
            dst = xr if m < 4 else xi
            p.vec(lambda v, dst=dst, mm=m % 4, ps=ps:
                  v.tensor_copy(ch(dst, mm), ps[:, 0:W]))
        emit_mm(p, None,
                  lambda k, m: wxT[:, k * 1024 + m * 128: k * 1024 + m * 128 + 128],
                  lambda k: ch(xn, k), 8, 8, W, evac_x)

        FW = slice(0, 4 * W)

        def xcf(v):
            v.tensor_mul(f1[:, FW], xr[:, FW], xr[:, FW])
            v.tensor_mul(f2[:, FW], xi[:, FW], xi[:, FW])
            v.tensor_add(f1[:, FW], f1[:, FW], f2[:, FW])
        p.vec(xcf)

        def sqxf(a):
            a.activation(f2[:, FW], f1[:, FW], AF.Sqrt)
        p.act(sqxf)

        def xc2(v):
            for c in range(4):
                v.tensor_mul(winr[:, c * WN + 1: c * WN + 1 + W], ch(xr, c), ch(f2, c))
                v.tensor_mul(wini[:, c * WN + 1: c * WN + 1 + W], ch(xi, c), ch(f2, c))
        p.vec(xc2)

        def gate(v):
            v.tensor_mul(f1[:, FW], ar[:, FW], ar[:, FW])
            v.tensor_mul(f2[:, FW], ai[:, FW], ai[:, FW])
            v.tensor_add(f1[:, FW], f1[:, FW], f2[:, FW])
        p.vec(gate)

        def sqf(a):
            a.activation(f2[:, FW], f1[:, FW], AF.Sqrt)
        p.act(sqf)

        def gate2(v):
            v.tensor_scalar_add(f3[:, FW], f1[:, FW], 1.0)
            v.reciprocal(f3[:, FW], f3[:, FW])
            v.tensor_mul(rho[:, FW], f1[:, FW], f3[:, FW])
            v.tensor_mul(f3[:, FW], f2[:, FW], f3[:, FW])
            v.tensor_mul(gar[:, FW], f3[:, FW], ar[:, FW])
            v.tensor_mul(gai[:, FW], f3[:, FW], ai[:, FW])
            v.tensor_add(f2[:, FW], f2[:, FW], ar[:, FW])
            v.tensor_mul(f3[:, FW], f2[:, FW], f2[:, FW])
            v.tensor_mul(f1[:, FW], ai[:, FW], ai[:, FW])
            v.tensor_add(f3[:, FW], f3[:, FW], f1[:, FW])
        p.vec(gate2)

        def sq2f(a):
            a.activation(f1[:, FW], f3[:, FW], AF.Sqrt)
        p.act(sq2f)

        def gate3(v):
            v.tensor_add(f1[:, FW], f1[:, FW], f2[:, FW])
            v.tensor_scalar_max(f1[:, FW], f1[:, FW], 1e-30)
            v.reciprocal(f1[:, FW], f1[:, FW])
            v.tensor_mul(f2[:, FW], f1[:, FW], ai[:, FW])
        p.vec(gate3)

        def atanf(a):
            a.activation(thh[:, FW], f2[:, FW], AF.Arctan)
        p.act(atanf)

        for mch in range(4):
            ps = p.psum()

            def ufn(t, mch=mch, ps=ps):
                for k in range(4):
                    t.matmul(ps[:, 0:W],
                             Wi_bf_r[:, k * 512 + mch * 128: k * 512 + mch * 128 + 128],
                             winr[:, k * WN: k * WN + W], start=(k == 0), stop=False)
                last = None
                for k in range(4):
                    last = t.matmul(
                        ps[:, 0:W],
                        Wi_bf_ni[:, k * 512 + mch * 128: k * 512 + mch * 128 + 128],
                        wini[:, k * WN: k * WN + W], start=False, stop=(k == 3))
                return last
            p.tensor(ufn)
            p.vec(lambda v, mch=mch, ps=ps: v.tensor_copy(ch(ur, mch), ps[:, 0:W]))
            ps2 = p.psum()

            def ufn2(t, mch=mch, ps=ps2):
                for k in range(4):
                    t.matmul(ps[:, 0:W],
                             Wi_bf_i[:, k * 512 + mch * 128: k * 512 + mch * 128 + 128],
                             winr[:, k * WN: k * WN + W], start=(k == 0), stop=False)
                last = None
                for k in range(4):
                    last = t.matmul(
                        ps[:, 0:W],
                        Wi_bf_r[:, k * 512 + mch * 128: k * 512 + mch * 128 + 128],
                        wini[:, k * WN: k * WN + W], start=False, stop=(k == 3))
                return last
            p.tensor(ufn2)
            p.vec(lambda v, mch=mch, ps=ps2: v.tensor_copy(ch(ui, mch), ps[:, 0:W]))

        def scans1(v):
            for c in range(4):
                v.tensor_tensor_scan(ch(Th, c), onesW[:, 0:W], ch(thh, c),
                                     carT[:, c:c + 1], ALU.mult, ALU.add)
            for c in range(4):
                v.tensor_copy(carT[:, c:c + 1], Th[:, c * W + W - 1: c * W + W])
        p.vec(scans1)

        MAGIC = 12582912.0

        def rr1(v):
            H = Th[:, FW]
            v.tensor_scalar_mul(f1[:, FW], H, 2.0 / PI)
            v.tensor_scalar_add(f1[:, FW], f1[:, FW], MAGIC)
            v.tensor_scalar_add(f1[:, FW], f1[:, FW], -MAGIC)
            v.scalar_tensor_tensor(f1[:, FW], f1[:, FW], -PI / 2, H,
                                   ALU.mult, ALU.add)
            v.tensor_scalar(f2[:, FW], H, 2.0 / PI, 0.25, ALU.mult, ALU.add)
            v.tensor_scalar_add(f2[:, FW], f2[:, FW], MAGIC)
            v.tensor_scalar_add(f2[:, FW], f2[:, FW], -MAGIC)
            v.tensor_scalar_mul(f2[:, FW], f2[:, FW], -PI / 2)
            v.tensor_add(f2[:, FW], f2[:, FW], H)
            v.tensor_scalar_add(f2[:, FW], f2[:, FW], PI / 8)
            for t_ in (f1[:, FW], f2[:, FW]):
                v.tensor_scalar_min(t_, t_, 0.785396)
                v.tensor_scalar_max(t_, t_, -0.785396)
        p.vec(rr1)

        def sincos(a):
            a.activation(sinT[:, FW], f1[:, FW], AF.Sin, scale=4.0)
            a.activation(cosT[:, FW], f2[:, FW], AF.Sin, scale=4.0)
        p.act(sincos)

        def bmix(v):
            v.tensor_mul(f1[:, FW], gar[:, FW], ur[:, FW])
            v.tensor_mul(f2[:, FW], gai[:, FW], ui[:, FW])
            v.tensor_sub(f1[:, FW], f1[:, FW], f2[:, FW])
            v.tensor_mul(f2[:, FW], gar[:, FW], ui[:, FW])
            v.tensor_mul(f3[:, FW], gai[:, FW], ur[:, FW])
            v.tensor_add(f2[:, FW], f2[:, FW], f3[:, FW])
            v.tensor_mul(f3[:, FW], cosT[:, FW], f1[:, FW])
            v.tensor_mul(btr[:, FW], sinT[:, FW], f2[:, FW])
            v.tensor_add(btr[:, FW], btr[:, FW], f3[:, FW])
            v.tensor_mul(f3[:, FW], cosT[:, FW], f2[:, FW])
            v.tensor_mul(f2[:, FW], sinT[:, FW], f1[:, FW])
            v.tensor_sub(bti[:, FW], f3[:, FW], f2[:, FW])
        p.vec(bmix)

        def scans2(v):
            for c in range(4):
                v.tensor_tensor_scan(ch(zr, c), ch(rho, c), ch(btr, c),
                                     carZr[:, c:c + 1], ALU.mult, ALU.add)
                v.tensor_tensor_scan(ch(zi, c), ch(rho, c), ch(bti, c),
                                     carZi[:, c:c + 1], ALU.mult, ALU.add)
            for c in range(4):
                v.tensor_copy(carZr[:, c:c + 1], zr[:, c * W + W - 1: c * W + W])
                v.tensor_copy(carZi[:, c:c + 1], zi[:, c * W + W - 1: c * W + W])
        p.vec(scans2)

        sr, si = gar, gai

        def smix(v):
            v.tensor_mul(f1[:, FW], cosT[:, FW], zr[:, FW])
            v.tensor_mul(f2[:, FW], sinT[:, FW], zi[:, FW])
            v.tensor_sub(sr[:, FW], f1[:, FW], f2[:, FW])
            v.tensor_mul(f1[:, FW], cosT[:, FW], zi[:, FW])
            v.tensor_mul(f2[:, FW], sinT[:, FW], zr[:, FW])
            v.tensor_add(si[:, FW], f1[:, FW], f2[:, FW])
        p.vec(smix)

        for mcH in range(4):
            ps = p.psum()

            def hfn(t, mcH=mcH, ps=ps):
                for k in range(4):
                    t.matmul(ps[:, 0:W],
                             Vt_bf_r[:, k * 512 + mcH * 128: k * 512 + mcH * 128 + 128],
                             ch(sr, k), start=(k == 0), stop=False)
                last = None
                for k in range(4):
                    last = t.matmul(
                        ps[:, 0:W],
                        Vt_bf_ni[:, k * 512 + mcH * 128: k * 512 + mcH * 128 + 128],
                        ch(si, k), start=False, stop=(k == 3))
                return last
            p.tensor(hfn)
            p.vec(lambda v, mcH=mcH, ps=ps:
                  v.tensor_add(ch(hrt, mcH), ps[:, 0:W],
                               winr[:, mcH * WN + 1: mcH * WN + 1 + W]))

        def wincar(v):
            for c in range(4):
                v.tensor_copy(winr[:, c * WN: c * WN + 1],
                              winr[:, c * WN + W: c * WN + W + 1])
                v.tensor_copy(wini[:, c * WN: c * WN + 1],
                              wini[:, c * WN + W: c * WN + W + 1])
        p.vec(wincar)

        def evac_mo(m, ps):
            p.vec(lambda v, m=m, ps=ps: v.tensor_copy(ch(mo, m), ps[:, 0:W]))
        emit_mm(p, None,
                  lambda k, m: woT[:, k * 1024 + m * 128: k * 1024 + m * 128 + 128],
                  lambda k: ch(hrt, k), 4, 8, W, evac_mo)

        emit_ln(p, lambda c: ch(mo, c), lambda c: ch(x1s, c), W, lnsc,
                  x_full=mo[:, 0:8 * W], out_full=x1s[:, 0:8 * W])

        def resid1(v):
            for c in range(8):
                v.tensor_add(ch(x1s, c), ch(x1s, c), ch(xs, c))
        p.vec(resid1)

        for c in range(8):
            p.dma_s(x1_hbm.ap()[:, c * T + col0: c * T + col0 + W], ch(x1s, c),
                    wait=([(p.el, p.nel)] if c == 0 else None))

    p.el_barrier([(p.qs, p.nqs)])
    p1_stack.close()

    nc.leave_named_scope("p1_mixer", _sc1, False)
    _sc2 = nc.enter_named_scope("p2_ffn", False)[0]
    WF = W2W
    nslab2 = T // WF
    W1t = p.sb("s_W1t", [128, 8 * 4096], BF16)
    w2b = p.sb("t_w2b", [128, 2 * 1024], BF16)
    w2stage = p.sb("t_w2stage", [128, 2 * 1024], BF16)
    b1c = p.sb("s_b1c", [128, 32], F32)
    b2c = p.sb("s_b2c", [128, 8], F32)
    p.dma_s(b1c[:, :], din["b1c"], wait=[(p.el, p.nel)])
    p.dma_s(b2c[:, :], din["b2c"])
    h1 = p.sb("t_h1", [128, 32 * WF], BF16)
    wld = h1[:, 0:2048].bitcast(F32)
    wabs = h1[:, 2048:4096].bitcast(F32)
    accv = p.sb("t_acc", [128, 2], F32)
    sc11 = p.sb("t_sc11", [1, 1], F32)
    scbc = p.sb("t_scbc", [128, 1], F32)
    x2s = lnsc["sq"]
    fo = xn

    def ternarize(dram_ap, nsub, width, out_tile, n_elem, out_hbm=None,
                  stage=None):
        for c in range(nsub):
            p.dma_s(wld[:, 0:width], dram_ap[:, c * width:(c + 1) * width],
                    wait=[(p.el, p.nel)])
            tgt = p.nqs

            def absf(a, c=c):
                a.activation(wabs[:, 0:width], wld[:, 0:width], AF.Abs)
            p.act(absf, wait_dma=[(p.qs, tgt)])

            def red(v, c=c):
                if c == 0:
                    v.reduce_sum(accv[:, 0:1], wabs[:, 0:width], axis=AX)
                else:
                    v.reduce_sum(accv[:, 1:2], wabs[:, 0:width], axis=AX)
                    v.tensor_add(accv[:, 0:1], accv[:, 0:1], accv[:, 1:2])
            p.vec(red)

        def pt(t):
            return t.matmul(p.misc1[0:1, 0:1], accv[:, 0:1], onesc[:, :],
                            start=True, stop=True)
        p.tensor(pt)

        def scale_f(v):
            v.tensor_scalar_mul(sc11[:, :], p.misc1[0:1, 0:1], 1.0 / n_elem)
            v.tensor_scalar_max(sc11[:, :], sc11[:, :], 1e-8)
            v.reciprocal(sc11[:, :], sc11[:, :])
        p.vec(scale_f)

        def bcm(t):
            return t.matmul(p.misc2[:, 0:1], onesr[:, :], sc11[:, :],
                            start=True, stop=True)
        p.tensor(bcm)
        p.vec(lambda v: v.tensor_copy(scbc[:, :], p.misc2[:, 0:1]))

        for c in range(nsub):
            p.dma_s(wld[:, 0:width], dram_ap[:, c * width:(c + 1) * width],
                    wait=[(p.el, p.nel)])
            tgt = p.nqs
            if out_hbm is None:
                dst = out_tile[:, c * width:(c + 1) * width]
            else:
                dst = stage[:, (c % 2) * width:(c % 2 + 1) * width]

            def tern(v, c=c, dst=dst):
                MAGIC = 12582912.0
                v.tensor_scalar_mul(wabs[:, 0:width], wld[:, 0:width], scbc[:, 0:1])
                v.tensor_scalar_min(wabs[:, 0:width], wabs[:, 0:width], 2.5)
                v.tensor_scalar_max(wabs[:, 0:width], wabs[:, 0:width], -2.5)
                v.tensor_scalar_add(wabs[:, 0:width], wabs[:, 0:width], MAGIC)
                v.tensor_scalar_add(wabs[:, 0:width], wabs[:, 0:width], -MAGIC)
                v.tensor_scalar_min(wabs[:, 0:width], wabs[:, 0:width], 1.0)
                v.tensor_scalar_max(wabs[:, 0:width], wabs[:, 0:width], -1.0)
                v.tensor_copy(dst, wabs[:, 0:width])
            p.vec(tern, wait_dma=[(p.qs, tgt)])
            if out_hbm is not None:
                p.dma_s(out_hbm[:, c * width:(c + 1) * width], dst,
                        wait=[(p.el, p.nel)])

    ternarize(din["W1T"], 32, 1024, W1t, 4096 * 1024)
    ternarize(din["W2T"], 32, 1024, None, 4096 * 1024,
              out_hbm=w2t_hbm.ap(), stage=w2stage)

    for s in range(nslab2):
        col0 = s * WF
        for c in range(8):
            p.dma_s(xs[:, c * WF:(c + 1) * WF],
                    x1_hbm.ap()[:, c * T + col0: c * T + col0 + WF],
                    wait=([(p.el, p.nel)] if c == 0 else None))
        tgt = p.nqs
        p.el_barrier([(p.qs, tgt)])

        emit_ln(p, lambda c: xs[:, c * WF:(c + 1) * WF],
                  lambda c: xn[:, c * WF:(c + 1) * WF], WF, lnsc,
                  x_full=xs[:, 0:8 * WF], out_full=xn[:, 0:8 * WF])

        nel_ge = {-2: p.nel, -1: p.nel}
        for m in range(32):
            ps = p.psum()

            def m1(t, m=m, ps=ps):
                t.wait_ge(p.el, nel_ge[m - 2])
                last = None
                for k in range(8):
                    last = t.matmul(
                        ps[:, 0:WF],
                        W1t[:, k * 4096 + m * 128: k * 4096 + m * 128 + 128],
                        xn[:, k * WF:(k + 1) * WF], start=(k == 0), stop=(k == 7))
                return last
            p.tensor(m1, wait_el=False)

            def ge(a, m=m, ps=ps):
                a.activation(h1[:, m * WF:(m + 1) * WF], ps[:, 0:WF], AF.Gelu,
                             bias=b1c[:, m:m + 1])
            p.act(ge)
            nel_ge[m] = p.nel

        psm_all = [p.ps[0], p.ps[1], p.ps[2], p.ps[3],
                   p.pst[0], p.pst[1], p.misc1, p.misc2]
        m2_pe = {}
        for k in range(32):
            buf = w2b[:, (k % 2) * 1024:(k % 2 + 1) * 1024]
            wlk = [(p.qs, p.nqs)]
            if k >= 2:
                wlk.append((p.pe, m2_pe[k - 2]))
            else:
                wlk.append((p.el, p.nel))
            p.dma_s(buf, w2t_hbm.ap()[:, k * 1024:(k + 1) * 1024], wait=wlk)
            wt = p.nqs

            def m2k(t, k=k, buf=buf, wt=wt):
                t.wait_ge(p.qs, wt)
                last = None
                for m in range(8):
                    last = t.matmul(psm_all[m][:, 0:WF],
                                    buf[:, m * 128:(m + 1) * 128],
                                    h1[:, k * WF:(k + 1) * WF],
                                    start=(k == 0), stop=(k == 31))
                return last
            p.tensor(m2k, wait_el=(k == 0))
            m2_pe[k] = p.npe

        def bia(v):
            for m in range(8):
                v.tensor_scalar_add(fo[:, m * WF:(m + 1) * WF],
                                    psm_all[m][:, 0:WF], b2c[:, m:m + 1])
        p.vec(bia)

        emit_ln(p, lambda c: fo[:, c * WF:(c + 1) * WF],
                  lambda c: x2s[:, c * WF:(c + 1) * WF], WF, lnsc,
                  x_full=fo[:, 0:8 * WF], out_full=x2s[:, 0:8 * WF])

        def resid2(v):
            v.tensor_add(x2s[:, 0:8 * WF], x2s[:, 0:8 * WF], xs[:, 0:8 * WF])
        p.vec(resid2)

        for c in range(8):
            p.dma_s(x2T_out[:, c * T + col0: c * T + col0 + WF],
                    x2s[:, c * WF:(c + 1) * WF],
                    wait=([(p.el, p.nel)] if c == 0 else None))

    p.el_barrier([(p.qs, p.nqs), (p.qg, p.nqg)])
    nc.leave_named_scope("p2_ffn", _sc2, False)
    p.stack = outer_stack
    l0_stack.close()
    return None


F32 = mybir.dt.float32
BF16 = mybir.dt.bfloat16
I32 = mybir.dt.int32
AF = mybir.ActivationFunctionType
ALU = mybir.AluOpType
AX = mybir.AxisListType.X

WS = 512
RMS_EPS = 1.1920929e-07
SCALE = 64.0 ** -0.5
LN_EPS_IN = 1e-12


def build_F(T=8192, leaf_iters=26, polish=6):
    TD = T // 4
    nslab = T // WS
    nheads = 16
    ntb = T // 128
    nc = bass.Bass("TRN2", target_bir_lowering=False)
    din = {}
    for nm, shp in AB_INPUTS:
        if shp is None:
            shp = [128, 8 * T]
        din[nm] = nc.dram_tensor(nm, shp, F32, kind="ExternalInput").ap()
    for nm, shp, dt in [
        ("dqc", [128, 4 * 64], F32), ("uqt", [128, 4 * 64], F32),
        ("dkvc", [128, 8 * 64], F32), ("ukvt", [128, 8 * 128], F32),
        ("qrt", [64, 64], F32), ("krt", [64, 64], F32), ("wot", [64, 64], F32),
        ("qnw", [64, 1], F32), ("knw", [64, 1], F32),
        ("masks16", [128, 16 * 512], F32), ("rmask", [128, 4], F32),
        ("dup64", [64, 128], F32), ("knw2", [128, 1], F32),
        ("osel", [128, 128], F32), ("esel", [16, 8 * 128], F32),
        ("wrc", [128, 8 * 4], F32), ("w1e", [128, 4 * 32 * 1024], F32),
        ("b1cD", [128, 4 * 32], F32), ("woc", [128, 32 * 1024], F32),
        ("boc", [128, 8], F32), ("gam", [128, 8], F32), ("bet", [128, 8], F32),
    ]:
        din[nm] = nc.dram_tensor(nm, shp, dt, kind="ExternalInput").ap()
    outT = nc.dram_tensor("outT", [128, 8 * TD], F32, kind="ExternalOutput").ap()
    x2_hbm = nc.dram_tensor("x2_scratch", [128, 8 * T], F32)
    x2own_hbm = nc.dram_tensor("x2own_scratch", [128, 8 * TD], F32)
    w1bf_hbm = nc.dram_tensor("w1bf_scratch", [128, 4 * 32 * 1024], BF16)
    wocbf_hbm = nc.dram_tensor("wocbf_scratch", [128, 32 * 1024], BF16)
    kv_hbm = nc.dram_tensor("kv_scratch", [128, nheads * T], BF16)
    vt_hbm = nc.dram_tensor("vt_scratch", [128, nheads * ntb * 65], BF16)
    ao_hbm = nc.dram_tensor("ao_scratch", [128, 8 * TD], BF16)

    stack = contextlib.ExitStack()
    block = stack.enter_context(nc.Block())
    p = Prog(nc, block, stack)
    p.alloc_psum(n_big=4)

    emit_L0(nc, stack, block, p, din, x2_hbm.ap(), T, leaf_iters, polish)

    srm = p.sb("srmask", [128, 4], F32)
    p.dma_s(srm[:, :], din["rmask"], wait=[(p.el, p.nel)])

    p3 = contextlib.ExitStack()
    p.stack = p3

    def p3sb(name, shape, dtype=F32):
        return p3.enter_context(nc.sbuf_tensor(name, shape, dtype))

    onescb2 = p3sb("onescb2", [128, 1], BF16)
    qnw = p3sb("sqnw", [64, 1], F32)
    knw = p3sb("sknw", [64, 1], F32)
    identf = p3sb("sidentf", [128, 128], F32)
    masks = p3sb("smasks16", [128, 16 * 512], BF16)
    wsm = {}
    for nm, shp in [("dqc", [128, 4 * 64]), ("uqt", [128, 4 * 64]),
                    ("dkvc", [128, 8 * 64]), ("ukvt", [128, 8 * 128]),
                    ("qrt", [64, 64]), ("krt", [64, 64]), ("wot", [64, 64])]:
        wsm[nm] = p3sb("s_" + nm, shp, BF16)
    p.dma_s(qnw[:, :], din["qnw"], wait=[(p.el, p.nel)])
    p.dma_s(knw[:, :], din["knw"])
    p.dma_s(identf[:, :], din["ident"])
    p.dma_g(masks[:, :], din["masks16"], wait=[(p.el, p.nel)])
    for nm in ["dqc", "uqt", "dkvc", "ukvt", "qrt", "krt", "wot"]:
        p.dma_g(wsm[nm][:, :], din[nm])
    nq3s, nq3g = p.nqs, p.nqg

    xs3 = p3sb("t_xs3", [128, 8 * WS], F32)
    xns = p3sb("t_xns", [128, 8 * WS], BF16)
    xh0 = p3sb("t_xh0", [64, WS], BF16)
    kpack = p3sb("t_kpack", [128, max(T, 16 * WS)], BF16)
    kvgrp = p3sb("t_kvgrp", [128, 8 * WS], F32)
    rsall = p3sb("t_rsall", [16, WS], F32)
    blkkc = p3sb("t_blkkc", [128, 128], BF16)
    blkkr = p3sb("t_blkkr", [128, 128], BF16)
    blkv = p3sb("t_blkv", [128, 128], BF16)
    dup64 = p3sb("s_dup64", [64, 128], BF16)
    osel = p3sb("s_osel", [128, 128], BF16)
    eselS = p3sb("s_esel", [16, 8 * 128], F32)
    knw2 = p3sb("s_knw2", [128, 1], F32)
    vtp = p3sb("t_vtp", [128, 8 * 65], BF16)
    rbc = p3sb("t_rbc", [128, WS], F32)
    att = p3sb("t_att", [128, 4 * WS], F32)
    attb = p3sb("t_attb", [128, 4 * WS], BF16)
    qct = p3sb("t_qct", [64, 64], BF16)
    kvct = p3sb("t_kvct", [64, 128], BF16)
    sonesc3 = p3sb("sonesc3", [128, 1], F32)
    sonesr3 = p3sb("sonesr3", [1, 128], F32)
    lnsc3 = make_ln_scratch(p, WS, tag="3")
    lnsc3["ones_col"] = sonesc3[:, :]
    lnsc3["ones_col_b"] = onescb2[:, :]
    lnsc3["ones_row"] = sonesr3[:, :]
    p.dma_s(lnsc3["ones_col"], din["onesc"])
    p.dma_s(lnsc3["ones_row"], din["onesr"])
    p.dma_s(eselS[:, :], din["esel"])
    p.dma_s(knw2[:, :], din["knw2"])
    p.dma_g(dup64[:, :], din["dup64"])
    p.dma_g(osel[:, :], din["osel"])
    nq3s = p.nqs
    nq3g = p.nqg

    p.el_barrier([(p.qs, nq3s), (p.qg, nq3g)])
    p.vec(lambda v: v.memset(onescb2[:, :], 1.0))

    def qc_fn(t):
        last = None
        for k in range(4):
            last = t.matmul(p.misc1[0:64, 0:64], wsm["dqc"][:, k * 64:(k + 1) * 64],
                            wsm["uqt"][:, k * 64:(k + 1) * 64],
                            start=(k == 0), stop=(k == 3))
        return last
    p.tensor(qc_fn)
    p.vec(lambda v: v.tensor_copy(qct[:, :], p.misc1[0:64, 0:64]))

    def kvc_fn(t):
        last = None
        for k in range(8):
            last = t.matmul(p.misc2[0:64, 0:128], wsm["dkvc"][:, k * 64:(k + 1) * 64],
                            wsm["ukvt"][:, k * 128:(k + 1) * 128],
                            start=(k == 0), stop=(k == 7))
        return last
    p.tensor(kvc_fn)
    p.vec(lambda v: v.tensor_copy(kvct[:, :], p.misc2[0:64, 0:128]))

    def bd_fn(t):
        t.matmul(p.misc1[:, 0:64], dup64[:, :], kvct[:, 0:64],
                 start=True, stop=True)
        t.matmul(p.misc2[:, 0:64], dup64[:, :], kvct[:, 64:128],
                 start=True, stop=True)
        return t.matmul(p.pst[1][:, 0:64], dup64[:, :], wsm["krt"][:, :],
                        start=True, stop=True)
    p.tensor(bd_fn)

    def bd_ev(v):
        v.memset(blkkc[:, :], 0.0)
        v.memset(blkv[:, :], 0.0)
        v.memset(blkkr[:, :], 0.0)
        v.tensor_copy(blkkc[0:64, 0:64], p.misc1[0:64, 0:64])
        v.tensor_copy(blkkc[64:128, 64:128], p.misc1[64:128, 0:64])
        v.tensor_copy(blkv[0:64, 0:64], p.misc2[0:64, 0:64])
        v.tensor_copy(blkv[64:128, 64:128], p.misc2[64:128, 0:64])
        v.tensor_copy(blkkr[0:64, 0:64], p.pst[1][0:64, 0:64])
        v.tensor_copy(blkkr[64:128, 64:128], p.pst[1][64:128, 0:64])
        v.memset(vtp[:, :].rearrange("p (x o) -> p x o", o=65)[:, :, 64:65], 1.0)
    p.vec(bd_ev)

    _sc3a = nc.enter_named_scope("p3a_kv", False)[0]
    nqg_kv = p.nqg
    for s in range(nslab):
        col0 = s * WS
        for c in range(8):
            p.dma_s(xs3[:, c * WS:(c + 1) * WS],
                    x2_hbm.ap()[:, c * T + col0: c * T + col0 + WS],
                    wait=([(p.el, p.nel)] if c == 0 else None))
        tgt = p.nqs
        p.el_barrier([(p.qs, tgt)])
        emit_ln(p, lambda c: xs3[:, c * WS:(c + 1) * WS],
                  lambda c: xns[:, c * WS:(c + 1) * WS], WS, lnsc3,
                  x_full=xs3[:, 0:8 * WS], out_full=xns[:, 0:8 * WS])

        kcall = kvgrp[:, 0:8 * WS]
        knall = kpack[:, 0:8 * WS]
        krall = kpack[:, 8 * WS:16 * WS]
        kcb = lnsc3["sq"][:, 0:4 * WS].bitcast(BF16)

        for j in range(8):
            ps = p.psum()

            def km2(t, j=j, ps=ps):
                return t.matmul(ps[:, 0:WS], blkkc[:, :],
                                xns[:, j * WS:(j + 1) * WS],
                                start=True, stop=True)
            p.tensor(km2)
            wl = [(p.qg, nqg_kv)] if (j == 0 and s > 0) else None
            p.vec(lambda v, j=j, ps=ps:
                  v.tensor_copy(kcall[:, j * WS:(j + 1) * WS], ps[:, 0:WS]),
                  wait_dma=wl)
            ps2 = p.psum()

            def kr2m(t, j=j, ps=ps2):
                return t.matmul(ps[:, 0:WS], blkkr[:, :],
                                xns[:, j * WS:(j + 1) * WS],
                                start=True, stop=True)
            p.tensor(kr2m)
            p.vec(lambda v, j=j, ps=ps2:
                  v.tensor_copy(krall[:, j * WS:(j + 1) * WS], ps[:, 0:WS]))

        p.act(lambda a: a.activation(kcb, kcall, AF.Square))

        def summ(t):
            last = None
            for j in range(8):
                last = t.matmul(p.misc1[0:16, 0:WS], osel[:, j * 16:(j + 1) * 16],
                                kcb[:, j * WS:(j + 1) * WS],
                                start=(j == 0), stop=(j == 7))
            return last
        p.tensor(summ)

        def kr1(v):
            v.tensor_scalar(rsall[:, :], p.misc1[0:16, 0:WS], 1.0 / 64,
                            RMS_EPS, ALU.mult, ALU.add)
        p.vec(kr1)

        def krl(a):
            a.activation(rbc[0:16, 0:WS], rsall[:, :], AF.Ln)
            a.activation(rsall[:, :], rbc[0:16, 0:WS], AF.Exp, scale=-0.5)
        p.act(krl)

        for j in range(8):
            ps = p.psum()

            def bcm(t, j=j, ps=ps):
                return t.matmul(ps[:, 0:WS], eselS[:, j * 128:(j + 1) * 128],
                                rsall[:, :], start=True, stop=True)
            p.tensor(bcm)

            def knp(v, j=j, ps=ps):
                v.tensor_mul(att[:, 0:WS], kcall[:, j * WS:(j + 1) * WS],
                             ps[:, 0:WS])
                v.tensor_scalar(knall[:, j * WS:(j + 1) * WS], att[:, 0:WS],
                                knw2[:, 0:1], 1.0, ALU.mult, ALU.mult)
            p.vec(knp)

        for h in range(nheads):
            j, half = h // 2, (h % 2) * 64
            p.dma_g(kv_hbm.ap()[0:64, h * T + col0: h * T + col0 + WS],
                    knall[half:half + 64, j * WS:(j + 1) * WS],
                    wait=([(p.el, p.nel)] if h == 0 else None))
            p.dma_g(kv_hbm.ap()[64:128, h * T + col0: h * T + col0 + WS],
                    krall[half:half + 64, j * WS:(j + 1) * WS])

        nqg_v = p.nqg
        for j in range(8):
            for i in range(4):
                ps = p.psum()

                def vm2(t, j=j, i=i, ps=ps):
                    return t.matmul(ps[0:128, 0:128],
                                    xns[:, j * WS + i * 128: j * WS + (i + 1) * 128],
                                    blkv[:, :], start=True, stop=True)
                p.tensor(vm2)

                def vev(v, i=i, ps=ps):
                    dst = vtp[:, :].rearrange("p (b k) -> p b k", b=2)[:, :, i * 65:i * 65 + 64]
                    src = ps[0:128, 0:128].rearrange("p (b d) -> p b d", b=2)
                    v.tensor_copy(dst, src)
                p.vec(vev, wait_dma=([(p.qg, nqg_v)] if i == 0 and j > 0 else None))
            nqg_v = p.nqg
            base0 = (2 * j) * ntb * 65 + 4 * s * 65
            base1 = (2 * j + 1) * ntb * 65 + 4 * s * 65
            p.dma_g(vt_hbm.ap()[:, base0: base0 + 260], vtp[:, 0:260],
                    wait=[(p.el, p.nel)])
            p.dma_g(vt_hbm.ap()[:, base1: base1 + 260], vtp[:, 260:520])
            nqg_v = p.nqg
        nqg_kv = p.nqg

    nqg_after_3a = p.nqg
    nc.leave_named_scope("p3a_kv", _sc3a, False)
    _sc3b = nc.enter_named_scope("p3b_attn", False)[0]
    oxn = p3sb("t_oxn", [128, 8 * TD], BF16)
    Kh = kpack[:, 0:T]
    vTh = p3sb("t_vTh", [128, ntb * 65], BF16)
    Qown = p3sb("t_Qo", [128, TD], BF16)
    qcs = att[0:64, WS:2 * WS]
    rm = p3sb("t_rm", [128, TD // 128 + 1], F32)
    rmrow = p3sb("t_rmrow", [1, 128], F32)
    nmbc = p3sb("t_nmbc", [128, WS], F32)
    ctxb = p3sb("t_ctxb", [64, WS], BF16)
    ao = p3sb("t_ao", [128, 8 * TD], BF16)
    n_own_slab = TD // WS
    n_qt = TD // 128
    n_g = TD // 512

    xg = kvgrp[:, 0:2 * 2048]
    sel_done = {}
    nqs_writes = 0
    for s4 in range(n_own_slab):
        for c in range(8):
            k = s4 * 8 + c
            buf = xg[:, (k % 2) * 2048:(k % 2) * 2048 + 2048]
            src = x2_hbm.ap()[:, c * T + 2048 * s4: c * T + 2048 * s4 + 2048] \
                .rearrange("p (g r i) -> p r g i", g=4, r=4)
            for j in range(4):
                wl0 = None
                if j == 0:
                    wl0 = [(p.qg, p.nqg)]
                    wl0.append((p.el, sel_done[k - 2]) if k >= 2
                               else (p.el, p.nel))
                p.dma_g(buf[:, j * WS:(j + 1) * WS], src[:, j], wait=wl0)
            wt = p.nqg
            wl = [(p.qg, wt)]
            if c == 0 and s4 > 0:
                wl.append((p.qs, nqs_writes))

            def sel(v, c=c, buf=buf):
                d = xs3[:, c * WS:(c + 1) * WS]
                v.tensor_scalar(d, buf[:, 0:WS], srm[:, 0:1], None, ALU.mult)
                for j in range(1, 4):
                    v.scalar_tensor_tensor(d, buf[:, j * WS:(j + 1) * WS],
                                           srm[:, j:j + 1], d, ALU.mult, ALU.add)
            p.vec(sel, wait_dma=wl)
            sel_done[k] = p.nel
        for c in range(8):
            p.dma_s(x2own_hbm.ap()[:, c * TD + s4 * WS: c * TD + s4 * WS + WS],
                    xs3[:, c * WS:(c + 1) * WS],
                    wait=([(p.el, p.nel)] if c == 0 else None))
        nqs_writes = p.nqs
        emit_ln(p, lambda c: xs3[:, c * WS:(c + 1) * WS],
                  lambda c: oxn[:, c * TD + s4 * WS: c * TD + s4 * WS + WS],
                  WS, lnsc3, x_full=xs3[:, 0:8 * WS])

    for i in range(64):
        p.dma_g(w1bf_hbm.ap()[:, i * 2048:(i + 1) * 2048],
                din["w1e"][:, i * 2048:(i + 1) * 2048],
                wait=([(p.el, p.nel)] if i == 0 else None))
    for i in range(16):
        p.dma_g(wocbf_hbm.ap()[:, i * 2048:(i + 1) * 2048],
                din["woc"][:, i * 2048:(i + 1) * 2048])
    nqg_cast = p.nqg

    for h in range(nheads):
        hc, hr = h // 2, (h % 2) * 64
        p.dma_s(Kh[:, :], kv_hbm.ap()[:, h * T:(h + 1) * T],
                wait=([(p.qg, nqg_after_3a), (p.el, p.nel)] if h == 0
                      else [(p.el, p.nel)]))
        p.dma_s(vTh[:, :], vt_hbm.ap()[:, h * ntb * 65:(h + 1) * ntb * 65])
        nkv = p.nqs
        p.el_barrier([(p.qs, nkv)])

        for nt in range(n_own_slab):
            sl = slice(nt * WS, (nt + 1) * WS)

            def cpo(v, sl=sl, hc=hc, hr=hr):
                v.tensor_copy(xh0[:, :], oxn[hr:hr + 64, hc * TD + nt * WS: hc * TD + nt * WS + WS])
            p.vec(cpo)
            ps = p.psum()

            def qm(t, ps=ps):
                return t.matmul(ps[0:64, 0:WS], qct[:, :], xh0[:, :],
                                start=True, stop=True)
            p.tensor(qm)
            p.vec(lambda v, ps=ps: v.tensor_copy(qcs[:, :], ps[0:64, 0:WS]))

            def sqq(a):
                a.activation(att[0:64, 0:WS], qcs[:, :], AF.Square)
            p.act(sqq)

            def msq(t):
                return t.matmul(p.misc1[0:1, 0:WS], lnsc3["ones_col"][0:64, 0:1],
                                att[0:64, 0:WS], start=True, stop=True)
            p.tensor(msq)

            def qr1(v):
                v.tensor_scalar(rbc[0:1, :], p.misc1[0:1, 0:WS], 1.0 / 64,
                                RMS_EPS, ALU.mult, ALU.add)
            p.vec(qr1)

            def qr2(a):
                a.activation(att[0:1, 0:WS], rbc[0:1, :], AF.Ln)
                a.activation(rbc[0:1, :], att[0:1, 0:WS], AF.Exp, scale=-0.5)
            p.act(qr2)

            def qr3(t):
                return t.matmul(p.misc1[:, 0:WS], lnsc3["ones_row"], rbc[0:1, :],
                                start=True, stop=True)
            p.tensor(qr3)

            def qn(v, sl=sl):
                v.tensor_copy(rbc[:, :], p.misc1[:, 0:WS])
                v.tensor_mul(qcs[:, :], qcs[:, :], rbc[0:64, :])
                v.tensor_scalar(Qown[0:64, sl], qcs[:, :], qnw[:, 0:1], SCALE,
                                ALU.mult, ALU.mult)
            p.vec(qn)
            ps2 = p.psum()

            def qrm(t, ps=ps2, sl=sl):
                return t.matmul(ps[0:64, 0:WS], wsm["qrt"][:, :], Qown[0:64, sl],
                                start=True, stop=True)
            p.tensor(qrm)
            p.vec(lambda v, ps=ps2, sl=sl:
                  v.tensor_copy(Qown[64:128, sl], ps[0:64, 0:WS]))

        pe_sA = {}
        nel_mx = {}
        ikb = [(i, kb) for i in range(n_qt) for kb in range(i + 1)]
        nA = len(ikb)

        def emit_mx(idx):
            i, kb = ikb[idx]
            bank = p.ps[idx % 4]

            def mx(v, bank=bank, i=i, kb=kb):
                if kb == 0:
                    v.reduce_max(rm[:, i:i + 1], bank[:, 0:WS], axis=AX,
                                 negate=True)
                else:
                    v.reduce_max(rm[:, n_qt:n_qt + 1], bank[:, 0:WS], axis=AX,
                                 negate=True)
                    v.tensor_tensor(rm[:, i:i + 1], rm[:, i:i + 1],
                                    rm[:, n_qt:n_qt + 1], ALU.min)
            p.vec(mx, wait_pe=False, wait_dma=[(p.pe, pe_sA[idx])])
            nel_mx[idx] = p.nel

        for idx in range(nA):
            i, kb = ikb[idx]
            bank = p.ps[idx % 4]

            def sA(t, bank=bank, i=i, kb=kb, idx=idx):
                t.wait_ge(p.el, nel_mx[idx - 4] if idx >= 4 else p.nel)
                return t.matmul(bank[:, 0:WS], Qown[:, i * 128:(i + 1) * 128],
                                Kh[:, kb * 512:(kb + 1) * 512],
                                start=True, stop=True)
            p.tensor(sA, wait_el=False)
            pe_sA[idx] = p.npe
            if idx >= 1:
                emit_mx(idx - 1)
        emit_mx(nA - 1)

        for g in range(n_g):
            for j in range(4):
                def trj(t, g=g, j=j):
                    return t.transpose(p.pst[1][0:1, 0:128],
                                       rm[:, 4 * g + j:4 * g + j + 1], identf[:, :])
                p.tensor(trj)
                p.vec(lambda v: v.tensor_copy(rmrow[:, :], p.pst[1][0:1, 0:128]))

                def bcj(t, j=j):
                    return t.matmul(p.misc1[:, j * 128:(j + 1) * 128],
                                    lnsc3["ones_row"], rmrow[:, :],
                                    start=True, stop=True)
                p.tensor(bcj)
            p.vec(lambda v: v.tensor_copy(nmbc[:, :], p.misc1[:, 0:512]))

            pe_sB = {}
            pe_av = {}
            nel_sub = {}
            nea_exd = {}
            nck = 16 * g + 16

            def emit_sub(ck, g=g):
                d = ck - 16 * g
                bank = p.ps[ck % 4]
                atts = att[:, (ck % 4) * WS:(ck % 4 + 1) * WS]

                def sub(v, bank=bank, d=d, atts=atts):
                    v.tensor_add(atts, bank[:, 0:WS], nmbc[:, :])
                    if d >= 0:
                        v.tensor_add(atts, atts,
                                     masks[:, d * 512:(d + 1) * 512])
                p.vec(sub, wait_pe=False, wait_dma=[(p.pe, pe_sB[ck])])
                nel_sub[ck] = p.nel

            def emit_ex(ck):
                atts = att[:, (ck % 4) * WS:(ck % 4 + 1) * WS]
                attbs = attb[:, (ck % 4) * WS:(ck % 4 + 1) * WS]

                @block.scalar
                def _(a, atts=atts, attbs=attbs,
                      nsub=nel_sub[ck],
                      pav=(pe_av[ck - 4] if ck >= 4 else p.npe)):
                    a.wait_ge(p.el, nsub)
                    a.wait_ge(p.pe, pav)
                    r = a.activation(attbs, atts, AF.Exp)
                    r._wait_ge(p.ea, p.nea)
                    r.then_inc(p.ea, 1)
                p.nea += 1
                nea_exd[ck] = p.nea

            def emit_av(ck, g=g):
                attbs = attb[:, (ck % 4) * WS:(ck % 4 + 1) * WS]

                def av(t, ck=ck, attbs=attbs, nea_ex=nea_exd[ck]):
                    t.wait_ge(p.ea, nea_ex)
                    return t.matmul(p.pst[0][0:65, 0:WS],
                                    vTh[:, ck * 65:(ck + 1) * 65], attbs,
                                    start=(ck == 0), stop=(ck == nck - 1))
                p.tensor(av, wait_el=False)
                pe_av[ck] = p.npe

            for ck in range(nck):
                bank = p.ps[ck % 4]

                def sB(t, bank=bank, g=g, ck=ck):
                    t.wait_ge(p.el, nel_sub[ck - 4] if ck >= 4 else p.nel)
                    return t.matmul(bank[:, 0:WS], Kh[:, ck * 128:(ck + 1) * 128],
                                    Qown[:, g * 512:(g + 1) * 512],
                                    start=True, stop=True)
                p.tensor(sB, wait_el=False)
                pe_sB[ck] = p.npe
                emit_sub(ck)
                emit_ex(ck)
                if ck >= 2:
                    emit_av(ck - 2)
            emit_av(nck - 2)
            emit_av(nck - 1)

            def nrm(v):
                v.reciprocal(rbc[0:1, :], p.pst[0][64:65, 0:WS])
            p.vec(nrm)

            def nbc(t):
                return t.matmul(p.misc2[0:64, 0:WS], lnsc3["ones_row"][:, 0:64],
                                rbc[0:1, :], start=True, stop=True)
            p.tensor(nbc)

            def cmul(v):
                v.tensor_copy(rbc[0:64, :], p.misc2[0:64, 0:WS])
                v.tensor_mul(att[0:64, 0:WS], p.pst[0][0:64, 0:WS], rbc[0:64, :])
                v.tensor_copy(ctxb[:, :], att[0:64, 0:WS])
            p.vec(cmul)
            ps5 = p.psum()

            def wof(t, ps=ps5):
                return t.matmul(ps[0:64, 0:WS], wsm["wot"][:, :], ctxb[:, :],
                                start=True, stop=True)
            p.tensor(wof)
            p.vec(lambda v, ps=ps5, g=g, hc=hc, hr=hr:
                  v.tensor_copy(ao[hr:hr + 64, hc * TD + g * 512: hc * TD + g * 512 + WS],
                                ps[0:64, 0:WS]))

    p.el_barrier([])
    for c in range(8):
        p.dma_s(ao_hbm.ap()[:, c * TD:(c + 1) * TD], ao[:, c * TD:(c + 1) * TD],
                wait=[(p.el, p.nel)])
    p.el_barrier([(p.qs, p.nqs)])
    nc.leave_named_scope("p3b_attn", _sc3b, False)
    p.stack = stack
    p3.close()

    _sc4 = nc.enter_named_scope("p4_moe", False)[0]
    emit_moe(nc, stack, block, p, din, outT, x2own_hbm, ao_hbm, TD, T,
             w1bf_hbm, wocbf_hbm, nqg_cast)
    nc.leave_named_scope("p4_moe", _sc4, False)

    @block.sync
    def _(s):
        s.wait_ge(p.qs, p.nqs)

    @block.gpsimd
    def _(g):
        g.wait_ge(p.qg, p.nqg)

    nc._stack = stack
    return nc


def emit_moe(nc, stack, block, p, din, outT, x2own_hbm, ao_hbm, TD, T,
             w1bf_hbm, wocbf_hbm, nqg_cast):
    W = 512
    nslab = TD // W
    p4 = contextlib.ExitStack()
    outer = p.stack
    p.stack = p4

    def sb4(name, shape, dtype=F32):
        return p4.enter_context(nc.sbuf_tensor(name, shape, dtype))

    onesc = sb4("onesc4", [128, 1], F32)
    onesr = sb4("onesr4", [1, 128], F32)
    onescb = sb4("onescb4", [128, 1], BF16)
    wrc = sb4("s_wrc", [128, 8 * 4], BF16)
    b1c = sb4("s_b1cD", [128, 4 * 32], F32)
    woc = sb4("s_woc", [128, 32 * 1024], BF16)
    boc = sb4("s_boc", [128, 8], F32)
    gam = sb4("s_gam", [128, 8], F32)
    bet = sb4("s_bet", [128, 8], F32)
    p.dma_s(onesc[:, :], din["onesc"], wait=[(p.el, p.nel)])
    p.dma_s(onesr[:, :], din["onesr"])
    p.dma_s(b1c[:, :], din["b1cD"])
    p.dma_s(boc[:, :], din["boc"])
    p.dma_s(gam[:, :], din["gam"])
    p.dma_s(bet[:, :], din["bet"])
    p.dma_g(wrc[:, :], din["wrc"], wait=[(p.el, p.nel)])
    p.dma_s(woc[:, :], wocbf_hbm.ap(), wait=[(p.qg, p.nqg), (p.el, p.nel)])
    nq_s, nq_g = p.nqs, p.nqg

    x2s = sb4("t_x2_4", [128, 8 * W], F32)
    ats = sb4("t_at_4", [128, 8 * W], BF16)
    x3s = sb4("t_x3_4", [128, 8 * W], F32)
    xn3 = sb4("t_xn3_4", [128, 8 * W], BF16)
    inter = sb4("t_inter4", [128, 32 * W], BF16)
    hs = sb4("t_hs4", [128, 2 * W], BF16)
    ys = x2s
    lgr = sb4("t_lgr4", [1, 4 * W], F32)
    mxr = sb4("t_mxr4", [1, W], F32)
    mrow = sb4("t_mrow4", [1, 4 * W], F32)
    accm = sb4("t_accm4", [1, W], F32)
    mbc = sb4("t_mbc4", [128, W], BF16)
    w1b = [sb4(f"t_w1b4{i}", [128, 1024], BF16) for i in range(3)]
    lnsc = make_ln_scratch(p, W, tag="4")
    lnsc["ones_col"] = onesc[:, :]
    lnsc["ones_col_b"] = onescb[:, :]
    lnsc["ones_row"] = onesr[:, :]

    def ch(tile, c, w=W):
        return tile[:, c * w:(c + 1) * w]

    p.el_barrier([(p.qs, nq_s), (p.qg, nq_g)])
    p.vec(lambda v: v.memset(onescb[:, :], 1.0))

    for s in range(nslab):
        col0 = s * W
        for c in range(8):
            p.dma_s(ch(x2s, c),
                    x2own_hbm.ap()[:, c * TD + col0: c * TD + col0 + W],
                    wait=([(p.el, p.nel)] if c == 0 else None))
        for c in range(8):
            p.dma_s(ch(ats, c), ao_hbm.ap()[:, c * TD + col0: c * TD + col0 + W],
                    wait=([(p.el, p.nel)] if c == 0 else None))
        p.el_barrier([(p.qs, p.nqs), (p.qg, p.nqg)])

        emit_ln(p, lambda c: ch(ats, c), lambda c: ch(x3s, c), W, lnsc,
                  x_full=ats[:, 0:8 * W], out_full=x3s[:, 0:8 * W])

        def r1(v):
            v.tensor_add(x3s[:, 0:8 * W], x3s[:, 0:8 * W], x2s[:, 0:8 * W])
        p.vec(r1)

        emit_ln(p, lambda c: ch(x3s, c), lambda c: ch(xn3, c), W, lnsc,
                  x_full=x3s[:, 0:8 * W], out_full=xn3[:, 0:8 * W])

        for e in range(4):
            def lg(t, e=e):
                last = None
                for k in range(8):
                    last = t.matmul(p.misc1[0:1, 0:W],
                                    wrc[:, k * 4 + e: k * 4 + e + 1],
                                    ch(xn3, k), start=(k == 0), stop=(k == 7))
                return last
            p.tensor(lg)
            p.vec(lambda v, e=e: v.tensor_copy(lgr[:, e * W:(e + 1) * W],
                                               p.misc1[0:1, 0:W]))

        def am(v):
            v.tensor_tensor(mxr[:, :], lgr[:, 0:W], lgr[:, W:2 * W], ALU.max)
            v.tensor_tensor(mxr[:, :], mxr[:, :], lgr[:, 2 * W:3 * W], ALU.max)
            v.tensor_tensor(mxr[:, :], mxr[:, :], lgr[:, 3 * W:4 * W], ALU.max)
            v.memset(accm[:, :], 0.0)
            for e in range(4):
                sl = slice(e * W, (e + 1) * W)
                v.tensor_tensor(mrow[:, sl], lgr[:, sl], mxr[:, :], ALU.is_ge)
                v.scalar_tensor_tensor(mrow[:, sl], accm[:, :], -1.0,
                                       mrow[:, sl], ALU.mult, ALU.add)
                v.tensor_scalar_max(mrow[:, sl], mrow[:, sl], 0.0)
                v.tensor_add(accm[:, :], accm[:, :], mrow[:, sl])
        p.vec(am)
        ge_done = {}
        nel_preexp = p.nel

        for e in range(4):
            def mb(t, e=e):
                return t.matmul(p.misc2[:, 0:W], onesr[:, :],
                                mrow[:, e * W:(e + 1) * W], start=True, stop=True)
            p.tensor(mb)
            p.vec(lambda v: v.tensor_copy(mbc[:, :], p.misc2[:, 0:W]))

            for i in range(32):
                j = e * 32 + i
                buf = w1b[j % 3]
                wl = [(p.qs, p.nqs), (p.el, ge_done.get(j - 3, nel_preexp))]
                if j == 0 and s == 0:
                    wl = [(p.qg, p.nqg), (p.qs, p.nqs), (p.el, p.nel)]
                p.dma_s(buf[:, :],
                        w1bf_hbm.ap()[:, j * 1024:(j + 1) * 1024], wait=wl)
                wt = p.nqs
                ps = p.psum()
                wl2 = [(p.qs, wt), (p.el, ge_done.get(j - 2, nel_preexp))]

                def mm1(t, buf=buf, ps=ps, wl2=wl2):
                    for sem, cnt in wl2:
                        t.wait_ge(sem, cnt)
                    last = None
                    for k in range(8):
                        last = t.matmul(ps[:, 0:W], buf[:, k * 128:(k + 1) * 128],
                                        ch(xn3, k), start=(k == 0), stop=(k == 7))
                    return last
                p.tensor(mm1, wait_el=False)

                def ge(a, e=e, i=i, ps=ps):
                    a.activation(hs[:, (i % 2) * W:(i % 2 + 1) * W], ps[:, 0:W],
                                 AF.Gelu, bias=b1c[:, e * 32 + i: e * 32 + i + 1])
                p.act(ge)

                def acc(v, e=e, i=i):
                    h1 = hs[:, (i % 2) * W:(i % 2 + 1) * W]
                    if e == 0:
                        v.tensor_mul(ch(inter, i), h1, mbc[:, :])
                    else:
                        v.tensor_mul(h1, h1, mbc[:, :])
                        v.tensor_add(ch(inter, i), ch(inter, i), h1)
                p.vec(acc)
                ge_done[j] = p.nel

        for m in range(8):
            ps = p.psum()

            def mo(t, m=m, ps=ps):
                last = None
                for k in range(32):
                    last = t.matmul(ps[:, 0:W],
                                    woc[:, k * 1024 + m * 128: k * 1024 + m * 128 + 128],
                                    ch(inter, k), start=(k == 0), stop=(k == 31))
                return last
            p.tensor(mo)

            def ye(v, m=m, ps=ps):
                v.tensor_scalar_add(ch(ys, m), ps[:, 0:W], boc[:, m:m + 1])
                v.tensor_add(ch(ys, m), ch(ys, m), ch(xn3, m))
            p.vec(ye)

        y2 = inter[:, 0:16 * W].bitcast(F32)
        emit_ln(p, lambda c: ch(ys, c), lambda c: ch(y2, c), W, lnsc,
                  eps=LN_EPS_IN, x_full=ys[:, 0:8 * W], out_full=y2[:, 0:8 * W])

        def gb(v):
            y3 = y2[:, 0:8 * W].rearrange("p (c w) -> p c w", c=8)
            g3 = gam[:, 0:8].unsqueeze(2).broadcast_to([128, 8, W])
            b3 = bet[:, 0:8].unsqueeze(2).broadcast_to([128, 8, W])
            v.tensor_tensor(y3, y3, g3, ALU.mult)
            v.tensor_tensor(y3, y3, b3, ALU.add)
        p.vec(gb)

        emit_ln(p, lambda c: ch(y2, c), lambda c: ch(y2, c), W, lnsc,
                  x_full=y2[:, 0:8 * W], out_full=y2[:, 0:8 * W])

        def r2(v):
            v.tensor_add(y2[:, 0:8 * W], y2[:, 0:8 * W], x3s[:, 0:8 * W])
        p.vec(r2)

        for c in range(8):
            p.dma_s(outT[:, c * TD + col0: c * TD + col0 + W], ch(y2, c),
                    wait=([(p.el, p.nel)] if c == 0 else None))
    p.stack = outer
    p4.close()



_BUILT = {}


def _get(name, fn):
    if name not in _BUILT:
        _BUILT[name] = fn()
    return _BUILT[name]


def kernel(**inputs):
    d = {k: np.asarray(v) for k, v in inputs.items()}
    x = np.asarray(d["L0_attn_x"], np.float32)
    B, T, DM = x.shape
    ncF = _get("F", lambda: build_F(T=T))
    in_maps = [f_inputs(x[c // 4], d, T, c % 4) for c in range(8)]
    res = run_bass_kernel_spmd(ncF, in_maps, core_ids=list(range(8)))
    outs = [res.results[c]["outT"] for c in range(8)]
    return f_reassemble(outs, T)
